# version 04
# baseline (speedup 1.0000x reference)
import numpy as np
import ml_dtypes
import concourse.bass as bass
import concourse.mybir as mybir
from concourse.bass_utils import run_bass_kernel_spmd

F32 = mybir.dt.float32
BF16 = mybir.dt.bfloat16
I32 = mybir.dt.int32
AF = mybir.ActivationFunctionType
OP = mybir.AluOpType
AX = mybir.AxisListType.X

D = 4096
T = 1024
HALO = 128
TH = T + HALO
H, P, N, G, CK, CS = 128, 64, 128, 8, 4, 256
INTER = H * P
NCH = T // CS
EPS = 1e-5

N_DT, N_B, N_C, N_HS, N_GATE = 1, 8, 8, 64, 64
WT0 = 8 * 128
N_FT = N_DT + N_B + N_C + N_HS + N_GATE


def _proj_tile_order():
    tiles = [18432 // 128]
    tiles += [16384 // 128 + i for i in range(8)]
    tiles += [17408 // 128 + i for i in range(8)]
    tiles += [8192 // 128 + i for i in range(64)]
    tiles += list(range(64))
    return tiles


def _panels(wt, n_out_tiles, pad_rows=None):
    if pad_rows is not None and wt.shape[0] < pad_rows:
        wt = np.concatenate([wt, np.zeros((pad_rows - wt.shape[0], wt.shape[1]),
                                          wt.dtype)], 0)
    d_in, d_out = wt.shape
    nci = d_in // 128
    a = wt.reshape(nci, 128, n_out_tiles, 128).transpose(2, 1, 0, 3)
    return np.ascontiguousarray(a.reshape(n_out_tiles, 128, nci * 128))


def make_inputs(full):
    x = np.asarray(full["L0_attn_input_states"])
    wproj = np.asarray(full["L0_attn_in_proj_w"])
    conv_w = np.asarray(full["L0_attn_conv_w"]).reshape(10240, CK)
    conv_b = np.asarray(full["L0_attn_conv_b"])
    dt_bias = np.asarray(full["L0_attn_dt_bias"])
    a_log = np.asarray(full["L0_attn_A_log"])
    d_vec = np.asarray(full["L0_attn_D"])
    norm_w = np.asarray(full["L0_attn_norm_w"])
    out_w = np.asarray(full["L0_attn_out_w"])
    wffn = np.asarray(full["L0_ffn_weight"])
    ffn_b = np.asarray(full["L0_ffn_bias"])
    lora_a = np.asarray(full["L0_ffn_lora_A"])
    lora_b = np.asarray(full["L0_ffn_lora_B"])
    lsi = np.asarray(full["L0_ffn_lora_start_indices"]).astype(np.int32)

    lb_t = lora_b.transpose(0, 2, 1).reshape(48, 4096)
    wffn_aug = np.concatenate([wffn.T, lb_t,
                               np.zeros((80, 4096), wffn.dtype)], 0)
    wffn_p = _panels(np.ascontiguousarray(wffn_aug), 32)
    wproj_all = _panels(np.ascontiguousarray(wproj.T), 18560 // 128)
    wproj_p = np.ascontiguousarray(wproj_all[_proj_tile_order()])
    wout_p = _panels(np.ascontiguousarray(out_w.T), 32)
    la_t = np.ascontiguousarray(lora_a.reshape(48, 4096).T)

    shared = {
        "wffn_p": wffn_p, "wproj_p": wproj_p, "wout_p": wout_p, "la_t": la_t,
        "conv_w_cols": np.ascontiguousarray(conv_w.reshape(80, 128, CK)),
        "conv_b_cols": np.ascontiguousarray(conv_b.reshape(80, 128)),
        "ffn_b_cols": np.ascontiguousarray(ffn_b.reshape(32, 128)),
        "d_rep": np.ascontiguousarray(np.repeat(d_vec, P).reshape(64, 128)),
        "nw_rep": np.ascontiguousarray(norm_w.reshape(64, 128)),
        "dtb_row": np.ascontiguousarray(dt_bias.reshape(1, 128)),
        "alog_row": np.ascontiguousarray(a_log.reshape(1, 128)),
        "ident_f32": np.eye(128, dtype=np.float32),
        "lsi_rep": np.ascontiguousarray(np.broadcast_to(lsi[None, :], (128, 4))),
    }
    in_maps = []
    for k in range(8):
        b, p = k // 4, k % 4
        t0 = T * p
        xs = x[b, t0:t0 + T, :]
        halo = (np.zeros((HALO, D), np.float32) if p == 0
                else x[b, t0 - HALO:t0, :])
        meta = np.zeros((128, 8), np.float32)
        meta[:, 0] = float(b)
        meta[:, 1] = 0.0 if p == 0 else 1.0
        for j in range(4):
            meta[:, 2 + j] = 1.0 if j < p else 0.0
        m = dict(shared)
        m["x_fm"] = np.ascontiguousarray(np.concatenate([halo, xs], 0).T)
        m["meta"] = np.ascontiguousarray(meta)
        in_maps.append(m)
    return in_maps


class Sem:
    def __init__(self, nc, name, step=1):
        self.h = nc.alloc_semaphore(name)
        self.step = step
        self.n = 0

    def inc(self, instr, k=1):
        self.n += self.step * k
        return instr.then_inc(self.h, self.step * k)

    def wait(self, eng, at=None):
        eng.wait_ge(self.h, self.n if at is None else at)


class Chan:

    def __init__(self, nc, name, nslots, step=16):
        self.sems = [nc.alloc_semaphore(f"{name}{j}") for j in range(nslots)]
        self.n = [0] * nslots
        self.step = step

    def inc(self, instr, slot):
        self.n[slot] += self.step
        return instr.then_inc(self.sems[slot], self.step)

    def val(self, slot):
        return self.n[slot]

    def wait(self, eng, slot, at=None):
        eng.wait_ge(self.sems[slot], self.n[slot] if at is None else at)

    def wait_all(self, eng):
        for j, s in enumerate(self.sems):
            if self.n[j]:
                eng.wait_ge(s, self.n[j])


class KB:
    def __init__(self, taps=()):
        self.taps = set(taps)
        nc = self.nc = bass.Bass("TRN2", target_bir_lowering=False)
        dram = nc.dram_tensor
        self.i_x = dram("x_fm", [D, TH], F32, kind="ExternalInput").ap()
        self.i_wffn = dram("wffn_p", [32, 128, 4224], F32, kind="ExternalInput").ap()
        self.i_wproj = dram("wproj_p", [N_FT, 128, 4096], F32, kind="ExternalInput").ap()
        self.i_wout = dram("wout_p", [32, 128, 8192], F32, kind="ExternalInput").ap()
        self.i_la = dram("la_t", [D, 48], F32, kind="ExternalInput").ap()
        self.i_convw = dram("conv_w_cols", [80, 128, CK], F32, kind="ExternalInput").ap()
        self.i_convb = dram("conv_b_cols", [80, 128], F32, kind="ExternalInput").ap()
        self.i_fb = dram("ffn_b_cols", [32, 128], F32, kind="ExternalInput").ap()
        self.i_drep = dram("d_rep", [64, 128], F32, kind="ExternalInput").ap()
        self.i_nwrep = dram("nw_rep", [64, 128], F32, kind="ExternalInput").ap()
        self.i_dtb = dram("dtb_row", [1, 128], F32, kind="ExternalInput").ap()
        self.i_alog = dram("alog_row", [1, 128], F32, kind="ExternalInput").ap()
        self.i_idf32 = dram("ident_f32", [128, 128], F32, kind="ExternalInput").ap()
        self.i_lsi = dram("lsi_rep", [128, 4], I32, kind="ExternalInput").ap()
        self.i_meta = dram("meta", [128, 8], F32, kind="ExternalInput").ap()
        self.o_y = dram("y_fm", [32, 128, T], F32, kind="ExternalOutput").ap()

        self.h_r1 = dram("r1_sp", [32, 128, TH], F32).ap()
        self.h_r2 = dram("r2_sp", [32, 128, T], F32).ap()
        self.h_gate = dram("gate_sp", [64, 128, T], BF16).ap()
        self.h_y1 = dram("y1_sp", [64, 128, T], BF16).ap()
        self.h_states = dram("states_sp", [NCH, 128, INTER], F32).ap()
        self.h_g = dram("g_sp", [64, 128, T], BF16).ap()
        self.h_agin = dram("ag_in", [128, INTER + 1], F32).ap()
        self.h_agout = dram("ag_out", [512, INTER + 1], F32).ap()

        sb = self._sb_perm = {}
        def L(name, shape, dt):
            t = nc.alloc_sbuf_tensor(name, shape, dt, side="left")
            sb[name] = t
            return t
        self.consts = L("consts", [128, 16], F32)
        self.convw = L("convw", [128, 80 * CK], F32)
        self.convb = L("convb", [128, 80], F32)
        self.drep = L("drep", [128, 64], F32)
        self.nwrep = L("nwrep", [128, 64], F32)
        self.hbcol = L("hbcol", [128, 32], F32)
        self.idf32 = L("idf32", [128, 128], F32)
        self.la = L("la", [128, 32 * 48], BF16)
        self.xa = L("xa", [128, TH], BF16)
        self.scol = L("scol", [48, 1], F32)
        self.ones1 = L("ones1", [1, 128], BF16)
        self.onescol = L("onescol", [128, 1], BF16)
        self.masks = L("masks", [128, 2 * CS], F32)
        self.scr = L("scr", [128, 64], F32)

        self._guards = {}

        self.s_in = Sem(nc, "s_in", 16)
        self.s_ing = Sem(nc, "s_ing", 16)
        self.c_w = Chan(nc, "c_w", 2)
        self.c_x = Chan(nc, "c_x", 2)
        self.c_sp = Chan(nc, "c_sp", 2)
        self.c_ld = Chan(nc, "c_ld", 4)
        self.c_tp = Chan(nc, "c_tp", 2)
        self.c_st = Chan(nc, "c_st", 2)
        self.c_y1 = Chan(nc, "c_y1", 2)
        self.c_gt = Chan(nc, "c_gt", 2)
        self.c_ss = Chan(nc, "c_ss", 2)
        self.c_bt = Chan(nc, "c_bt", 2)
        self.c_ag = Chan(nc, "c_ag", 1)
        self.c_sg = Chan(nc, "c_sg", 2)
        self.s_pe = Sem(nc, "s_pe", 1)
        self.s_pe2 = Sem(nc, "s_pe2", 1)
        self.s_dve = Sem(nc, "s_dve", 1)
        self.s_dve2 = Sem(nc, "s_dve2", 1)
        self.s_act = Sem(nc, "s_act", 1)
        self.s_act2 = Sem(nc, "s_act2", 1)
        self.s_gp = Sem(nc, "s_gp", 1)
        self.s_cc = Sem(nc, "s_cc", 1)
        self.all_sems = [self.s_in, self.s_ing, self.s_pe, self.s_pe2,
                         self.s_dve, self.s_dve2, self.s_act, self.s_act2,
                         self.s_gp, self.s_cc]
        self.all_chans = [self.c_w, self.c_x, self.c_sp, self.c_ld, self.c_tp,
                          self.c_st, self.c_y1, self.c_gt, self.c_ss, self.c_bt,
                          self.c_ag, self.c_sg]
        self.dbg = {}

    def push(self, name, shape, dt):
        self._pushid = getattr(self, "_pushid", 0) + 1
        g = self.nc.sbuf_tensor(f"{name}_{self._pushid}", shape, dt, side="right")
        t = g.__enter__()
        self._guards[name] = g
        setattr(self, name, t)
        return t

    def pop(self, *names):
        for name in names:
            self._guards.pop(name).__exit__(None, None, None)
            delattr(self, name)

    def tap(self, name, shape, dtype=F32):
        if name not in self.taps:
            return None
        ap = self.nc.dram_tensor("dbg_" + name, shape, dtype, kind="ExternalOutput").ap()
        self.dbg[name] = ap
        return ap

    def barrier(self, blk, engines=("sync", "vector", "scalar", "tensor", "gpsimd")):
        for en in engines:
            @getattr(blk, en)
            def _(eng, _en=en):
                for s in self.all_sems:
                    if s.n:
                        eng.wait_ge(s.h, s.n)
                for c in self.all_chans:
                    c.wait_all(eng)

    def wait_everything(self, eng):
        for s in self.all_sems:
            if s.n:
                eng.wait_ge(s.h, s.n)
        for c in self.all_chans:
            c.wait_all(eng)

    def emit_init(self):
        nc = self.nc
        with nc.Block("init") as blk:
            @blk.sync
            def _(sync):
                s = self.s_in
                s.inc(sync.dma_start(out=self.convw[:, :].rearrange("p (t k) -> p t k", t=80),
                                     in_=self.i_convw.rearrange("t p k -> p t k")))
                s.inc(sync.dma_start(allow_slow_non_contiguous=True, out=self.convb[:, :],
                                     in_=self.i_convb.rearrange("t p -> p t")))
                s.inc(sync.dma_start(allow_slow_non_contiguous=True, out=self.drep[:, :],
                                     in_=self.i_drep.rearrange("t p -> p t")))
                s.inc(sync.dma_start(allow_slow_non_contiguous=True, out=self.nwrep[:, :],
                                     in_=self.i_nwrep.rearrange("t p -> p t")))
                s.inc(sync.dma_start(allow_slow_non_contiguous=True, out=self.hbcol[:, :],
                                     in_=self.i_fb.rearrange("t p -> p t")))
                s.inc(sync.dma_start(out=self.idf32[:, :], in_=self.i_idf32[:, :]))
                s.inc(sync.dma_start(allow_slow_non_contiguous=True, out=self.consts[:, 0:1],
                                     in_=self.i_dtb.rearrange("o p -> p o")))
                s.inc(sync.dma_start(allow_slow_non_contiguous=True, out=self.consts[:, 1:2],
                                     in_=self.i_alog.rearrange("o p -> p o")))
                s.inc(sync.dma_start(out=self.consts[:, 3:11], in_=self.i_meta[:, :]))

            @blk.gpsimd
            def _(gp):
                s = self.s_ing
                s.inc(gp.dma_start(out=self.scr[:, 0:4].bitcast(I32), in_=self.i_lsi[:, :]))
                s.inc(gp.dma_start(
                    out=self.la[:, :].rearrange("p (c r) -> p c r", c=32),
                    in_=self.i_la.rearrange("(c p) r -> p c r", p=128)))
                self.s_gp.inc(gp.iota(self.masks[:, 0:CS], [[1, CS]],
                                      channel_multiplier=0,
                                      allow_small_or_imprecise_dtypes=True))
                self.s_gp.inc(gp.iota(self.scr[:, 16:17], [[0, 1]],
                                      channel_multiplier=1,
                                      allow_small_or_imprecise_dtypes=True))

            @blk.vector
            def _(v):
                self.s_in.wait(v)
                self.s_ing.wait(v)
                self.s_gp.wait(v)
                v.memset(self.ones1[:, :], 1.0)
                v.memset(self.onescol[:, :], 1.0)
                v.memset(self.xa[:, :], 0.0)
                self.s_dve.inc(v.tensor_copy(out=self.scr[:, 4:8],
                                             in_=self.scr[:, 0:4].bitcast(I32)))
                self.s_dve.inc(v.tensor_scalar(out=self.masks[:, CS:2 * CS],
                                               in0=self.masks[:, 0:CS],
                                               scalar1=self.scr[:, 16:17], scalar2=128.0,
                                               op0=OP.subtract, op1=OP.is_equal))
                self.s_dve.wait(v)
                self.s_dve.inc(v.tensor_scalar(out=self.masks[:, 0:CS],
                                               in0=self.masks[:, 0:CS],
                                               scalar1=self.scr[:, 16:17], scalar2=0.0,
                                               op0=OP.subtract, op1=OP.is_equal))
                self.s_dve.inc(v.tensor_scalar(out=self.scr[:, 8:12], in0=self.scr[:, 4:8],
                                               scalar1=-1.0, scalar2=self.consts[:, 3:4],
                                               op0=OP.mult, op1=OP.add))
                self.s_dve.wait(v)
                self.s_dve.inc(v.tensor_scalar(out=self.scr[:, 8:12], in0=self.scr[:, 8:12],
                                               scalar1=1.0, scalar2=0.0,
                                               op0=OP.add, op1=OP.max))
                self.s_dve.wait(v)
                self.s_dve.inc(v.tensor_scalar(out=self.scr[:, 8:12], in0=self.scr[:, 8:12],
                                               scalar1=1.0, scalar2=None, op0=OP.min))
                self.s_dve.wait(v)
                self.s_dve.inc(v.reduce_sum(out=self.scr[:, 12:13], in_=self.scr[:, 8:12],
                                            axis=AX))
                self.s_dve.wait(v)
                self.s_dve.inc(v.tensor_scalar(out=self.scr[:, 13:14], in0=self.scr[:, 12:13],
                                               scalar1=-1.0, scalar2=None, op0=OP.add))
                self.s_dve.wait(v)
                self.s_dve.inc(v.tensor_scalar(out=self.scr[:, 14:15], in0=self.scr[:, 13:14],
                                               scalar1=0.0, scalar2=1.0,
                                               op0=OP.max, op1=OP.min))
                self.s_dve.inc(v.tensor_scalar(out=self.scr[:, 15:16], in0=self.scr[:, 13:14],
                                               scalar1=-1.0, scalar2=0.0,
                                               op0=OP.add, op1=OP.max))
                self.s_dve.wait(v)
                self.s_dve.inc(v.tensor_scalar(out=self.scr[:, 15:16], in0=self.scr[:, 15:16],
                                               scalar1=2.0, scalar2=None, op0=OP.min))
                v.memset(self.scr[:, 24:25], 0.0)
                v.memset(self.scr[:, 25:26], 1.0)
                self.s_dve.inc(v.memset(self.scr[:, 26:27], 2.0))
                self.s_dve.wait(v)
                self.s_dve.inc(v.tensor_scalar(out=self.scr[:, 27:30], in0=self.scr[:, 24:27],
                                               scalar1=-1.0, scalar2=self.scr[:, 15:16],
                                               op0=OP.mult, op1=OP.add))
                self.s_dve.wait(v)
                self.s_dve.inc(v.tensor_tensor(out=self.scr[:, 27:30], in0=self.scr[:, 27:30],
                                               in1=self.scr[:, 27:30], op=OP.mult))
                self.s_dve.wait(v)
                self.s_dve.inc(v.tensor_scalar(out=self.scr[:, 27:30], in0=self.scr[:, 27:30],
                                               scalar1=-1.0, scalar2=1.0,
                                               op0=OP.mult, op1=OP.add))
                self.s_dve.wait(v)
                self.s_dve.inc(v.tensor_scalar(out=self.scr[:, 27:30], in0=self.scr[:, 27:30],
                                               scalar1=0.0, scalar2=self.scr[:, 14:15],
                                               op0=OP.max, op1=OP.mult))
                self.s_dve.wait(v)
                self.s_dve.inc(v.tensor_scalar(out=self.hbcol[:, :], in0=self.hbcol[:, :],
                                               scalar1=0.5, scalar2=None, op0=OP.mult))

            @blk.sync
            def _(sync):
                self.s_dve.wait(sync)
                for kk in range(3):
                    self.s_in.inc(sync.dma_start(
                        allow_slow_non_contiguous=True,
                        out=self.scol[16 * kk:16 * (kk + 1), 0:1],
                        in_=self.scr[16 * kk:16 * (kk + 1), 27 + kk:28 + kk]))

            @blk.scalar
            def _(act):
                self.s_in.wait(act)
                self.s_act.inc(act.activation(out=self.consts[:, 2:3],
                                              in_=self.consts[:, 1:2], func=AF.Exp))

            @blk.vector
            def _(v):
                self.s_act.wait(v)
                self.s_dve.inc(v.tensor_scalar(out=self.consts[:, 2:3],
                                               in0=self.consts[:, 2:3],
                                               scalar1=-1.0, scalar2=None, op0=OP.mult))

    def emit_ln(self, tag, src_kind, ncols, mask_halo):
        nc = self.nc
        src = {"x": None, "r1": self.h_r1, "r2": self.h_r2}[src_kind]
        n_mm = (ncols + 511) // 512
        inv_d = 1.0 / D
        xst = self.xst

        bxs = [self.c_x.val(j) for j in range(2)]
        bd, ba = self.s_dve.n, self.s_act.n
        x_done = [bxs[dt % 2] + 16 * (dt // 2 + 1) for dt in range(32)]
        cast_done = [bd + (dt + 1) for dt in range(32)]
        sq_done = [ba + (dt + 1) for dt in range(32)]
        bp = self.s_pe.n

        with (nc.psum_tensor("ps_sum_" + tag, [1, 1536], F32) as ps_sum,
              nc.psum_tensor("ps_sq_" + tag, [1, 1536], F32) as ps_sq,
              nc.Block("lnA_" + tag) as blk):
            @blk.sync
            def _(sync):
                for dt in range(32):
                    if dt >= 2:
                        sync.wait_ge(self.s_dve.h, cast_done[dt - 2])
                    buf = self.xtile[:, (dt % 2) * TH:(dt % 2) * TH + ncols]
                    if src_kind == "x":
                        ap = self.i_x[128 * dt:128 * (dt + 1), 0:ncols]
                    else:
                        ap = src[dt][:, 0:ncols]
                    self.c_x.inc(sync.dma_start(out=buf, in_=ap), dt % 2)

            @blk.vector
            def _(v):
                for dt in range(32):
                    self.c_x.wait(v, dt % 2, at=x_done[dt])
                    buf = self.xtile[:, (dt % 2) * TH:(dt % 2) * TH + ncols]
                    self.s_dve.inc(v.tensor_copy(
                        out=self.xn[:, dt * xst:dt * xst + ncols], in_=buf))

            @blk.scalar
            def _(act):
                for dt in range(32):
                    act.wait_ge(self.s_dve.h, cast_done[dt])
                    if dt >= 2:
                        act.wait_ge(self.s_pe.h, bp + dt - 1)
                    sq = self.sqb[:, (dt % 2) * TH:(dt % 2) * TH + ncols]
                    self.s_act.inc(act.activation(
                        out=sq, in_=self.xn[:, dt * xst:dt * xst + ncols],
                        func=AF.Square))

            @blk.tensor
            def _(te):
                for dt in range(32):
                    te.wait_ge(self.s_act.h, sq_done[dt])
                    xb = self.xn[:, dt * xst:dt * xst + ncols]
                    sq = self.sqb[:, (dt % 2) * TH:(dt % 2) * TH + ncols]
                    for j in range(n_mm):
                        c0 = 512 * j
                        c1 = min(512 * (j + 1), ncols)
                        te.matmul(out=ps_sum[0:1, c0:c1], lhsT=self.onescol[:, :],
                                  rhs=xb[:, c0:c1], start=(dt == 0), stop=(dt == 31))
                        mm = te.matmul(out=ps_sq[0:1, c0:c1], lhsT=self.onescol[:, :],
                                       rhs=sq[:, c0:c1], start=(dt == 0), stop=(dt == 31))
                    self.s_pe.inc(mm)

            @blk.scalar
            def _(act):
                self.s_pe.wait(act)
                self.s_act.inc(act.activation(out=self.rows[0:1, 0:ncols],
                                              in_=ps_sum[0:1, 0:ncols],
                                              func=AF.Copy, scale=np.float32(inv_d)))
                act.wait_ge(self.s_act.h, self.s_act.n)
                self.s_act.inc(act.activation(out=self.rows[0:1, TH:TH + ncols],
                                              in_=ps_sq[0:1, 0:ncols],
                                              func=AF.Copy, scale=np.float32(inv_d)))

            @blk.vector
            def _(v):
                self.s_act.wait(v)
                m = self.rows[0:1, 0:ncols]
                e2 = self.rows[0:1, TH:TH + ncols]
                c = self.rows[0:1, 2 * TH:2 * TH + ncols]
                self.s_dve.inc(v.tensor_tensor(out=c, in0=m, in1=m, op=OP.mult))
                self.s_dve.wait(v)
                self.s_dve.inc(v.tensor_tensor(out=e2, in0=e2, in1=c, op=OP.subtract))

            @blk.scalar
            def _(act):
                self.s_dve.wait(act)
                self.s_act.inc(act.activation(out=self.rows[0:1, 2 * TH:2 * TH + ncols],
                                              in_=self.rows[0:1, TH:TH + ncols],
                                              func=AF.Sqrt, bias=np.float32(EPS)))

            @blk.vector
            def _(v):
                self.s_act.wait(v)
                rs = self.rows[0:1, 2 * TH:2 * TH + ncols]
                m = self.rows[0:1, 0:ncols]
                self.s_dve.inc(v.reciprocal(out=rs, in_=rs))
                self.s_dve.wait(v)
                self.s_dve.inc(v.tensor_copy(out=self.rows_bf[0:1, 0:ncols], in_=rs))
                self.s_dve.inc(v.tensor_tensor(out=m, in0=m, in1=rs, op=OP.mult))
                self.s_dve.wait(v)
                self.s_dve.inc(v.tensor_scalar(out=m, in0=m, scalar1=-1.0,
                                               scalar2=None, op0=OP.mult))
                self.s_dve.wait(v)
                self.s_dve.inc(v.tensor_copy(out=self.rows_bf[0:1, TH:TH + ncols],
                                             in_=m))

        with (nc.psum_tensor("ps_rs_" + tag, [128, 1536], F32) as ps_rs,
              nc.psum_tensor("ps_mrs_" + tag, [128, 1536], F32) as ps_mrs,
              nc.Block("lnB_" + tag) as blk):
            @blk.tensor
            def _(te):
                self.s_dve.wait(te)
                for j in range(n_mm):
                    c0, c1 = 512 * j, min(512 * (j + 1), ncols)
                    te.matmul(out=ps_rs[:, c0:c1], lhsT=self.ones1[:, :],
                              rhs=self.rows_bf[0:1, c0:c1], start=True, stop=True)
                    mm = te.matmul(out=ps_mrs[:, c0:c1], lhsT=self.ones1[:, :],
                                   rhs=self.rows_bf[0:1, TH + c0:TH + c1],
                                   start=True, stop=True)
                self.s_pe.inc(mm)

            @blk.vector
            def _(v):
                self.s_pe.wait(v)
                for dt in range(32):
                    xb = self.xn[:, dt * xst:dt * xst + ncols]
                    tmp = self.evt[:, (dt % 2) * TH:(dt % 2) * TH + ncols]
                    self.s_dve.inc(v.tensor_tensor(out=tmp, in0=xb,
                                                   in1=ps_rs[:, 0:ncols], op=OP.mult))
                    self.s_dve.wait(v)
                    if mask_halo:
                        self.s_dve.inc(v.tensor_tensor(out=tmp, in0=tmp,
                                                       in1=ps_mrs[:, 0:ncols], op=OP.add))
                        self.s_dve.wait(v)
                        self.s_dve.inc(v.tensor_scalar(
                            out=self.xn[:, dt * xst:dt * xst + HALO], in0=tmp[:, 0:HALO],
                            scalar1=self.consts[:, 4:5], scalar2=None, op0=OP.mult))
                        self.s_dve.inc(v.tensor_copy(
                            out=self.xn[:, dt * xst + HALO:dt * xst + ncols],
                            in_=tmp[:, HALO:ncols]))
                        self.s_dve.wait(v)
                    else:
                        self.s_dve.inc(v.tensor_tensor(out=xb, in0=tmp,
                                                       in1=ps_mrs[:, 0:ncols], op=OP.add))
                        self.s_dve.wait(v)

    def emit_ffn(self, tag, ncols, res_src, dst):
        nc = self.nc
        n_mm = (ncols + 511) // 512
        xst = self.xst

        with (nc.psum_tensor("ps_lo_" + tag, [48, 1536], F32) as ps_lo,
              nc.Block("lora_" + tag) as blk):
            @blk.tensor
            def _(te):
                self.wait_everything(te)
                for d in range(32):
                    lt = self.la[:, d * 48:(d + 1) * 48]
                    for j in range(n_mm):
                        c0, c1 = 512 * j, min(512 * (j + 1), ncols)
                        mm = te.matmul(out=ps_lo[0:48, c0:c1], lhsT=lt,
                                       rhs=self.xn[:, d * xst + c0:d * xst + c1],
                                       start=(d == 0), stop=(d == 31))
                self.s_pe.inc(mm)

            @blk.vector
            def _(v):
                self.s_pe.wait(v)
                self.s_dve.inc(v.tensor_scalar(out=self.xa[0:48, 0:ncols],
                                               in0=ps_lo[0:48, 0:ncols],
                                               scalar1=self.scol[:, 0:1], scalar2=None,
                                               op0=OP.mult))

        bws = [self.c_w.val(j) for j in range(2)]
        bxs = [self.c_x.val(j) for j in range(2)]
        bsp = [self.c_sp.val(j) for j in range(2)]
        bp2, bd2 = self.s_pe2.n, self.s_dve2.n
        w_done = [bws[dt % 2] + 16 * (dt // 2 + 1) for dt in range(32)]
        x_done = [bxs[dt % 2] + 16 * (dt // 2 + 1) for dt in range(32)]
        sp_done = [bsp[dt % 2] + 16 * (dt // 2 + 1) for dt in range(32)]
        pe_done = [bp2 + (dt + 1) for dt in range(32)]
        ev_done = [bd2 + 2 * (dt + 1) for dt in range(32)]

        with (nc.psum_tensor("ps_mA_" + tag, [128, 1536], F32) as psA,
              nc.psum_tensor("ps_mB_" + tag, [128, 1536], F32) as psB,
              nc.Block("ffn_" + tag) as blk):
            @blk.gpsimd
            def _(gp):
                for dt in range(32):
                    if dt >= 2:
                        gp.wait_ge(self.s_pe2.h, pe_done[dt - 2])
                    slab = self.wslab[:, (dt % 2) * 4224:(dt % 2) * 4224 + 4224]
                    self.c_w.inc(gp.dma_start(out=slab, in_=self.i_wffn[dt]), dt % 2)

            @blk.sync
            def _(sync):
                for dt in range(32):
                    if dt >= 2:
                        sync.wait_ge(self.s_dve2.h, ev_done[dt - 2])
                    buf = self.xtile[:, (dt % 2) * TH:(dt % 2) * TH + ncols]
                    if res_src == "x":
                        ap = self.i_x[128 * dt:128 * (dt + 1), 0:ncols]
                    else:
                        ap = self.h_r2[dt][:, 0:ncols]
                    self.c_x.inc(sync.dma_start(out=buf, in_=ap), dt % 2)
                    if dt >= 1:
                        sync.wait_ge(self.s_dve2.h, ev_done[dt - 1])
                        eb = self.evt[:, ((dt - 1) % 2) * TH:((dt - 1) % 2) * TH + ncols]
                        self.c_sp.inc(sync.dma_start(out=dst[dt - 1][:, 0:ncols],
                                                     in_=eb), (dt - 1) % 2)
                sync.wait_ge(self.s_dve2.h, ev_done[31])
                eb = self.evt[:, (31 % 2) * TH:(31 % 2) * TH + ncols]
                self.c_sp.inc(sync.dma_start(out=dst[31][:, 0:ncols], in_=eb), 31 % 2)
                self.c_sp.wait(sync, 0)
                self.c_sp.wait(sync, 1)

            @blk.tensor
            def _(te):
                self.s_dve.wait(te)
                for dt in range(32):
                    self.c_w.wait(te, dt % 2, at=w_done[dt])
                    if dt >= 2:
                        te.wait_ge(self.s_dve2.h, ev_done[dt - 2])
                    ps = psA if dt % 2 == 0 else psB
                    slab = self.wslab[:, (dt % 2) * 4224:(dt % 2) * 4224 + 4224]
                    for d in range(33):
                        lt = slab[:, d * 128:(d + 1) * 128]
                        rhs_base = (self.xn[:, d * xst:d * xst + ncols] if d < 32
                                    else self.xa[:, 0:ncols])
                        for j in range(n_mm):
                            c0, c1 = 512 * j, min(512 * (j + 1), ncols)
                            mm = te.matmul(out=ps[:, c0:c1], lhsT=lt,
                                           rhs=rhs_base[:, c0:c1],
                                           start=(d == 0), stop=(d == 32))
                    self.s_pe2.inc(mm)

            @blk.vector
            def _(v):
                for dt in range(32):
                    v.wait_ge(self.s_pe2.h, pe_done[dt])
                    self.c_x.wait(v, dt % 2, at=x_done[dt])
                    if dt >= 2:
                        self.c_sp.wait(v, dt % 2, at=sp_done[dt - 2])
                    ps = psA if dt % 2 == 0 else psB
                    eb = self.evt[:, (dt % 2) * TH:(dt % 2) * TH + ncols]
                    self.s_dve2.inc(v.tensor_scalar(out=eb, in0=ps[:, 0:ncols],
                                                    scalar1=0.5,
                                                    scalar2=self.hbcol[:, dt:dt + 1],
                                                    op0=OP.mult, op1=OP.add))
                    self.s_dve2.wait(v)
                    self.s_dve2.inc(v.tensor_tensor(
                        out=eb, in0=eb,
                        in1=self.xtile[:, (dt % 2) * TH:(dt % 2) * TH + ncols],
                        op=OP.add))

    def emit_p2(self):
        nc = self.nc
        ev = {}
        DT0, CU0, SC0 = 0, T, 2 * T

        def cw_col(ct, kk):
            return self.convw[:, ct * CK + kk:ct * CK + kk + 1]

        with (nc.psum_tensor("psP_A", [128, 1536], F32) as psPA,
              nc.psum_tensor("psP_B", [128, 1536], F32) as psPB,
              nc.psum_tensor("psS1", [128, 512], F32) as psS1,
              nc.psum_tensor("psS2", [128, 512], F32) as psS2,
              nc.Block("p2") as blk):

            self.barrier(blk)

            @blk.vector
            def _(v):
                self.s_dve2.inc(v.memset(self.runbf[:, :], 0.0))

            def gp_prefetch(ft):
                if ft >= N_FT:
                    return
                @blk.gpsimd
                def _(gp):
                    if ft >= 2:
                        gp.wait_ge(self.s_pe2.h, ev[("proj", ft - 2)])
                        gp.wait_ge(self.s_dve2.h, ev[("psfree", ft - 2)])
                    slab = self.wslab2[:, (ft % 2) * 4096:(ft % 2) * 4096 + 4096]
                    self.c_w.inc(gp.dma_start(out=slab, in_=self.i_wproj[ft]), ft % 2)
                    ev[("slab", ft)] = self.c_w.val(ft % 2)

            def te_proj(ft):
                @blk.tensor
                def _(te):
                    self.c_w.wait(te, ft % 2, at=ev[("slab", ft)])
                    if ft >= 2:
                        te.wait_ge(self.s_dve2.h, ev[("psfree", ft - 2)])
                        te.wait_ge(self.s_act2.h, ev[("psfree_a", ft - 2)])
                    ps = psPA if ft % 2 == 0 else psPB
                    slab = self.wslab2[:, (ft % 2) * 4096:(ft % 2) * 4096 + 4096]
                    for d in range(32):
                        lt = slab[:, d * 128:(d + 1) * 128]
                        for (c0, c1) in ((0, 512), (512, 1024), (1024, TH)):
                            mm = te.matmul(out=ps[:, c0:c1], lhsT=lt,
                                           rhs=self.xn[:, d * TH + c0:d * TH + c1],
                                           start=(d == 0), stop=(d == 31))
                    self.s_pe2.inc(mm)
                    ev[("proj", ft)] = self.s_pe2.n

            def dve_conv(ft, ct):
                ps = psPA if ft % 2 == 0 else psPB
                @blk.vector
                def _(v):
                    cbuf = (self.cacc[:, 0:T] if ft % 2 == 0
                            else self.planes[:, SC0:SC0 + T])
                    v.wait_ge(self.s_pe2.h, ev[("proj", ft)])
                    if ft == 1:
                        v.wait_ge(self.s_dve.h, ev[("tmev", 7)])
                        v.wait_ge(self.s_dve.h, ev[("wglog", )] if False else ev["wglog"])
                    if ("siluref", ft - 2) in ev:
                        v.wait_ge(self.s_act2.h, ev[("siluref", ft - 2)])
                    self.s_dve2.inc(v.tensor_scalar(
                        out=cbuf, in0=ps[:, 125:125 + T],
                        scalar1=cw_col(ct, 0), scalar2=None, op0=OP.mult))
                    for kk in range(1, CK):
                        self.s_dve2.wait(v)
                        self.s_dve2.inc(v.scalar_tensor_tensor(
                            out=cbuf, in0=ps[:, 125 + kk:125 + kk + T],
                            scalar=cw_col(ct, kk), in1=cbuf,
                            op0=OP.mult, op1=OP.add))
                    ev[("psfree", ft)] = self.s_dve2.n
                    ev[("conv", ft)] = self.s_dve2.n

            def act_silu(ft, ct, dst_ap, hs_k=None):
                @blk.scalar
                def _(act):
                    act.wait_ge(self.s_dve2.h, ev[("conv", ft)])
                    if hs_k is not None and ("y1ev", hs_k - 2, NCH - 1) in ev:
                        act.wait_ge(self.s_dve2.h, ev[("y1ev", hs_k - 2, NCH - 1)])
                    cbuf = (self.cacc[:, 0:T] if ft % 2 == 0
                            else self.planes[:, SC0:SC0 + T])
                    self.s_act2.inc(act.activation(
                        out=dst_ap, in_=cbuf,
                        func=AF.Silu, bias=self.convb[:, ct:ct + 1]))
                    ev[("silu", ft)] = self.s_act2.n
                    ev[("siluref", ft)] = self.s_act2.n
                    ev[("psfree_a", ft)] = self.s_act2.n

            gp_prefetch(0)
            gp_prefetch(1)
            te_proj(0)

            @blk.vector
            def _(v):
                v.wait_ge(self.s_pe2.h, ev[("proj", 0)])
                self.s_dve2.inc(v.tensor_copy(out=self.planes[:, DT0:DT0 + T],
                                              in_=psPA[:, 128:TH]))
                ev[("psfree", 0)] = self.s_dve2.n
                ev[("dtraw", 0)] = self.s_dve2.n
            ev[("psfree_a", 0)] = 0

            @blk.scalar
            def _(act):
                act.wait_ge(self.s_dve2.h, ev[("dtraw", 0)])
                self.s_act.inc(act.activation(out=self.planes[:, SC0:SC0 + T],
                                              in_=self.planes[:, DT0:DT0 + T],
                                              func=AF.Abs, bias=self.consts[:, 0:1]))
                self.s_act.wait(act)
                self.s_act.inc(act.activation(out=self.planes[:, SC0:SC0 + T],
                                              in_=self.planes[:, SC0:SC0 + T],
                                              func=AF.Exp, scale=np.float32(-1.0)))
                self.s_act.wait(act)
                self.s_act.inc(act.activation(out=self.planes[:, SC0:SC0 + T],
                                              in_=self.planes[:, SC0:SC0 + T],
                                              func=AF.Ln, bias=np.float32(1.0)))
                self.s_act.wait(act)
                self.s_act.inc(act.activation(out=self.cacc[:, 0:T],
                                              in_=self.planes[:, DT0:DT0 + T],
                                              func=AF.Relu, bias=self.consts[:, 0:1]))
                ev["sp_parts"] = self.s_act.n

            @blk.vector
            def _(v):
                v.wait_ge(self.s_act.h, ev["sp_parts"])
                self.s_dve.inc(v.tensor_add(out=self.planes[:, DT0:DT0 + T],
                                            in0=self.cacc[:, 0:T],
                                            in1=self.planes[:, SC0:SC0 + T]))
                self.s_dve.wait(v)
                self.s_dve.inc(v.tensor_scalar(out=self.planes[:, SC0:SC0 + T],
                                               in0=self.planes[:, DT0:DT0 + T],
                                               scalar1=self.consts[:, 2:3], scalar2=None,
                                               op0=OP.mult))
                self.s_dve.wait(v)
                for c in range(NCH):
                    s0 = SC0 + c * CS
                    self.s_dve.inc(v.tensor_tensor_scan(
                        out=self.planes[:, CU0 + c * CS:CU0 + (c + 1) * CS],
                        data0=self.planes[:, s0:s0 + CS],
                        data1=self.planes[:, s0:s0 + CS],
                        initial=0.0, op0=OP.add, op1=OP.bypass))
                ev["cum"] = self.s_dve.n

            @blk.scalar
            def _(act):
                act.wait_ge(self.s_dve.h, ev["cum"])
                self.s_act.inc(act.activation(out=self.planes_bf[:, 0:T],
                                              in_=self.planes[:, SC0:SC0 + T],
                                              func=AF.Exp))
                ev["edA"] = self.s_act.n
                for c in range(NCH):
                    act.wait_ge(self.s_act.h, self.s_act.n)
                    self.s_act.inc(act.activation(
                        out=self.cacc[:, c * CS:(c + 1) * CS],
                        in_=self.planes[:, CU0 + c * CS:CU0 + (c + 1) * CS],
                        func=AF.Exp, scale=np.float32(-1.0),
                        bias=self.planes[:, CU0 + (c + 1) * CS - 1:CU0 + (c + 1) * CS]))
                ev["wexp"] = self.s_act.n

            @blk.vector
            def _(v):
                v.wait_ge(self.s_act.h, ev["wexp"])
                self.s_dve.inc(v.tensor_tensor(out=self.planes[:, SC0:SC0 + T],
                                               in0=self.cacc[:, 0:T],
                                               in1=self.planes[:, DT0:DT0 + T],
                                               op=OP.mult))
                self.s_dve.wait(v)
                self.s_dve.inc(v.tensor_copy(out=self.scr[:, 30:31],
                                             in_=self.planes[:, CU0 + CS - 1:CU0 + CS]))
                for c in range(1, NCH):
                    self.s_dve.wait(v)
                    self.s_dve.inc(v.tensor_add(
                        out=self.scr[:, 30:31], in0=self.scr[:, 30:31],
                        in1=self.planes[:, CU0 + (c + 1) * CS - 1:CU0 + (c + 1) * CS]))
                ev["wglog"] = self.s_dve.n

            for tc in range(8):
                @blk.tensor
                def _(te, tc=tc):
                    te.wait_ge(self.s_dve.h, ev["wglog"] if tc == 0 else ev[("tmev", tc - 1)])
                    te.transpose(out=psS2[:, 0:128],
                                 in_=self.planes[:, DT0 + tc * 128:DT0 + (tc + 1) * 128],
                                 identity=self.idf32[:, :])
                    self.s_pe.inc(te.transpose(
                        out=psS2[:, 128:256],
                        in_=self.planes[:, SC0 + tc * 128:SC0 + (tc + 1) * 128],
                        identity=self.idf32[:, :]))
                    ev[("tmtp", tc)] = self.s_pe.n

                @blk.vector
                def _(v, tc=tc):
                    v.wait_ge(self.s_pe.h, ev[("tmtp", tc)])
                    self.s_dve.inc(v.tensor_copy(
                        out=self.tmcols[:, tc * 128:(tc + 1) * 128],
                        in_=psS2[:, 0:128]))
                    self.s_dve.inc(v.tensor_copy(
                        out=self.tmcols[:, WT0 + tc * 128:WT0 + (tc + 1) * 128],
                        in_=psS2[:, 128:256]))
                    ev[("tmev", tc)] = self.s_dve.n

            for c in range(NCH):
                @blk.tensor
                def _(te, c=c):
                    te.wait_ge(self.s_dve.h, ev[("tmev", 7)] if c == 0 else ev[("tcev", c - 1)])
                    self.s_pe.inc(te.transpose(
                        out=psS2[0:1, 0:128],
                        in_=self.planes[:, CU0 + (c + 1) * CS - 1:CU0 + (c + 1) * CS],
                        identity=self.idf32[:, :]))
                    ev[("cdtp", c)] = self.s_pe.n

                @blk.scalar
                def _(act, c=c):
                    act.wait_ge(self.s_pe.h, ev[("cdtp", c)])
                    self.s_act.inc(act.activation(
                        out=self.planes_bf[0:1, T + c * 128:T + (c + 1) * 128],
                        in_=psS2[0:1, 0:128], func=AF.Exp))
                    ev[("cdrow", c)] = self.s_act.n

                @blk.tensor
                def _(te, c=c):
                    te.wait_ge(self.s_act.h, ev[("cdrow", c)])
                    self.s_pe.inc(te.matmul(
                        out=psS2[:, 128:256],
                        lhsT=self.ones1[:, :],
                        rhs=self.planes_bf[0:1, T + c * 128:T + (c + 1) * 128],
                        start=True, stop=True))
                    ev[("tck1", c)] = self.s_pe.n

                @blk.vector
                def _(v, c=c):
                    v.wait_ge(self.s_pe.h, ev[("tck1", c)])
                    self.s_dve.inc(v.tensor_copy(
                        out=self.Tc_all[:, c * 128:(c + 1) * 128],
                        in_=psS2[:, 128:256]))
                    ev[("tcev", c)] = self.s_dve.n

            for ft in range(1, 17):
                gp_prefetch(ft + 1)
                te_proj(ft)
                if ft <= 8:
                    g = ft - 1
                    ct = 64 + g
                    dst = self.B_fm[:, g * T:(g + 1) * T]
                else:
                    g = ft - 9
                    ct = 72 + g
                    dst = self.C_fm[:, g * T:(g + 1) * T]
                dve_conv(ft, ct)
                act_silu(ft, ct, dst)

            for ft in range(17, 81):
                k = ft - 17
                gp_prefetch(ft + 1)
                te_proj(ft)
                if k >= 1:
                    self._p2_ssd_pair(blk, ev, psS1, psS2, k - 1)
                dve_conv(ft, 0 + k)
                act_silu(ft, 0 + k, self.hs_fm[:, (k % 2) * T:(k % 2) * T + T],
                         hs_k=k)
                self._p2_hs_stage(blk, ev, k)
            self._p2_ssd_pair(blk, ev, psS1, psS2, 63)

            @blk.gpsimd
            def _(gp):
                gp.wait_ge(self.s_dve2.h, ev[("run", 63, 3)])
                self.c_ag.inc(gp.dma_start(out=self.h_agin[:, 0:INTER],
                                           in_=self.runbf[:, :]), 0)
                self.c_ag.inc(gp.dma_start(allow_slow_non_contiguous=True,
                                           out=self.h_agin[:, INTER:INTER + 1],
                                           in_=self.scr[:, 30:31]), 0)
                self.c_ag.wait(gp, 0)
                gp.collective_compute(
                    "AllGather", OP.bypass,
                    replica_groups=[[0, 1, 2, 3], [4, 5, 6, 7]],
                    ins=[self.h_agin.opt()], outs=[self.h_agout.opt()],
                ).then_inc(self.s_cc.h, 1)
                self.s_cc.n += 1

            for ft in range(81, 145):
                gt = ft - 81
                gp_prefetch(ft + 1)
                te_proj(ft)
                ps = psPA if ft % 2 == 0 else psPB

                @blk.scalar
                def _(act, ft=ft, gt=gt, ps=ps):
                    act.wait_ge(self.s_pe2.h, ev[("proj", ft)])
                    if gt >= 2:
                        self.c_gt.wait(act, gt % 2, at=ev[("gtsp", gt - 2)])
                    self.s_act2.inc(act.activation(
                        out=self.gatest[:, (gt % 2) * T:(gt % 2) * T + T],
                        in_=ps[:, 128:TH], func=AF.Silu))
                    ev[("psfree_a", ft)] = self.s_act2.n
                    ev[("gsilu", gt)] = self.s_act2.n
                ev[("psfree", ft)] = self.s_dve2.n

                @blk.sync
                def _(sync, gt=gt):
                    sync.wait_ge(self.s_act2.h, ev[("gsilu", gt)])
                    self.c_gt.inc(sync.dma_start(
                        out=self.h_gate[gt],
                        in_=self.gatest[:, (gt % 2) * T:(gt % 2) * T + T]), gt % 2)
                    ev[("gtsp", gt)] = self.c_gt.val(gt % 2)

    def _p2_hs_stage(self, blk, ev, k):
        ft = 17 + k

        @blk.scalar
        def _(act):
            act.wait_ge(self.s_act2.h, ev[("silu", ft)])
            if k >= 2:
                act.wait_ge(self.s_pe.h, ev[("ssdpe", k - 2)])
            srcb = self.hs_fm[:, (k % 2) * T:(k % 2) * T + T]
            dst = self.hs_tm[:, (k % 2) * T:(k % 2) * T + T]
            for tc in range(8):
                self.c_tp.inc(act.dma_start(out=dst[:, tc * 128:(tc + 1) * 128],
                                            in_=srcb[:, tc * 128:(tc + 1) * 128],
                                            transpose=True), k % 2)
            ev[("hstm", k)] = self.c_tp.val(k % 2)
            act.wait_ge(self.s_act.h, ev["edA"])
            st = self.sstage
            if ("edbc", k - 1, NCH - 1) in ev:
                act.wait_ge(self.s_pe.h, ev[("edbc", k - 1, NCH - 1)])
            self.c_st.inc(act.dma_start(
                allow_slow_non_contiguous=True,
                out=st[0:1, 0:T],
                in_=self.planes_bf[2 * k:2 * k + 1, 0:T]), 0)
            self.c_st.inc(act.dma_start(
                allow_slow_non_contiguous=True,
                out=st[0:1, T:2 * T],
                in_=self.planes_bf[2 * k + 1:2 * k + 2, 0:T]), 0)
            ev[("edrow", k)] = self.c_st.val(0)

        if k % 8 == 0:
            g = k // 8

            @blk.scalar
            def _(act):
                act.wait_ge(self.s_act2.h, ev[("silu", 1 + g)])
                if g >= 1:
                    act.wait_ge(self.s_pe.h, ev[("stmm", 8 * g - 1, NCH - 1)])
                srcb = self.B_fm[:, g * T:(g + 1) * T]
                for tc in range(8):
                    self.c_bt.inc(act.dma_start(
                        out=self.B_tm[:, tc * 128:(tc + 1) * 128],
                        in_=srcb[:, tc * 128:(tc + 1) * 128],
                        transpose=True), g % 2)
                ev[("btm", g)] = self.c_bt.val(g % 2)

    def _p2_ssd_pair(self, blk, ev, psS1, psS2, k):
        g = k // 8
        ft = 17 + k

        if k % 8 == 0:
            for c in range(NCH):
                @blk.tensor
                def _(te, c=c):
                    if c == 0:
                        te.wait_ge(self.s_act2.h, ev[("silu", 1 + g)])
                        te.wait_ge(self.s_act2.h, ev[("silu", 9 + g)])
                        if ("y1ev", k - 1, NCH - 1) in ev:
                            te.wait_ge(self.s_dve2.h, ev[("y1ev", k - 1, NCH - 1)])
                    else:
                        te.wait_ge(self.s_dve.h, ev[("rawSev", g, c - 1)])
                    Bg = self.B_fm[:, g * T:(g + 1) * T]
                    Cg = self.C_fm[:, g * T:(g + 1) * T]
                    te.matmul(out=psS2[:, 0:256],
                              lhsT=Bg[:, c * CS:c * CS + 128],
                              rhs=Cg[:, c * CS:(c + 1) * CS], start=True, stop=True)
                    self.s_pe.inc(te.matmul(out=psS2[:, 256:384],
                                            lhsT=Bg[:, c * CS + 128:(c + 1) * CS],
                                            rhs=Cg[:, c * CS + 128:(c + 1) * CS],
                                            start=True, stop=True))
                    ev[("rawSmm", g, c)] = self.s_pe.n

                @blk.vector
                def _(v, c=c):
                    v.wait_ge(self.s_pe.h, ev[("rawSmm", g, c)])
                    self.s_dve.inc(v.tensor_copy(
                        out=self.rawS[:, c * 384:c * 384 + 384],
                        in_=psS2[:, 0:384]))
                    ev[("rawSev", g, c)] = self.s_dve.n
            ev[("rawSdone", g)] = ev[("rawSev", g, NCH - 1)]

        for c in range(NCH):
            @blk.tensor
            def _(te, c=c):
                self.c_st.wait(te, 0, at=ev[("edrow", k)])
                if c == 0:
                    if k >= 1:
                        te.wait_ge(self.s_dve2.h, ev[("stev", k - 1, NCH - 1)])
                else:
                    te.wait_ge(self.s_dve2.h, ev[("stev", k, c - 1)])
                st = self.sstage
                te.matmul(out=psS1[:, 0:256],
                          lhsT=self.ones1[:, :],
                          rhs=st[0:1, c * CS:(c + 1) * CS],
                          start=True, stop=True)
                self.s_pe.inc(te.matmul(
                    out=psS1[:, 256:512], lhsT=self.ones1[:, :],
                    rhs=st[0:1, T + c * CS:T + (c + 1) * CS],
                    start=True, stop=True))
                ev[("edbc", k, c)] = self.s_pe.n

            @blk.vector
            def _(v, c=c):
                v.wait_ge(self.s_pe.h, ev[("edbc", k, c)])
                if ("y1ev", k, c - 1) in ev:
                    v.wait_ge(self.s_dve2.h, ev[("y1ev", k, c - 1)])
                for hp in range(2):
                    e0 = hp * 256
                    d0 = hp * 384
                    self.s_dve2.inc(v.tensor_tensor_scan(
                        out=self.d1b[:, d0:d0 + 256],
                        data0=psS1[:, e0:e0 + 256], data1=self.masks[:, 0:256],
                        initial=0.0, op0=OP.mult, op1=OP.add))
                    self.s_dve2.inc(v.tensor_tensor_scan(
                        out=self.d1b[:, d0 + 256:d0 + 384],
                        data0=psS1[:, e0 + 128:e0 + 256], data1=self.masks[:, 384:512],
                        initial=0.0, op0=OP.mult, op1=OP.add))
                self.s_dve2.wait(v)
                ev[("scan", k, c)] = self.s_dve2.n
                for hp in range(2):
                    h = 2 * k + hp
                    d0 = hp * 384
                    for jt in range(2):
                        tcol = self.tmcols[:, (2 * c + jt) * 128 + h:(2 * c + jt) * 128 + h + 1]
                        self.s_dve2.inc(v.scalar_tensor_tensor(
                            out=self.stb[:, d0 + jt * 256:d0 + jt * 256 + (256 if jt == 0 else 128)],
                            in0=self.d1b[:, d0 + jt * 256:d0 + jt * 256 + (256 if jt == 0 else 128)],
                            scalar=tcol,
                            in1=self.rawS[:, c * 384 + jt * 256:c * 384 + jt * 256 + (256 if jt == 0 else 128)],
                            op0=OP.mult, op1=OP.mult))
                self.s_dve2.wait(v)
                ev[("st", k, c)] = self.s_dve2.n

            @blk.tensor
            def _(te, c=c):
                te.wait_ge(self.s_dve2.h, ev[("st", k, c)])
                self.c_tp.wait(te, k % 2, at=ev[("hstm", k)])
                if k % 8 == 0 and c == 0:
                    te.wait_ge(self.s_dve.h, ev[("rawSdone", g)])
                if ("y1ev", k, c - 1) in ev:
                    te.wait_ge(self.s_dve2.h, ev[("y1ev", k, c - 1)])
                elif ("y1ev", k - 1, NCH - 1) in ev:
                    te.wait_ge(self.s_dve2.h, ev[("y1ev", k - 1, NCH - 1)])
                hst = self.hs_tm[:, (k % 2) * T:(k % 2) * T + T]
                for hp in range(2):
                    d0 = hp * 384
                    lt0 = hst[:, (2 * c) * 128 + hp * 64:(2 * c) * 128 + hp * 64 + 64]
                    lt1 = hst[:, (2 * c + 1) * 128 + hp * 64:(2 * c + 1) * 128 + hp * 64 + 64]
                    te.matmul(out=psS2[hp * 64:hp * 64 + 64, 0:256],
                              lhsT=lt0, rhs=self.stb[:, d0:d0 + 256],
                              start=True, stop=False)
                    mm = te.matmul(out=psS2[hp * 64:hp * 64 + 64, 128:256],
                                   lhsT=lt1, rhs=self.stb[:, d0 + 256:d0 + 384],
                                   start=False, stop=True)
                self.s_pe.inc(mm)
                ev[("ymm", k, c)] = self.s_pe.n

            @blk.vector
            def _(v, c=c):
                self.c_bt.wait(v, g % 2, at=ev[("btm", g)])
                self.c_tp.wait(v, k % 2, at=ev[("hstm", k)])
                v.wait_ge(self.s_pe.h, ev.get(("stmm", k, c - 1), ev.get(("stmm", k - 1, NCH - 1), 0)))
                hst = self.hs_tm[:, (k % 2) * T:(k % 2) * T + T]
                for hp in range(2):
                    h = 2 * k + hp
                    for jt in range(2):
                        wcol = self.tmcols[:, WT0 + (2 * c + jt) * 128 + h:WT0 + (2 * c + jt) * 128 + h + 1]
                        self.s_dve2.inc(v.tensor_scalar(
                            out=self.wbb[:, (hp * 2 + jt) * 64:(hp * 2 + jt) * 64 + 64],
                            in0=hst[:, (2 * c + jt) * 128 + hp * 64:(2 * c + jt) * 128 + hp * 64 + 64],
                            scalar1=wcol, scalar2=None, op0=OP.mult))
                self.s_dve2.wait(v)
                ev[("wb", k, c)] = self.s_dve2.n

            @blk.vector
            def _(v, c=c):
                v.wait_ge(self.s_pe.h, ev[("ymm", k, c)])
                if c == 0:
                    self.c_y1.wait(v, k % 2)
                self.s_dve2.inc(v.scalar_tensor_tensor(
                    out=self.y1st[:, (k % 2) * T + c * CS:(k % 2) * T + (c + 1) * CS],
                    in0=self.hs_fm[:, (k % 2) * T + c * CS:(k % 2) * T + (c + 1) * CS],
                    scalar=self.drep[:, k:k + 1], in1=psS2[:, 0:256],
                    op0=OP.mult, op1=OP.add))
                ev[("y1ev", k, c)] = self.s_dve2.n

            @blk.tensor
            def _(te, c=c):
                te.wait_ge(self.s_dve2.h, ev[("wb", k, c)])
                te.wait_ge(self.s_dve2.h, ev[("scan", k, c)])
                for hp in range(2):
                    te.matmul(out=psS1[:, hp * 64:hp * 64 + 64],
                              lhsT=self.B_tm[:, (2 * c) * 128:(2 * c) * 128 + 128],
                              rhs=self.wbb[:, (hp * 2) * 64:(hp * 2) * 64 + 64],
                              start=True, stop=False)
                    mm = te.matmul(out=psS1[:, hp * 64:hp * 64 + 64],
                                   lhsT=self.B_tm[:, (2 * c + 1) * 128:(2 * c + 1) * 128 + 128],
                                   rhs=self.wbb[:, (hp * 2 + 1) * 64:(hp * 2 + 1) * 64 + 64],
                                   start=False, stop=True)
                self.s_pe.inc(mm)
                ev[("stmm", k, c)] = self.s_pe.n
                if c == NCH - 1:
                    ev[("ssdpe", k)] = self.s_pe.n

            @blk.vector
            def _(v, c=c):
                v.wait_ge(self.s_pe.h, ev[("stmm", k, c)])
                slot = (k * NCH + c) % 2
                self.c_ss.wait(v, slot)
                self.s_dve2.inc(v.tensor_copy(
                    out=self.sst[:, slot * 128:slot * 128 + 128],
                    in_=psS1[:, 0:128]))
                self.s_dve2.wait(v)
                self.s_dve2.inc(v.tensor_tensor(
                    out=self.sscr[:, 0:128].rearrange("p (a b) -> p a b", a=2),
                    in0=self.runbf[:, k * 128:(k + 1) * 128].rearrange("p (a b) -> p a b", a=2),
                    in1=self.Tc_all[:, c * 128 + 2 * k:c * 128 + 2 * k + 2]
                        .unsqueeze(2).broadcast_to((128, 2, 64)),
                    op=OP.mult))
                self.s_dve2.wait(v)
                self.s_dve2.inc(v.tensor_add(
                    out=self.runbf[:, k * 128:(k + 1) * 128],
                    in0=self.sscr[:, 0:128],
                    in1=self.sst[:, slot * 128:slot * 128 + 128]))
                self.s_dve2.wait(v)
                ev[("stev", k, c)] = self.s_dve2.n
                if c == NCH - 1 and k == 63:
                    ev[("run", 63, 3)] = self.s_dve2.n
                else:
                    ev[("run", k, c)] = self.s_dve2.n

            @blk.sync
            def _(sync, c=c):
                slot = (k * NCH + c) % 2
                sync.wait_ge(self.s_dve2.h, ev[("stev", k, c)])
                self.c_ss.inc(sync.dma_start(
                    out=self.h_states[c][:, k * 128:(k + 1) * 128],
                    in_=self.sst[:, slot * 128:slot * 128 + 128]), slot)

        @blk.sync
        def _(sync):
            sync.wait_ge(self.s_dve2.h, ev[("y1ev", k, NCH - 1)])
            self.c_y1.inc(sync.dma_start(
                out=self.h_y1[k],
                in_=self.y1st[:, (k % 2) * T:(k % 2) * T + T]), k % 2)
            ev[("y1sp", k)] = self.c_y1.val(k % 2)

    def emit_pb(self):
        nc = self.nc
        ev = {}

        with (nc.psum_tensor("psB1", [128, 1024], F32) as psB1,
              nc.psum_tensor("psB2", [128, 1024], F32) as psB2,
              nc.psum_tensor("psN", [1, 512], F32) as psN,
              nc.psum_tensor("psRq", [128, 512], F32) as psRq,
              nc.psum_tensor("psW", [128, 512], F32) as psW,
              nc.Block("pb") as blk):

            @blk.gpsimd
            def _(gp):
                gp.wait_ge(self.s_cc.h, self.s_cc.n)
                for j in range(4):
                    self.c_ag.inc(gp.dma_start(
                        allow_slow_non_contiguous=True,
                        out=self.scr[:, 32 + j:33 + j],
                        in_=self.h_agout[128 * j:128 * (j + 1), INTER:INTER + 1]), 0)
                ev["glogs"] = self.c_ag.val(0)

            @blk.vector
            def _(v):
                self.c_ag.wait(v, 0, at=ev["glogs"])
                self.s_dve.inc(v.tensor_tensor(out=self.scr[:, 36:40],
                                               in0=self.scr[:, 32:36],
                                               in1=self.consts[:, 5:9], op=OP.mult))
                self.s_dve.wait(v)
                v.memset(self.scr[:, 43:44], 0.0)
                self.s_dve.inc(v.tensor_copy(out=self.scr[:, 42:43],
                                             in_=self.scr[:, 39:40]))
                self.s_dve.wait(v)
                self.s_dve.inc(v.tensor_add(out=self.scr[:, 41:42],
                                            in0=self.scr[:, 38:39],
                                            in1=self.scr[:, 42:43]))
                self.s_dve.wait(v)
                self.s_dve.inc(v.tensor_add(out=self.scr[:, 40:41],
                                            in0=self.scr[:, 37:38],
                                            in1=self.scr[:, 41:42]))
                ev["tj"] = self.s_dve.n

            @blk.scalar
            def _(act):
                act.wait_ge(self.s_dve.h, ev["tj"])
                self.s_act.inc(act.activation(out=self.scr[:, 44:48],
                                              in_=self.scr[:, 40:44], func=AF.Exp))
                ev["wexp"] = self.s_act.n

            @blk.vector
            def _(v):
                v.wait_ge(self.s_act.h, ev["wexp"])
                self.s_dve.inc(v.tensor_tensor(out=self.scr[:, 44:48],
                                               in0=self.scr[:, 44:48],
                                               in1=self.consts[:, 5:9], op=OP.mult))
                ev["wj"] = self.s_dve.n

            @blk.gpsimd
            def _(gp):
                gp.wait_ge(self.s_dve.h, ev["wj"])
                for j in range(4):
                    self.c_sg.inc(gp.dma_start(
                        allow_slow_non_contiguous=True,
                        out=self.wrow[0:1, 128 * j:128 * j + 128],
                        in_=self.scr[:, 44 + j:45 + j]), 1)
                ev["wrows"] = self.c_sg.val(1)

            for j in range(4):
                @blk.tensor
                def _(te, j=j):
                    self.c_sg.wait(te, 1, at=ev["wrows"])
                    if j > 0:
                        te.wait_ge(self.s_dve.h, ev[("sin", j - 1)])
                    self.s_pe.inc(te.matmul(out=psW[:, 0:128], lhsT=self.ones1[:, :],
                                            rhs=self.wrow[0:1, 128 * j:128 * j + 128],
                                            start=True, stop=True))
                    ev[("wk1", j)] = self.s_pe.n

                for pc in range(8):
                    @blk.sync
                    def _(sync, j=j, pc=pc):
                        if j == 0 and pc == 0:
                            self.wait_everything(sync)
                        slot = pc % 2
                        if (j * 8 + pc) >= 2:
                            self.s_dve.wait(sync, at=ev.get(("sacc", j, pc - 2),
                                            ev.get(("sacc", j - 1, 6 + pc), 0)))
                        self.c_ld.inc(sync.dma_start(
                            out=self.sjp[:, slot * 1024:slot * 1024 + 1024],
                            in_=self.h_agout[128 * j:128 * (j + 1),
                                             1024 * pc:1024 * (pc + 1)]), slot)
                        ev[("sj", j, pc)] = self.c_ld.val(slot)

                    @blk.vector
                    def _(v, j=j, pc=pc):
                        slot = pc % 2
                        self.c_ld.wait(v, slot, at=ev[("sj", j, pc)])
                        v.wait_ge(self.s_pe.h, ev[("wk1", j)])
                        if j == 0:
                            self.s_dve.inc(v.tensor_tensor(
                                out=self.prev[:, 1024 * pc:1024 * (pc + 1)]
                                    .rearrange("p (a b) -> p a b", a=16),
                                in0=self.sjp[:, slot * 1024:slot * 1024 + 1024]
                                    .rearrange("p (a b) -> p a b", a=16),
                                in1=psW[:, 16 * pc:16 * pc + 16].unsqueeze(2)
                                    .broadcast_to((128, 16, 64)),
                                op=OP.mult))
                        else:
                            self.s_dve.inc(v.tensor_tensor(
                                out=self.sjp[:, slot * 1024:slot * 1024 + 1024]
                                    .rearrange("p (a b) -> p a b", a=16),
                                in0=self.sjp[:, slot * 1024:slot * 1024 + 1024]
                                    .rearrange("p (a b) -> p a b", a=16),
                                in1=psW[:, 16 * pc:16 * pc + 16].unsqueeze(2)
                                    .broadcast_to((128, 16, 64)),
                                op=OP.mult))
                            self.s_dve.wait(v)
                            self.s_dve.inc(v.tensor_add(
                                out=self.prev[:, 1024 * pc:1024 * (pc + 1)],
                                in0=self.prev[:, 1024 * pc:1024 * (pc + 1)],
                                in1=self.sjp[:, slot * 1024:slot * 1024 + 1024]))
                        ev[("sacc", j, pc)] = self.s_dve.n
                        if pc == 7:
                            ev[("sin", j)] = self.s_dve.n
            ev["sin"] = ev[("sacc", 3, 7)]

            for c in range(NCH):
                self._pb_chunk(blk, ev, psB1, psB2, psN, psRq, c)

    def _pb_chunk(self, blk, ev, psB1, psB2, psN, psRq, c):
        nc = self.nc

        @blk.scalar
        def _(act):
            if c == 0:
                act.wait_ge(self.s_dve.h, ev["sin"])
            else:
                act.wait_ge(self.s_dve2.h, ev[("ydone", c - 1)])
            self.s_act.inc(act.activation(
                out=self.evpl[:, 0:CS],
                in_=self.planes[:, T + c * CS:T + (c + 1) * CS], func=AF.Exp))
            ev[("evpl", c)] = self.s_act.n

        @blk.vector
        def _(v):
            if c == 0:
                v.wait_ge(self.s_dve.h, ev["sin"])
            else:
                v.wait_ge(self.s_dve.h, ev[("supd", c - 1, 7)])
                v.wait_ge(self.s_pe.h, ev[("ymm2", c - 1, 63)])
            for q in range(8):
                self.s_dve2.inc(v.tensor_copy(
                    out=self.prunb[:, q * 1024:(q + 1) * 1024],
                    in_=self.prev[:, q * 1024:(q + 1) * 1024]))
            ev[("prunb", c)] = self.s_dve2.n

        for half in range(8):
            @blk.sync
            def _(sync, half=half):
                if c == 0 and half == 0:
                    sync.wait_ge(self.s_dve.h, ev["sin"])
                if ("ydone", c - 1) in ev:
                    sync.wait_ge(self.s_dve2.h, ev[("ydone", c - 1)])
                self.c_ld.inc(sync.dma_start(
                    out=self.yb[:, half * 2048:(half + 1) * 2048]
                        .rearrange("p (k t) -> p k t", k=8),
                    in_=self.h_y1[8 * half:8 * (half + 1), :, c * CS:(c + 1) * CS]
                        .rearrange("k p t -> p k t")), 2)
                self.c_ld.inc(sync.dma_start(
                    out=self.gateb[:, half * 2048:(half + 1) * 2048]
                        .rearrange("p (k t) -> p k t", k=8),
                    in_=self.h_gate[8 * half:8 * (half + 1), :, c * CS:(c + 1) * CS]
                        .rearrange("k p t -> p k t")), 3)
        ev[("reloadall", c)] = (self.c_ld.val(2), self.c_ld.val(3))

        for i in range(66):
            if i < 64 and i % 8 == 0:
                q = i // 8
                @blk.gpsimd
                def _(gp, q=q):
                    gp.wait_ge(self.s_act.h, ev[("evpl", c)])
                    if ("evk1", c, 8 * q - 2) in ev:
                        gp.wait_ge(self.s_pe.h, ev[("evk1", c, 8 * q - 2)])
                    self.c_sg.inc(gp.dma_start(
                        allow_slow_non_contiguous=True,
                        out=self.evst[0:1, (q % 2) * 4096:(q % 2) * 4096 + 4096]
                            .rearrange("o (h t) -> o h t", h=16),
                        in_=self.evpl[16 * q:16 * (q + 1), 0:CS]), 0)
                    ev[("evst", c, q)] = self.c_sg.val(0)
            if i < 64:
                self._pb_stage1(blk, ev, psB1, c, i)
            if 2 <= i <= 65:
                self._pb_stage3(blk, ev, psN, c, i - 2)
                if (i - 2) % 8 == 7:
                    self._pb_group_norm(blk, ev, psN, psRq, c, (i - 2) // 8)
            if 1 <= i <= 64:
                self._pb_stage2(blk, ev, psB1, psB2, c, i - 1)
        ev[("ydone", c)] = self.s_dve2.n

        for pc in range(8):
            @blk.sync
            def _(sync, pc=pc):
                sync.wait_ge(self.s_dve2.h, ev[("ydone", c)])
                slot = pc % 2
                if ("supd", c, pc - 2) in ev:
                    sync.wait_ge(self.s_dve.h, ev[("supd", c, pc - 2)])
                self.c_ld.inc(sync.dma_start(
                    out=self.sjp[:, slot * 1024:slot * 1024 + 1024],
                    in_=self.h_states[c][:, 1024 * pc:1024 * (pc + 1)]), slot)
                ev[("sld", c, pc)] = self.c_ld.val(slot)

            @blk.vector
            def _(v, pc=pc):
                slot = pc % 2
                self.c_ld.wait(v, slot, at=ev[("sld", c, pc)])
                self.s_dve.inc(v.tensor_tensor(
                    out=self.prev[:, 1024 * pc:1024 * (pc + 1)]
                        .rearrange("p (a b) -> p a b", a=16),
                    in0=self.prev[:, 1024 * pc:1024 * (pc + 1)]
                        .rearrange("p (a b) -> p a b", a=16),
                    in1=self.Tc_all[:, c * 128 + 16 * pc:c * 128 + 16 * pc + 16]
                        .unsqueeze(2).broadcast_to((128, 16, 64)),
                    op=OP.mult))
                self.s_dve.wait(v)
                self.s_dve.inc(v.tensor_add(
                    out=self.prev[:, 1024 * pc:1024 * (pc + 1)],
                    in0=self.prev[:, 1024 * pc:1024 * (pc + 1)],
                    in1=self.sjp[:, slot * 1024:slot * 1024 + 1024]))
                ev[("supd", c, pc)] = self.s_dve.n

    def _pb_stage1(self, blk, ev, psB1, c, k):
        g = k // 8
        q = k // 8
        sl = k % 2
        b1 = psB1[:, sl * 512:sl * 512 + 512]

        @blk.tensor
        def _(te):
            self.c_sg.wait(te, 0, at=ev[("evst", c, q)])
            if ("ctld", c, k - 2) in ev:
                te.wait_ge(self.s_dve2.h, ev[("ctld", c, k - 2)])
            elif c > 0 and ("ctld", c - 1, 62 + (k % 2)) in ev:
                te.wait_ge(self.s_dve2.h, ev[("ctld", c - 1, 62 + (k % 2))])
            r0 = (2 * k) % 16
            te.matmul(out=b1[:, 0:256], lhsT=self.ones1[:, :],
                      rhs=self.evst[0:1, (q % 2) * 4096 + r0 * CS:
                                    (q % 2) * 4096 + (r0 + 1) * CS],
                      start=True, stop=True)
            self.s_pe.inc(te.matmul(
                out=b1[:, 256:512], lhsT=self.ones1[:, :],
                rhs=self.evst[0:1, (q % 2) * 4096 + (r0 + 1) * CS:
                              (q % 2) * 4096 + (r0 + 2) * CS],
                start=True, stop=True))
            ev[("evk1", c, k)] = self.s_pe.n

        @blk.vector
        def _(v):
            v.wait_ge(self.s_pe.h, ev[("evk1", c, k)])
            v.wait_ge(self.s_dve2.h, ev[("prunb", c)])
            if ("ymm2", c, k - 2) in ev:
                v.wait_ge(self.s_pe.h, ev[("ymm2", c, k - 2)])
            elif c > 0 and ("ymm2", c - 1, 62 + (k % 2)) in ev:
                v.wait_ge(self.s_pe.h, ev[("ymm2", c - 1, 62 + (k % 2))])
            for hp in range(2):
                self.s_dve2.inc(v.tensor_tensor(
                    out=self.ctld[:, sl * 512 + hp * CS:sl * 512 + (hp + 1) * CS],
                    in0=self.C_fm[:, g * T + c * CS:g * T + (c + 1) * CS],
                    in1=b1[:, hp * 256:hp * 256 + 256], op=OP.mult))
            self.s_dve2.wait(v)
            ev[("ctld", c, k)] = self.s_dve2.n

    def _pb_stage2(self, blk, ev, psB1, psB2, c, k):
        sl = k % 2
        b2 = psB2[:, sl * 512:sl * 512 + 256]

        @blk.tensor
        def _(te):
            te.wait_ge(self.s_dve2.h, ev[("ctld", c, k)])
            if ("gev", c, k - 2) in ev:
                te.wait_ge(self.s_dve2.h, ev[("gev", c, k - 2)])
            elif c > 0 and ("gev", c - 1, 62 + (k % 2)) in ev:
                te.wait_ge(self.s_dve2.h, ev[("gev", c - 1, 62 + (k % 2))])
            for hp in range(2):
                mm = te.matmul(out=b2[hp * 64:hp * 64 + 64, 0:256],
                               lhsT=self.prunb[:, k * 128 + hp * 64:k * 128 + hp * 64 + 64],
                               rhs=self.ctld[:, sl * 512 + hp * CS:sl * 512 + (hp + 1) * CS],
                               start=True, stop=True)
            self.s_pe.inc(mm)
            ev[("ymm2", c, k)] = self.s_pe.n

        @blk.vector
        def _(v):
            v.wait_ge(self.s_pe.h, ev[("ymm2", c, k)])
            rl = ev[("reloadall", c)]
            self.c_ld.wait(v, 2, at=rl[0])
            self.c_ld.wait(v, 3, at=rl[1])
            half = k // 8
            kk = k % 8
            ybs = self.yb[:, half * 2048 + kk * CS:half * 2048 + (kk + 1) * CS]
            gbs = self.gateb[:, half * 2048 + kk * CS:half * 2048 + (kk + 1) * CS]
            self.s_dve2.inc(v.tensor_add(out=self.gy[:, sl * CS:(sl + 1) * CS],
                                         in0=ybs, in1=b2))
            self.s_dve2.wait(v)
            self.s_dve2.inc(v.tensor_tensor(
                out=self.gpre[:, (k % 8) * CS:(k % 8 + 1) * CS],
                in0=self.gy[:, sl * CS:(sl + 1) * CS], in1=gbs, op=OP.mult))
            self.s_dve2.wait(v)
            ev[("gev", c, k)] = self.s_dve2.n

        @blk.scalar
        def _(act):
            act.wait_ge(self.s_dve2.h, ev[("gev", c, k)])
            self.s_act2.inc(act.activation(
                out=self.gsq[:, sl * CS:(sl + 1) * CS],
                in_=self.gpre[:, (k % 8) * CS:(k % 8 + 1) * CS], func=AF.Square))
            ev[("gsq", c, k)] = self.s_act2.n

    def _pb_stage3(self, blk, ev, psN, c, k):
        sl = k % 2

        @blk.tensor
        def _(te):
            te.wait_ge(self.s_act2.h, ev[("gsq", c, k)])
            if k % 8 == 0:
                if ("nsq", c, k // 8 - 1) in ev:
                    te.wait_ge(self.s_act.h, ev[("nsq", c, k // 8 - 1)])
                elif c > 0 and ("nsq", c - 1, 7) in ev:
                    te.wait_ge(self.s_act.h, ev[("nsq", c - 1, 7)])
            self.s_pe.inc(te.matmul(out=psN[0:1, 0:256], lhsT=self.onescol[:, :],
                                    rhs=self.gsq[:, sl * CS:(sl + 1) * CS],
                                    start=(k % 8 == 0), stop=(k % 8 == 7)))
            ev[("nmm", c, k)] = self.s_pe.n

    def _pb_group_norm(self, blk, ev, psN, psRq, c, g):
        @blk.scalar
        def _(act):
            act.wait_ge(self.s_pe.h, ev[("nmm", c, 8 * g + 7)])
            self.s_act.inc(act.activation(out=self.nrow[0:1, 0:CS],
                                          in_=psN[0:1, 0:256], func=AF.Sqrt,
                                          scale=np.float32(1.0 / 1024.0),
                                          bias=np.float32(EPS)))
            ev[("nsq", c, g)] = self.s_act.n

        @blk.vector
        def _(v):
            v.wait_ge(self.s_act.h, ev[("nsq", c, g)])
            self.s_dve.inc(v.reciprocal(out=self.nrow[0:1, 0:CS],
                                        in_=self.nrow[0:1, 0:CS]))
            self.s_dve.wait(v)
            self.s_dve.inc(v.tensor_copy(out=self.nrow_bf[0:1, 0:CS],
                                         in_=self.nrow[0:1, 0:CS]))
            ev[("rqrow", c, g)] = self.s_dve.n

        @blk.tensor
        def _(te):
            te.wait_ge(self.s_dve.h, ev[("rqrow", c, g)])
            if ("gnorm", c, g - 1) in ev:
                te.wait_ge(self.s_dve2.h, ev[("gnorm", c, g - 1)])
            elif c > 0:
                te.wait_ge(self.s_dve2.h, ev[("gnorm", c - 1, 7)])
            self.s_pe.inc(te.matmul(
                out=psRq[:, 0:256], lhsT=self.ones1[:, :],
                rhs=self.nrow_bf[0:1, 0:CS],
                start=True, stop=True))
            ev[("rqk1", c, g)] = self.s_pe.n

        @blk.vector
        def _(v):
            v.wait_ge(self.s_pe.h, ev[("rqk1", c, g)])
            self.c_gt.wait(v, 0)
            self.c_gt.wait(v, 1)
            for kk in range(8):
                k = 8 * g + kk
                self.s_dve2.inc(v.scalar_tensor_tensor(
                    out=self.gst[:, kk * CS:(kk + 1) * CS],
                    in0=psRq[:, 0:256], scalar=self.nwrep[:, k:k + 1],
                    in1=self.gpre[:, kk * CS:(kk + 1) * CS],
                    op0=OP.mult, op1=OP.mult))
            self.s_dve2.wait(v)
            ev[("gnorm", c, g)] = self.s_dve2.n

        @blk.sync
        def _(sync):
            sync.wait_ge(self.s_dve2.h, ev[("gnorm", c, g)])
            self.c_gt.inc(sync.dma_start(
                out=self.h_g[8 * g:8 * (g + 1), :, c * CS:(c + 1) * CS]
                    .rearrange("k p t -> p k t"),
                in_=self.gst[:, 0:2048].rearrange("p (k t) -> p k t", k=8)), g % 2)
            ev[("gsp", c, g)] = self.c_gt.val(g % 2)

    def emit_p3a(self):
        nc = self.nc
        ev = {}
        with nc.Block("gl") as blk:
            @blk.sync
            def _(sync):
                self.wait_everything(sync)
                for q in range(4):
                    self.c_ld.inc(sync.dma_start(
                        out=self.g_all[:, q * 16384:(q + 1) * 16384]
                            .rearrange("p (k t) -> p k t", k=16),
                        in_=self.h_g[16 * q:16 * (q + 1)]
                            .rearrange("k p t -> p k t")), q % 2)

        bws = [self.c_w.val(j) for j in range(2)]
        bxs = [self.c_x.val(j) for j in range(2)]
        bsp = [self.c_sp.val(j) for j in range(2)]
        bp2, bd2 = self.s_pe2.n, self.s_dve2.n
        w_done = [bws[dt % 2] + 16 * (dt // 2 + 1) for dt in range(32)]
        x_done = [bxs[dt % 2] + 16 * (dt // 2 + 1) for dt in range(32)]
        sp_done = [bsp[dt % 2] + 16 * (dt // 2 + 1) for dt in range(32)]
        pe_done = [bp2 + (dt + 1) for dt in range(32)]
        ev_done = [bd2 + (dt + 1) for dt in range(32)]

        with (nc.psum_tensor("psO_A", [128, 1024], F32) as psA,
              nc.psum_tensor("psO_B", [128, 1024], F32) as psB,
              nc.Block("p3a") as blk):
            @blk.gpsimd
            def _(gp):
                self.wait_everything(gp)
                for dt in range(32):
                    if dt >= 2:
                        gp.wait_ge(self.s_pe2.h, pe_done[dt - 2])
                    slab = self.wslab3[:, (dt % 2) * 8192:(dt % 2) * 8192 + 8192]
                    self.c_w.inc(gp.dma_start(out=slab, in_=self.i_wout[dt]), dt % 2)

            @blk.sync
            def _(sync):
                for dt in range(32):
                    if dt >= 2:
                        sync.wait_ge(self.s_dve2.h, ev_done[dt - 2])
                    buf = self.xtile[:, (dt % 2) * TH:(dt % 2) * TH + T]
                    self.c_x.inc(sync.dma_start(out=buf,
                                                in_=self.h_r1[dt][:, HALO:TH]), dt % 2)
                    if dt >= 1:
                        sync.wait_ge(self.s_dve2.h, ev_done[dt - 1])
                        eb = self.evt[:, ((dt - 1) % 2) * TH:((dt - 1) % 2) * TH + T]
                        self.c_sp.inc(sync.dma_start(out=self.h_r2[dt - 1],
                                                     in_=eb), (dt - 1) % 2)
                sync.wait_ge(self.s_dve2.h, ev_done[31])
                eb = self.evt[:, (31 % 2) * TH:(31 % 2) * TH + T]
                self.c_sp.inc(sync.dma_start(out=self.h_r2[31], in_=eb), 31 % 2)
                self.c_sp.wait(sync, 0)
                self.c_sp.wait(sync, 1)

            @blk.tensor
            def _(te):
                self.c_ld.wait(te, 0)
                self.c_ld.wait(te, 1)
                for dt in range(32):
                    self.c_w.wait(te, dt % 2, at=w_done[dt])
                    if dt >= 2:
                        te.wait_ge(self.s_dve2.h, ev_done[dt - 2])
                    ps = psA if dt % 2 == 0 else psB
                    slab = self.wslab3[:, (dt % 2) * 8192:(dt % 2) * 8192 + 8192]
                    for f in range(64):
                        lt = slab[:, f * 128:(f + 1) * 128]
                        for (c0, c1) in ((0, 512), (512, 1024)):
                            mm = te.matmul(out=ps[:, c0:c1], lhsT=lt,
                                           rhs=self.g_all[:, f * T + c0:f * T + c1],
                                           start=(f == 0), stop=(f == 63))
                    self.s_pe2.inc(mm)

            @blk.vector
            def _(v):
                for dt in range(32):
                    v.wait_ge(self.s_pe2.h, pe_done[dt])
                    self.c_x.wait(v, dt % 2, at=x_done[dt])
                    if dt >= 2:
                        self.c_sp.wait(v, dt % 2, at=sp_done[dt - 2])
                    ps = psA if dt % 2 == 0 else psB
                    eb = self.evt[:, (dt % 2) * TH:(dt % 2) * TH + T]
                    self.s_dve2.inc(v.tensor_add(
                        out=eb, in0=ps[:, 0:T],
                        in1=self.xtile[:, (dt % 2) * TH:(dt % 2) * TH + T]))

    def build(self):
        self.emit_init()
        self.push("C_fm", [128, 8 * T], BF16)
        self.push("planes", [128, 3 * T], F32)
        self.push("planes_bf", [128, T + 512], BF16)
        self.push("tmcols", [128, 2 * 8 * 128], F32)
        self.push("Tc_all", [128, 4 * 128], F32)
        self.push("runbf", [128, INTER], BF16)
        self.push("xn", [128, 32 * TH], BF16)
        self.xst = TH
        self.push("rows", [1, 3 * TH], F32)
        self.push("rows_bf", [1, 2 * TH], BF16)
        self.push("xtile", [128, 2 * TH], F32)
        self.push("sqb", [128, 2 * TH], BF16)
        self.push("evt", [128, 2 * TH], F32)
        self.push("wslab", [128, 2 * 4224], BF16)
        self.emit_ln("ln1", "x", TH, mask_halo=True)
        self.emit_ffn("ffn1", TH, "x", self.h_r1)
        self.emit_ln("ln2", "r1", TH, mask_halo=True)
        self.pop("wslab", "evt", "sqb", "xtile", "rows_bf", "rows")
        self.emit_barrier()
        self.push("wslab2", [128, 2 * 4096], BF16)
        self.push("B_fm", [128, INTER], BF16)
        self.push("B_tm", [128, 1024], BF16)
        self.push("hs_fm", [128, 2 * T], BF16)
        self.push("hs_tm", [128, 2 * T], BF16)
        self.push("rawS", [128, 4 * 384], BF16)
        self.push("d1b", [128, 2 * 384], BF16)
        self.push("stb", [128, 2 * 384], BF16)
        self.push("y1st", [128, 2 * T], BF16)
        self.push("gatest", [128, 2 * T], BF16)
        self.push("cacc", [128, T], F32)
        self.push("wbb", [128, 4 * 64], BF16)
        self.push("sst", [128, 2 * 128], F32)
        self.push("sscr", [128, 128], F32)
        self.push("sstage", [1, 2 * T], BF16)
        self.emit_p2()
        self.pop("sstage", "sscr", "sst", "wbb", "cacc", "gatest", "y1st",
                 "stb", "d1b", "rawS", "hs_tm", "hs_fm", "B_tm", "B_fm",
                 "wslab2", "xn")
        self.pop("runbf")
        self.emit_barrier()
        self.push("prev", [128, INTER], F32)
        self.push("sjp", [128, 2 * 1024], F32)
        self.push("yb", [128, 64 * CS], BF16)
        self.push("gateb", [128, 64 * CS], BF16)
        self.push("gpre", [128, 8 * CS], BF16)
        self.push("gst", [128, 8 * CS], BF16)
        self.push("gy", [128, 2 * CS], F32)
        self.push("gsq", [128, 2 * CS], BF16)
        self.push("ctld", [128, 2 * 512], BF16)
        self.push("prunb", [128, INTER], BF16)
        self.push("evpl", [128, CS], BF16)
        self.push("evst", [1, 2 * 4096], BF16)
        self.push("nrow", [1, CS], F32)
        self.push("nrow_bf", [1, CS], BF16)
        self.push("wrow", [1, 512], BF16)
        self.emit_pb()
        self.pop("wrow", "nrow_bf", "nrow", "evst", "evpl", "prunb", "ctld", "gsq", "gy", "gst",
                 "gpre", "gateb", "yb", "sjp", "prev")
        self.pop("Tc_all", "tmcols", "planes_bf", "planes", "C_fm")
        self.emit_barrier()
        self.push("g_all", [128, 64 * T], BF16)
        self.push("wslab3", [128, 2 * 8192], BF16)
        self.push("xtile", [128, 2 * TH], F32)
        self.push("evt", [128, 2 * TH], F32)
        self.emit_p3a()
        self.pop("evt", "xtile", "wslab3", "g_all")
        self.emit_barrier()
        self.push("xn3", [128, 32 * T], BF16)
        self.xn = self.xn3
        self.xst = T
        self.push("rows", [1, 3 * TH], F32)
        self.push("rows_bf", [1, 2 * TH], BF16)
        self.push("xtile", [128, 2 * TH], F32)
        self.push("sqb", [128, 2 * TH], BF16)
        self.push("evt", [128, 2 * TH], F32)
        self.push("wslab", [128, 2 * 4224], BF16)
        self.emit_ln("ln3", "r2", T, mask_halo=False)
        self.emit_ffn("ffn2", T, "r2", self.o_y)
        self.finish()
        return self.nc

    def emit_barrier(self):
        self._barid = getattr(self, "_barid", 0) + 1
        with self.nc.Block(f"bar{self._barid}") as blk:
            self.barrier(blk)

    def finish(self):
        with self.nc.Block("fin") as blk:
            self.barrier(blk)


def kernel(**inputs):
    in_maps = make_inputs(inputs)
    kb = KB()
    nc = kb.build()
    res = run_bass_kernel_spmd(nc, in_maps, core_ids=list(range(8)))
    out = np.empty((2, 4096, 4096), np.float32)
    for k in range(8):
        y = res.results[k]["y_fm"].reshape(4096, T)
        out[k // 4, T * (k % 4):T * (k % 4 + 1), :] = y.T
    return out


# version 07
# speedup vs baseline: 1.0098x; 1.0098x over previous
import numpy as np
import ml_dtypes
import concourse.bass as bass
import concourse.mybir as mybir
from concourse.bass_utils import run_bass_kernel_spmd

F32 = mybir.dt.float32
BF16 = mybir.dt.bfloat16
I32 = mybir.dt.int32
AF = mybir.ActivationFunctionType
OP = mybir.AluOpType
AX = mybir.AxisListType.X

D = 4096
T = 1024
HALO = 128
TH = T + HALO
H, P, N, G, CK, CS = 128, 64, 128, 8, 4, 256
INTER = H * P
NCH = T // CS
EPS = 1e-5

N_DT, N_B, N_C, N_HS, N_GATE = 1, 8, 8, 64, 64
WT0 = 8 * 128
N_FT = N_DT + N_B + N_C + N_HS + N_GATE


def _proj_tile_order():
    tiles = [18432 // 128]
    tiles += [16384 // 128 + i for i in range(8)]
    tiles += [17408 // 128 + i for i in range(8)]
    tiles += [8192 // 128 + i for i in range(64)]
    tiles += list(range(64))
    return tiles


def _panels(wt, n_out_tiles, pad_rows=None):
    if pad_rows is not None and wt.shape[0] < pad_rows:
        wt = np.concatenate([wt, np.zeros((pad_rows - wt.shape[0], wt.shape[1]),
                                          wt.dtype)], 0)
    d_in, d_out = wt.shape
    nci = d_in // 128
    a = wt.reshape(nci, 128, n_out_tiles, 128).transpose(2, 1, 0, 3)
    return np.ascontiguousarray(a.reshape(n_out_tiles, 128, nci * 128))


def make_inputs(full):
    x = np.asarray(full["L0_attn_input_states"])
    wproj = np.asarray(full["L0_attn_in_proj_w"])
    conv_w = np.asarray(full["L0_attn_conv_w"]).reshape(10240, CK)
    conv_b = np.asarray(full["L0_attn_conv_b"])
    dt_bias = np.asarray(full["L0_attn_dt_bias"])
    a_log = np.asarray(full["L0_attn_A_log"])
    d_vec = np.asarray(full["L0_attn_D"])
    norm_w = np.asarray(full["L0_attn_norm_w"])
    out_w = np.asarray(full["L0_attn_out_w"])
    wffn = np.asarray(full["L0_ffn_weight"])
    ffn_b = np.asarray(full["L0_ffn_bias"])
    lora_a = np.asarray(full["L0_ffn_lora_A"])
    lora_b = np.asarray(full["L0_ffn_lora_B"])
    lsi = np.asarray(full["L0_ffn_lora_start_indices"]).astype(np.int32)

    lb_t = lora_b.transpose(0, 2, 1).reshape(48, 4096)
    wffn_aug = np.concatenate([wffn.T, lb_t,
                               np.zeros((80, 4096), wffn.dtype)], 0)
    wffn_p = _panels(np.ascontiguousarray(wffn_aug), 32)
    wproj_all = _panels(np.ascontiguousarray(wproj.T), 18560 // 128)
    wproj_p = np.ascontiguousarray(wproj_all[_proj_tile_order()])
    wout_p = _panels(np.ascontiguousarray(out_w.T), 32)
    la_t = np.ascontiguousarray(lora_a.reshape(48, 4096).T)

    shared = {
        "wffn_p": wffn_p, "wproj_p": wproj_p, "wout_p": wout_p, "la_t": la_t,
        "conv_w_cols": np.ascontiguousarray(conv_w.reshape(80, 128, CK)),
        "conv_b_cols": np.ascontiguousarray(conv_b.reshape(80, 128)),
        "ffn_b_cols": np.ascontiguousarray(ffn_b.reshape(32, 128)),
        "d_rep": np.ascontiguousarray(np.repeat(d_vec, P).reshape(64, 128)),
        "nw_rep": np.ascontiguousarray(norm_w.reshape(64, 128)),
        "dtb_row": np.ascontiguousarray(dt_bias.reshape(1, 128)),
        "alog_row": np.ascontiguousarray(a_log.reshape(1, 128)),
        "ident_f32": np.eye(128, dtype=np.float32),
        "lsi_rep": np.ascontiguousarray(np.broadcast_to(lsi[None, :], (128, 4))),
    }
    in_maps = []
    for k in range(8):
        b, p = k // 4, k % 4
        t0 = T * p
        xs = x[b, t0:t0 + T, :]
        halo = (np.zeros((HALO, D), np.float32) if p == 0
                else x[b, t0 - HALO:t0, :])
        meta = np.zeros((128, 8), np.float32)
        meta[:, 0] = float(b)
        meta[:, 1] = 0.0 if p == 0 else 1.0
        for j in range(4):
            meta[:, 2 + j] = 1.0 if j < p else 0.0
        m = dict(shared)
        m["x_fm"] = np.ascontiguousarray(np.concatenate([halo, xs], 0).T)
        m["meta"] = np.ascontiguousarray(meta)
        in_maps.append(m)
    return in_maps



def _prev_slot_pair(k, nsl=3):
    return [x for x in range(64 - nsl, 64) if x % nsl == k % nsl][0]

class Sem:
    def __init__(self, nc, name, step=1):
        self.h = nc.alloc_semaphore(name)
        self.step = step
        self.n = 0

    def inc(self, instr, k=1):
        self.n += self.step * k
        return instr.then_inc(self.h, self.step * k)

    def wait(self, eng, at=None):
        eng.wait_ge(self.h, self.n if at is None else at)


class Chan:

    def __init__(self, nc, name, nslots, step=16):
        self.sems = [nc.alloc_semaphore(f"{name}{j}") for j in range(nslots)]
        self.n = [0] * nslots
        self.step = step

    def inc(self, instr, slot):
        self.n[slot] += self.step
        return instr.then_inc(self.sems[slot], self.step)

    def val(self, slot):
        return self.n[slot]

    def wait(self, eng, slot, at=None):
        eng.wait_ge(self.sems[slot], self.n[slot] if at is None else at)

    def wait_all(self, eng):
        for j, s in enumerate(self.sems):
            if self.n[j]:
                eng.wait_ge(s, self.n[j])


class KB:
    def __init__(self, taps=()):
        self.taps = set(taps)
        nc = self.nc = bass.Bass("TRN2", target_bir_lowering=False)
        dram = nc.dram_tensor
        self.i_x = dram("x_fm", [D, TH], F32, kind="ExternalInput").ap()
        self.i_wffn = dram("wffn_p", [32, 128, 4224], F32, kind="ExternalInput").ap()
        self.i_wproj = dram("wproj_p", [N_FT, 128, 4096], F32, kind="ExternalInput").ap()
        self.i_wout = dram("wout_p", [32, 128, 8192], F32, kind="ExternalInput").ap()
        self.i_la = dram("la_t", [D, 48], F32, kind="ExternalInput").ap()
        self.i_convw = dram("conv_w_cols", [80, 128, CK], F32, kind="ExternalInput").ap()
        self.i_convb = dram("conv_b_cols", [80, 128], F32, kind="ExternalInput").ap()
        self.i_fb = dram("ffn_b_cols", [32, 128], F32, kind="ExternalInput").ap()
        self.i_drep = dram("d_rep", [64, 128], F32, kind="ExternalInput").ap()
        self.i_nwrep = dram("nw_rep", [64, 128], F32, kind="ExternalInput").ap()
        self.i_dtb = dram("dtb_row", [1, 128], F32, kind="ExternalInput").ap()
        self.i_alog = dram("alog_row", [1, 128], F32, kind="ExternalInput").ap()
        self.i_idf32 = dram("ident_f32", [128, 128], F32, kind="ExternalInput").ap()
        self.i_lsi = dram("lsi_rep", [128, 4], I32, kind="ExternalInput").ap()
        self.i_meta = dram("meta", [128, 8], F32, kind="ExternalInput").ap()
        self.o_y = dram("y_fm", [32, 128, T], F32, kind="ExternalOutput").ap()

        self.h_r1 = dram("r1_sp", [32, 128, TH], F32).ap()
        self.h_r2 = dram("r2_sp", [32, 128, T], F32).ap()
        self.h_gate = dram("gate_sp", [64, 128, T], BF16).ap()
        self.h_y1 = dram("y1_sp", [64, 128, T], BF16).ap()
        self.h_states = dram("states_sp", [NCH, 128, INTER], F32).ap()
        self.h_g = dram("g_sp", [64, 128, T], BF16).ap()
        self.h_agin = dram("ag_in", [128, INTER + 1], F32).ap()
        self.h_agout = dram("ag_out", [512, INTER + 1], F32).ap()

        sb = self._sb_perm = {}
        def L(name, shape, dt):
            t = nc.alloc_sbuf_tensor(name, shape, dt, side="left")
            sb[name] = t
            return t
        self.consts = L("consts", [128, 16], F32)
        self.convw = L("convw", [128, 80 * CK], F32)
        self.convb = L("convb", [128, 80], F32)
        self.drep = L("drep", [128, 64], F32)
        self.nwrep = L("nwrep", [128, 64], F32)
        self.hbcol = L("hbcol", [128, 32], F32)
        self.idf32 = L("idf32", [128, 128], F32)
        self.la = L("la", [128, 32 * 48], BF16)
        self.xa = L("xa", [128, TH], BF16)
        self.scol = L("scol", [48, 1], F32)
        self.ones1 = L("ones1", [1, 128], BF16)
        self.onescol = L("onescol", [128, 1], BF16)
        self.masks = L("masks", [128, 2 * CS], F32)
        self.scr = L("scr", [128, 64], F32)

        self._guards = {}

        self.s_in = Sem(nc, "s_in", 16)
        self.s_ing = Sem(nc, "s_ing", 16)
        self.c_w = Chan(nc, "c_w", 2)
        self.c_x = Chan(nc, "c_x", 2)
        self.c_sp = Chan(nc, "c_sp", 2)
        self.c_ld = Chan(nc, "c_ld", 4)
        self.c_tp = Chan(nc, "c_tp", 2)
        self.c_st = Chan(nc, "c_st", 2)
        self.c_y1 = Chan(nc, "c_y1", 2)
        self.c_gt = Chan(nc, "c_gt", 2)
        self.c_ss = Chan(nc, "c_ss", 2)
        self.c_bt = Chan(nc, "c_bt", 2)
        self.c_ag = Chan(nc, "c_ag", 1)
        self.c_sg = Chan(nc, "c_sg", 2)
        self.s_pe = Sem(nc, "s_pe", 1)
        self.s_pe2 = Sem(nc, "s_pe2", 1)
        self.s_dve = Sem(nc, "s_dve", 1)
        self.s_dve2 = Sem(nc, "s_dve2", 1)
        self.s_act = Sem(nc, "s_act", 1)
        self.s_act2 = Sem(nc, "s_act2", 1)
        self.s_gp = Sem(nc, "s_gp", 1)
        self.s_cc = Sem(nc, "s_cc", 1)
        self.all_sems = [self.s_in, self.s_ing, self.s_pe, self.s_pe2,
                         self.s_dve, self.s_dve2, self.s_act, self.s_act2,
                         self.s_gp, self.s_cc]
        self.all_chans = [self.c_w, self.c_x, self.c_sp, self.c_ld, self.c_tp,
                          self.c_st, self.c_y1, self.c_gt, self.c_ss, self.c_bt,
                          self.c_ag, self.c_sg]
        self.dbg = {}

    def push(self, name, shape, dt):
        self._pushid = getattr(self, "_pushid", 0) + 1
        g = self.nc.sbuf_tensor(f"{name}_{self._pushid}", shape, dt, side="right")
        t = g.__enter__()
        self._guards[name] = g
        setattr(self, name, t)
        return t

    def pop(self, *names):
        for name in names:
            self._guards.pop(name).__exit__(None, None, None)
            delattr(self, name)

    def tap(self, name, shape, dtype=F32):
        if name not in self.taps:
            return None
        ap = self.nc.dram_tensor("dbg_" + name, shape, dtype, kind="ExternalOutput").ap()
        self.dbg[name] = ap
        return ap

    def barrier(self, blk, engines=("sync", "vector", "scalar", "tensor", "gpsimd")):
        for en in engines:
            @getattr(blk, en)
            def _(eng, _en=en):
                for s in self.all_sems:
                    if s.n:
                        eng.wait_ge(s.h, s.n)
                for c in self.all_chans:
                    c.wait_all(eng)

    def wait_everything(self, eng):
        for s in self.all_sems:
            if s.n:
                eng.wait_ge(s.h, s.n)
        for c in self.all_chans:
            c.wait_all(eng)

    def emit_init(self):
        nc = self.nc
        with nc.Block("init") as blk:
            @blk.sync
            def _(sync):
                s = self.s_in
                s.inc(sync.dma_start(out=self.convw[:, :].rearrange("p (t k) -> p t k", t=80),
                                     in_=self.i_convw.rearrange("t p k -> p t k")))
                s.inc(sync.dma_start(allow_slow_non_contiguous=True, out=self.convb[:, :],
                                     in_=self.i_convb.rearrange("t p -> p t")))
                s.inc(sync.dma_start(allow_slow_non_contiguous=True, out=self.drep[:, :],
                                     in_=self.i_drep.rearrange("t p -> p t")))
                s.inc(sync.dma_start(allow_slow_non_contiguous=True, out=self.nwrep[:, :],
                                     in_=self.i_nwrep.rearrange("t p -> p t")))
                s.inc(sync.dma_start(allow_slow_non_contiguous=True, out=self.hbcol[:, :],
                                     in_=self.i_fb.rearrange("t p -> p t")))
                s.inc(sync.dma_start(out=self.idf32[:, :], in_=self.i_idf32[:, :]))
                s.inc(sync.dma_start(allow_slow_non_contiguous=True, out=self.consts[:, 0:1],
                                     in_=self.i_dtb.rearrange("o p -> p o")))
                s.inc(sync.dma_start(allow_slow_non_contiguous=True, out=self.consts[:, 1:2],
                                     in_=self.i_alog.rearrange("o p -> p o")))
                s.inc(sync.dma_start(out=self.consts[:, 3:11], in_=self.i_meta[:, :]))

            @blk.gpsimd
            def _(gp):
                s = self.s_ing
                s.inc(gp.dma_start(out=self.scr[:, 0:4].bitcast(I32), in_=self.i_lsi[:, :]))
                s.inc(gp.dma_start(
                    out=self.la[:, :].rearrange("p (c r) -> p c r", c=32),
                    in_=self.i_la.rearrange("(c p) r -> p c r", p=128)))
                self.s_gp.inc(gp.iota(self.masks[:, 0:CS], [[1, CS]],
                                      channel_multiplier=0,
                                      allow_small_or_imprecise_dtypes=True))
                self.s_gp.inc(gp.iota(self.scr[:, 16:17], [[0, 1]],
                                      channel_multiplier=1,
                                      allow_small_or_imprecise_dtypes=True))

            @blk.vector
            def _(v):
                self.s_in.wait(v)
                self.s_ing.wait(v)
                self.s_gp.wait(v)
                v.memset(self.ones1[:, :], 1.0)
                v.memset(self.onescol[:, :], 1.0)
                v.memset(self.xa[:, :], 0.0)
                self.s_dve.inc(v.tensor_copy(out=self.scr[:, 4:8],
                                             in_=self.scr[:, 0:4].bitcast(I32)))
                self.s_dve.inc(v.tensor_scalar(out=self.masks[:, CS:2 * CS],
                                               in0=self.masks[:, 0:CS],
                                               scalar1=self.scr[:, 16:17], scalar2=128.0,
                                               op0=OP.subtract, op1=OP.is_equal))
                self.s_dve.wait(v)
                self.s_dve.inc(v.tensor_scalar(out=self.masks[:, 0:CS],
                                               in0=self.masks[:, 0:CS],
                                               scalar1=self.scr[:, 16:17], scalar2=0.0,
                                               op0=OP.subtract, op1=OP.is_equal))
                self.s_dve.inc(v.tensor_scalar(out=self.scr[:, 8:12], in0=self.scr[:, 4:8],
                                               scalar1=-1.0, scalar2=self.consts[:, 3:4],
                                               op0=OP.mult, op1=OP.add))
                self.s_dve.wait(v)
                self.s_dve.inc(v.tensor_scalar(out=self.scr[:, 8:12], in0=self.scr[:, 8:12],
                                               scalar1=1.0, scalar2=0.0,
                                               op0=OP.add, op1=OP.max))
                self.s_dve.wait(v)
                self.s_dve.inc(v.tensor_scalar(out=self.scr[:, 8:12], in0=self.scr[:, 8:12],
                                               scalar1=1.0, scalar2=None, op0=OP.min))
                self.s_dve.wait(v)
                self.s_dve.inc(v.reduce_sum(out=self.scr[:, 12:13], in_=self.scr[:, 8:12],
                                            axis=AX))
                self.s_dve.wait(v)
                self.s_dve.inc(v.tensor_scalar(out=self.scr[:, 13:14], in0=self.scr[:, 12:13],
                                               scalar1=-1.0, scalar2=None, op0=OP.add))
                self.s_dve.wait(v)
                self.s_dve.inc(v.tensor_scalar(out=self.scr[:, 14:15], in0=self.scr[:, 13:14],
                                               scalar1=0.0, scalar2=1.0,
                                               op0=OP.max, op1=OP.min))
                self.s_dve.inc(v.tensor_scalar(out=self.scr[:, 15:16], in0=self.scr[:, 13:14],
                                               scalar1=-1.0, scalar2=0.0,
                                               op0=OP.add, op1=OP.max))
                self.s_dve.wait(v)
                self.s_dve.inc(v.tensor_scalar(out=self.scr[:, 15:16], in0=self.scr[:, 15:16],
                                               scalar1=2.0, scalar2=None, op0=OP.min))
                v.memset(self.scr[:, 24:25], 0.0)
                v.memset(self.scr[:, 25:26], 1.0)
                self.s_dve.inc(v.memset(self.scr[:, 26:27], 2.0))
                self.s_dve.wait(v)
                self.s_dve.inc(v.tensor_scalar(out=self.scr[:, 27:30], in0=self.scr[:, 24:27],
                                               scalar1=-1.0, scalar2=self.scr[:, 15:16],
                                               op0=OP.mult, op1=OP.add))
                self.s_dve.wait(v)
                self.s_dve.inc(v.tensor_tensor(out=self.scr[:, 27:30], in0=self.scr[:, 27:30],
                                               in1=self.scr[:, 27:30], op=OP.mult))
                self.s_dve.wait(v)
                self.s_dve.inc(v.tensor_scalar(out=self.scr[:, 27:30], in0=self.scr[:, 27:30],
                                               scalar1=-1.0, scalar2=1.0,
                                               op0=OP.mult, op1=OP.add))
                self.s_dve.wait(v)
                self.s_dve.inc(v.tensor_scalar(out=self.scr[:, 27:30], in0=self.scr[:, 27:30],
                                               scalar1=0.0, scalar2=self.scr[:, 14:15],
                                               op0=OP.max, op1=OP.mult))
                self.s_dve.wait(v)
                self.s_dve.inc(v.tensor_scalar(out=self.hbcol[:, :], in0=self.hbcol[:, :],
                                               scalar1=0.5, scalar2=None, op0=OP.mult))

            @blk.sync
            def _(sync):
                self.s_dve.wait(sync)
                for kk in range(3):
                    self.s_in.inc(sync.dma_start(
                        allow_slow_non_contiguous=True,
                        out=self.scol[16 * kk:16 * (kk + 1), 0:1],
                        in_=self.scr[16 * kk:16 * (kk + 1), 27 + kk:28 + kk]))

            @blk.scalar
            def _(act):
                self.s_in.wait(act)
                self.s_act.inc(act.activation(out=self.consts[:, 2:3],
                                              in_=self.consts[:, 1:2], func=AF.Exp))

            @blk.vector
            def _(v):
                self.s_act.wait(v)
                self.s_dve.inc(v.tensor_scalar(out=self.consts[:, 2:3],
                                               in0=self.consts[:, 2:3],
                                               scalar1=-1.0, scalar2=None, op0=OP.mult))

    def emit_ln(self, tag, src_kind, ncols, mask_halo):
        nc = self.nc
        src = {"x": None, "r1": self.h_r1, "r2": self.h_r2}[src_kind]
        n_mm = (ncols + 511) // 512
        inv_d = 1.0 / D
        xst = self.xst

        bxs = [self.c_x.val(j) for j in range(2)]
        bd, ba = self.s_dve.n, self.s_act.n
        x_done = [bxs[dt % 2] + 16 * (dt // 2 + 1) for dt in range(32)]
        cast_done = [bd + (dt + 1) for dt in range(32)]
        sq_done = [ba + (dt + 1) for dt in range(32)]
        bp = self.s_pe.n

        with (nc.psum_tensor("ps_sum_" + tag, [1, 1536], F32) as ps_sum,
              nc.psum_tensor("ps_sq_" + tag, [1, 1536], F32) as ps_sq,
              nc.Block("lnA_" + tag) as blk):
            @blk.sync
            def _(sync):
                for dt in range(32):
                    if dt >= 2:
                        sync.wait_ge(self.s_dve.h, cast_done[dt - 2])
                    buf = self.xtile[:, (dt % 2) * TH:(dt % 2) * TH + ncols]
                    if src_kind == "x":
                        ap = self.i_x[128 * dt:128 * (dt + 1), 0:ncols]
                    else:
                        ap = src[dt][:, 0:ncols]
                    self.c_x.inc(sync.dma_start(out=buf, in_=ap), dt % 2)

            @blk.vector
            def _(v):
                for dt in range(32):
                    self.c_x.wait(v, dt % 2, at=x_done[dt])
                    buf = self.xtile[:, (dt % 2) * TH:(dt % 2) * TH + ncols]
                    self.s_dve.inc(v.tensor_copy(
                        out=self.xn[:, dt * xst:dt * xst + ncols], in_=buf))

            @blk.scalar
            def _(act):
                for dt in range(32):
                    act.wait_ge(self.s_dve.h, cast_done[dt])
                    if dt >= 2:
                        act.wait_ge(self.s_pe.h, bp + dt - 1)
                    sq = self.sqb[:, (dt % 2) * TH:(dt % 2) * TH + ncols]
                    self.s_act.inc(act.activation(
                        out=sq, in_=self.xn[:, dt * xst:dt * xst + ncols],
                        func=AF.Square))

            @blk.tensor
            def _(te):
                for dt in range(32):
                    te.wait_ge(self.s_act.h, sq_done[dt])
                    xb = self.xn[:, dt * xst:dt * xst + ncols]
                    sq = self.sqb[:, (dt % 2) * TH:(dt % 2) * TH + ncols]
                    for j in range(n_mm):
                        c0 = 512 * j
                        c1 = min(512 * (j + 1), ncols)
                        te.matmul(out=ps_sum[0:1, c0:c1], lhsT=self.onescol[:, :],
                                  rhs=xb[:, c0:c1], start=(dt == 0), stop=(dt == 31))
                        mm = te.matmul(out=ps_sq[0:1, c0:c1], lhsT=self.onescol[:, :],
                                       rhs=sq[:, c0:c1], start=(dt == 0), stop=(dt == 31))
                    self.s_pe.inc(mm)

            @blk.scalar
            def _(act):
                self.s_pe.wait(act)
                self.s_act.inc(act.activation(out=self.rows[0:1, 0:ncols],
                                              in_=ps_sum[0:1, 0:ncols],
                                              func=AF.Copy, scale=np.float32(inv_d)))
                act.wait_ge(self.s_act.h, self.s_act.n)
                self.s_act.inc(act.activation(out=self.rows[0:1, TH:TH + ncols],
                                              in_=ps_sq[0:1, 0:ncols],
                                              func=AF.Copy, scale=np.float32(inv_d)))

            @blk.vector
            def _(v):
                self.s_act.wait(v)
                m = self.rows[0:1, 0:ncols]
                e2 = self.rows[0:1, TH:TH + ncols]
                c = self.rows[0:1, 2 * TH:2 * TH + ncols]
                self.s_dve.inc(v.tensor_tensor(out=c, in0=m, in1=m, op=OP.mult))
                self.s_dve.wait(v)
                self.s_dve.inc(v.tensor_tensor(out=e2, in0=e2, in1=c, op=OP.subtract))

            @blk.scalar
            def _(act):
                self.s_dve.wait(act)
                self.s_act.inc(act.activation(out=self.rows[0:1, 2 * TH:2 * TH + ncols],
                                              in_=self.rows[0:1, TH:TH + ncols],
                                              func=AF.Sqrt, bias=np.float32(EPS)))

            @blk.vector
            def _(v):
                self.s_act.wait(v)
                rs = self.rows[0:1, 2 * TH:2 * TH + ncols]
                m = self.rows[0:1, 0:ncols]
                self.s_dve.inc(v.reciprocal(out=rs, in_=rs))
                self.s_dve.wait(v)
                self.s_dve.inc(v.tensor_copy(out=self.rows_bf[0:1, 0:ncols], in_=rs))
                self.s_dve.inc(v.tensor_tensor(out=m, in0=m, in1=rs, op=OP.mult))
                self.s_dve.wait(v)
                self.s_dve.inc(v.tensor_scalar(out=m, in0=m, scalar1=-1.0,
                                               scalar2=None, op0=OP.mult))
                self.s_dve.wait(v)
                self.s_dve.inc(v.tensor_copy(out=self.rows_bf[0:1, TH:TH + ncols],
                                             in_=m))

        with (nc.psum_tensor("ps_rs_" + tag, [128, 1536], F32) as ps_rs,
              nc.psum_tensor("ps_mrs_" + tag, [128, 1536], F32) as ps_mrs,
              nc.Block("lnB_" + tag) as blk):
            @blk.tensor
            def _(te):
                self.s_dve.wait(te)
                for j in range(n_mm):
                    c0, c1 = 512 * j, min(512 * (j + 1), ncols)
                    te.matmul(out=ps_rs[:, c0:c1], lhsT=self.ones1[:, :],
                              rhs=self.rows_bf[0:1, c0:c1], start=True, stop=True)
                    mm = te.matmul(out=ps_mrs[:, c0:c1], lhsT=self.ones1[:, :],
                                   rhs=self.rows_bf[0:1, TH + c0:TH + c1],
                                   start=True, stop=True)
                self.s_pe.inc(mm)

            @blk.vector
            def _(v):
                self.s_pe.wait(v)
                for dt in range(32):
                    xb = self.xn[:, dt * xst:dt * xst + ncols]
                    tmp = self.evt[:, (dt % 2) * TH:(dt % 2) * TH + ncols]
                    self.s_dve.inc(v.tensor_tensor(out=tmp, in0=xb,
                                                   in1=ps_rs[:, 0:ncols], op=OP.mult))
                    self.s_dve.wait(v)
                    if mask_halo:
                        self.s_dve.inc(v.tensor_tensor(out=tmp, in0=tmp,
                                                       in1=ps_mrs[:, 0:ncols], op=OP.add))
                        self.s_dve.wait(v)
                        self.s_dve.inc(v.tensor_scalar(
                            out=self.xn[:, dt * xst:dt * xst + HALO], in0=tmp[:, 0:HALO],
                            scalar1=self.consts[:, 4:5], scalar2=None, op0=OP.mult))
                        self.s_dve.inc(v.tensor_copy(
                            out=self.xn[:, dt * xst + HALO:dt * xst + ncols],
                            in_=tmp[:, HALO:ncols]))
                        self.s_dve.wait(v)
                    else:
                        self.s_dve.inc(v.tensor_tensor(out=xb, in0=tmp,
                                                       in1=ps_mrs[:, 0:ncols], op=OP.add))
                        self.s_dve.wait(v)

    def emit_ffn(self, tag, ncols, res_src, dst):
        nc = self.nc
        n_mm = (ncols + 511) // 512
        xst = self.xst

        with (nc.psum_tensor("ps_lo_" + tag, [48, 1536], F32) as ps_lo,
              nc.Block("lora_" + tag) as blk):
            @blk.tensor
            def _(te):
                self.wait_everything(te)
                for d in range(32):
                    lt = self.la[:, d * 48:(d + 1) * 48]
                    for j in range(n_mm):
                        c0, c1 = 512 * j, min(512 * (j + 1), ncols)
                        mm = te.matmul(out=ps_lo[0:48, c0:c1], lhsT=lt,
                                       rhs=self.xn[:, d * xst + c0:d * xst + c1],
                                       start=(d == 0), stop=(d == 31))
                self.s_pe.inc(mm)

            @blk.vector
            def _(v):
                self.s_pe.wait(v)
                self.s_dve.inc(v.tensor_scalar(out=self.xa[0:48, 0:ncols],
                                               in0=ps_lo[0:48, 0:ncols],
                                               scalar1=self.scol[:, 0:1], scalar2=None,
                                               op0=OP.mult))

        bws = [self.c_w.val(j) for j in range(2)]
        bxs = [self.c_x.val(j) for j in range(2)]
        bsp = [self.c_sp.val(j) for j in range(2)]
        bp2, bd2 = self.s_pe2.n, self.s_dve2.n
        w_done = [bws[dt % 2] + 16 * (dt // 2 + 1) for dt in range(32)]
        x_done = [bxs[dt % 2] + 16 * (dt // 2 + 1) for dt in range(32)]
        sp_done = [bsp[dt % 2] + 16 * (dt // 2 + 1) for dt in range(32)]
        pe_done = [bp2 + (dt + 1) for dt in range(32)]
        ev_done = [bd2 + 2 * (dt + 1) for dt in range(32)]

        with (nc.psum_tensor("ps_mA_" + tag, [128, 1536], F32) as psA,
              nc.psum_tensor("ps_mB_" + tag, [128, 1536], F32) as psB,
              nc.Block("ffn_" + tag) as blk):
            @blk.gpsimd
            def _(gp):
                for dt in range(32):
                    if dt >= 2:
                        gp.wait_ge(self.s_pe2.h, pe_done[dt - 2])
                    slab = self.wslab[:, (dt % 2) * 4224:(dt % 2) * 4224 + 4224]
                    self.c_w.inc(gp.dma_start(out=slab, in_=self.i_wffn[dt]), dt % 2)

            @blk.sync
            def _(sync):
                for dt in range(32):
                    if dt >= 2:
                        sync.wait_ge(self.s_dve2.h, ev_done[dt - 2])
                    buf = self.xtile[:, (dt % 2) * TH:(dt % 2) * TH + ncols]
                    if res_src == "x":
                        ap = self.i_x[128 * dt:128 * (dt + 1), 0:ncols]
                    else:
                        ap = self.h_r2[dt][:, 0:ncols]
                    self.c_x.inc(sync.dma_start(out=buf, in_=ap), dt % 2)
                    if dt >= 1:
                        sync.wait_ge(self.s_dve2.h, ev_done[dt - 1])
                        eb = self.evt[:, ((dt - 1) % 2) * TH:((dt - 1) % 2) * TH + ncols]
                        self.c_sp.inc(sync.dma_start(out=dst[dt - 1][:, 0:ncols],
                                                     in_=eb), (dt - 1) % 2)
                sync.wait_ge(self.s_dve2.h, ev_done[31])
                eb = self.evt[:, (31 % 2) * TH:(31 % 2) * TH + ncols]
                self.c_sp.inc(sync.dma_start(out=dst[31][:, 0:ncols], in_=eb), 31 % 2)
                self.c_sp.wait(sync, 0)
                self.c_sp.wait(sync, 1)

            @blk.tensor
            def _(te):
                self.s_dve.wait(te)
                for dt in range(32):
                    self.c_w.wait(te, dt % 2, at=w_done[dt])
                    if dt >= 2:
                        te.wait_ge(self.s_dve2.h, ev_done[dt - 2])
                    ps = psA if dt % 2 == 0 else psB
                    slab = self.wslab[:, (dt % 2) * 4224:(dt % 2) * 4224 + 4224]
                    for d in range(33):
                        lt = slab[:, d * 128:(d + 1) * 128]
                        rhs_base = (self.xn[:, d * xst:d * xst + ncols] if d < 32
                                    else self.xa[:, 0:ncols])
                        for j in range(n_mm):
                            c0, c1 = 512 * j, min(512 * (j + 1), ncols)
                            mm = te.matmul(out=ps[:, c0:c1], lhsT=lt,
                                           rhs=rhs_base[:, c0:c1],
                                           start=(d == 0), stop=(d == 32))
                    self.s_pe2.inc(mm)

            @blk.vector
            def _(v):
                for dt in range(32):
                    v.wait_ge(self.s_pe2.h, pe_done[dt])
                    self.c_x.wait(v, dt % 2, at=x_done[dt])
                    if dt >= 2:
                        self.c_sp.wait(v, dt % 2, at=sp_done[dt - 2])
                    ps = psA if dt % 2 == 0 else psB
                    eb = self.evt[:, (dt % 2) * TH:(dt % 2) * TH + ncols]
                    self.s_dve2.inc(v.tensor_scalar(out=eb, in0=ps[:, 0:ncols],
                                                    scalar1=0.5,
                                                    scalar2=self.hbcol[:, dt:dt + 1],
                                                    op0=OP.mult, op1=OP.add))
                    self.s_dve2.wait(v)
                    self.s_dve2.inc(v.tensor_tensor(
                        out=eb, in0=eb,
                        in1=self.xtile[:, (dt % 2) * TH:(dt % 2) * TH + ncols],
                        op=OP.add))

    def emit_p2(self):
        nc = self.nc
        ev = {}
        DT0, CU0, SC0 = 0, T, 2 * T

        def cw_col(ct, kk):
            return self.convw[:, ct * CK + kk:ct * CK + kk + 1]

        with (nc.psum_tensor("psP_A", [128, 1536], F32) as psPA,
              nc.psum_tensor("psP_B", [128, 1536], F32) as psPB,
              nc.psum_tensor("psS1", [128, 512], F32) as psS1,
              nc.psum_tensor("psS2", [128, 512], F32) as psS2,
              nc.Block("p2") as blk):

            self.barrier(blk)

            @blk.vector
            def _(v):
                self.s_dve2.inc(v.memset(self.runbf[:, :], 0.0))

            def gp_prefetch(ft):
                if ft >= N_FT:
                    return
                @blk.gpsimd
                def _(gp):
                    if ft >= 2:
                        gp.wait_ge(self.s_pe2.h, ev[("proj", ft - 2)])
                        gp.wait_ge(self.s_dve2.h, ev[("psfree", ft - 2)])
                    slab = self.wslab2[:, (ft % 2) * 4096:(ft % 2) * 4096 + 4096]
                    self.c_w.inc(gp.dma_start(out=slab, in_=self.i_wproj[ft]), ft % 2)
                    ev[("slab", ft)] = self.c_w.val(ft % 2)

            def te_proj(ft):
                @blk.tensor
                def _(te):
                    self.c_w.wait(te, ft % 2, at=ev[("slab", ft)])
                    if ft >= 2:
                        te.wait_ge(self.s_dve2.h, ev[("psfree", ft - 2)])
                        te.wait_ge(self.s_act2.h, ev[("psfree_a", ft - 2)])
                    ps = psPA if ft % 2 == 0 else psPB
                    slab = self.wslab2[:, (ft % 2) * 4096:(ft % 2) * 4096 + 4096]
                    for d in range(32):
                        lt = slab[:, d * 128:(d + 1) * 128]
                        for (c0, c1) in ((0, 512), (512, 1024), (1024, TH)):
                            mm = te.matmul(out=ps[:, c0:c1], lhsT=lt,
                                           rhs=self.xn[:, d * TH + c0:d * TH + c1],
                                           start=(d == 0), stop=(d == 31))
                    self.s_pe2.inc(mm)
                    ev[("proj", ft)] = self.s_pe2.n

            def dve_conv(ft, ct):
                ps = psPA if ft % 2 == 0 else psPB
                @blk.vector
                def _(v):
                    cbuf = (self.cacc[:, 0:T] if ft % 2 == 0
                            else self.planes[:, SC0:SC0 + T])
                    v.wait_ge(self.s_pe2.h, ev[("proj", ft)])
                    if ft == 1:
                        v.wait_ge(self.s_dve.h, ev[("tmev", 7)])
                        v.wait_ge(self.s_dve.h, ev[("wglog", )] if False else ev["wglog"])
                    if ("siluref", ft - 2) in ev:
                        v.wait_ge(self.s_act2.h, ev[("siluref", ft - 2)])
                    self.s_dve2.inc(v.tensor_scalar(
                        out=cbuf, in0=ps[:, 125:125 + T],
                        scalar1=cw_col(ct, 0), scalar2=None, op0=OP.mult))
                    for kk in range(1, CK):
                        self.s_dve2.wait(v)
                        self.s_dve2.inc(v.scalar_tensor_tensor(
                            out=cbuf, in0=ps[:, 125 + kk:125 + kk + T],
                            scalar=cw_col(ct, kk), in1=cbuf,
                            op0=OP.mult, op1=OP.add))
                    ev[("psfree", ft)] = self.s_dve2.n
                    ev[("conv", ft)] = self.s_dve2.n

            def act_silu(ft, ct, dst_ap, hs_k=None):
                @blk.scalar
                def _(act):
                    act.wait_ge(self.s_dve2.h, ev[("conv", ft)])
                    if hs_k is not None and ("y1ev", hs_k - 2, NCH - 1) in ev:
                        act.wait_ge(self.s_dve2.h, ev[("y1ev", hs_k - 2, NCH - 1)])
                    cbuf = (self.cacc[:, 0:T] if ft % 2 == 0
                            else self.planes[:, SC0:SC0 + T])
                    self.s_act2.inc(act.activation(
                        out=dst_ap, in_=cbuf,
                        func=AF.Silu, bias=self.convb[:, ct:ct + 1]))
                    ev[("silu", ft)] = self.s_act2.n
                    ev[("siluref", ft)] = self.s_act2.n
                    ev[("psfree_a", ft)] = self.s_act2.n

            gp_prefetch(0)
            gp_prefetch(1)
            te_proj(0)

            @blk.vector
            def _(v):
                v.wait_ge(self.s_pe2.h, ev[("proj", 0)])
                self.s_dve2.inc(v.tensor_copy(out=self.planes[:, DT0:DT0 + T],
                                              in_=psPA[:, 128:TH]))
                ev[("psfree", 0)] = self.s_dve2.n
                ev[("dtraw", 0)] = self.s_dve2.n
            ev[("psfree_a", 0)] = 0

            @blk.scalar
            def _(act):
                act.wait_ge(self.s_dve2.h, ev[("dtraw", 0)])
                self.s_act.inc(act.activation(out=self.planes[:, SC0:SC0 + T],
                                              in_=self.planes[:, DT0:DT0 + T],
                                              func=AF.Abs, bias=self.consts[:, 0:1]))
                self.s_act.wait(act)
                self.s_act.inc(act.activation(out=self.planes[:, SC0:SC0 + T],
                                              in_=self.planes[:, SC0:SC0 + T],
                                              func=AF.Exp, scale=np.float32(-1.0)))
                self.s_act.wait(act)
                self.s_act.inc(act.activation(out=self.planes[:, SC0:SC0 + T],
                                              in_=self.planes[:, SC0:SC0 + T],
                                              func=AF.Ln, bias=np.float32(1.0)))
                self.s_act.wait(act)
                self.s_act.inc(act.activation(out=self.cacc[:, 0:T],
                                              in_=self.planes[:, DT0:DT0 + T],
                                              func=AF.Relu, bias=self.consts[:, 0:1]))
                ev["sp_parts"] = self.s_act.n

            @blk.vector
            def _(v):
                v.wait_ge(self.s_act.h, ev["sp_parts"])
                self.s_dve.inc(v.tensor_add(out=self.planes[:, DT0:DT0 + T],
                                            in0=self.cacc[:, 0:T],
                                            in1=self.planes[:, SC0:SC0 + T]))
                self.s_dve.wait(v)
                self.s_dve.inc(v.tensor_scalar(out=self.planes[:, SC0:SC0 + T],
                                               in0=self.planes[:, DT0:DT0 + T],
                                               scalar1=self.consts[:, 2:3], scalar2=None,
                                               op0=OP.mult))
                self.s_dve.wait(v)
                for c in range(NCH):
                    s0 = SC0 + c * CS
                    self.s_dve.inc(v.tensor_tensor_scan(
                        out=self.planes[:, CU0 + c * CS:CU0 + (c + 1) * CS],
                        data0=self.planes[:, s0:s0 + CS],
                        data1=self.planes[:, s0:s0 + CS],
                        initial=0.0, op0=OP.add, op1=OP.bypass))
                ev["cum"] = self.s_dve.n

            @blk.scalar
            def _(act):
                act.wait_ge(self.s_dve.h, ev["cum"])
                self.s_act.inc(act.activation(out=self.planes_bf[:, 0:T],
                                              in_=self.planes[:, SC0:SC0 + T],
                                              func=AF.Exp))
                ev["edA"] = self.s_act.n
                for c in range(NCH):
                    act.wait_ge(self.s_act.h, self.s_act.n)
                    self.s_act.inc(act.activation(
                        out=self.cacc[:, c * CS:(c + 1) * CS],
                        in_=self.planes[:, CU0 + c * CS:CU0 + (c + 1) * CS],
                        func=AF.Exp, scale=np.float32(-1.0),
                        bias=self.planes[:, CU0 + (c + 1) * CS - 1:CU0 + (c + 1) * CS]))
                ev["wexp"] = self.s_act.n

            @blk.vector
            def _(v):
                v.wait_ge(self.s_act.h, ev["wexp"])
                self.s_dve.inc(v.tensor_tensor(out=self.planes[:, SC0:SC0 + T],
                                               in0=self.cacc[:, 0:T],
                                               in1=self.planes[:, DT0:DT0 + T],
                                               op=OP.mult))
                self.s_dve.wait(v)
                self.s_dve.inc(v.tensor_copy(out=self.scr[:, 30:31],
                                             in_=self.planes[:, CU0 + CS - 1:CU0 + CS]))
                for c in range(1, NCH):
                    self.s_dve.wait(v)
                    self.s_dve.inc(v.tensor_add(
                        out=self.scr[:, 30:31], in0=self.scr[:, 30:31],
                        in1=self.planes[:, CU0 + (c + 1) * CS - 1:CU0 + (c + 1) * CS]))
                ev["wglog"] = self.s_dve.n

            for tc in range(8):
                @blk.tensor
                def _(te, tc=tc):
                    te.wait_ge(self.s_dve.h, ev["wglog"] if tc == 0 else ev[("tmev", tc - 1)])
                    te.transpose(out=psS2[:, 0:128],
                                 in_=self.planes[:, DT0 + tc * 128:DT0 + (tc + 1) * 128],
                                 identity=self.idf32[:, :])
                    self.s_pe.inc(te.transpose(
                        out=psS2[:, 128:256],
                        in_=self.planes[:, SC0 + tc * 128:SC0 + (tc + 1) * 128],
                        identity=self.idf32[:, :]))
                    ev[("tmtp", tc)] = self.s_pe.n

                @blk.vector
                def _(v, tc=tc):
                    v.wait_ge(self.s_pe.h, ev[("tmtp", tc)])
                    self.s_dve.inc(v.tensor_copy(
                        out=self.tmcols[:, tc * 128:(tc + 1) * 128],
                        in_=psS2[:, 0:128]))
                    self.s_dve.inc(v.tensor_copy(
                        out=self.tmcols[:, WT0 + tc * 128:WT0 + (tc + 1) * 128],
                        in_=psS2[:, 128:256]))
                    ev[("tmev", tc)] = self.s_dve.n

            for c in range(NCH):
                @blk.tensor
                def _(te, c=c):
                    te.wait_ge(self.s_dve.h, ev[("tmev", 7)] if c == 0 else ev[("tcev", c - 1)])
                    self.s_pe.inc(te.transpose(
                        out=psS2[0:1, 0:128],
                        in_=self.planes[:, CU0 + (c + 1) * CS - 1:CU0 + (c + 1) * CS],
                        identity=self.idf32[:, :]))
                    ev[("cdtp", c)] = self.s_pe.n

                @blk.scalar
                def _(act, c=c):
                    act.wait_ge(self.s_pe.h, ev[("cdtp", c)])
                    self.s_act.inc(act.activation(
                        out=self.planes_bf[0:1, T + c * 128:T + (c + 1) * 128],
                        in_=psS2[0:1, 0:128], func=AF.Exp))
                    ev[("cdrow", c)] = self.s_act.n

                @blk.tensor
                def _(te, c=c):
                    te.wait_ge(self.s_act.h, ev[("cdrow", c)])
                    self.s_pe.inc(te.matmul(
                        out=psS2[:, 128:256],
                        lhsT=self.ones1[:, :],
                        rhs=self.planes_bf[0:1, T + c * 128:T + (c + 1) * 128],
                        start=True, stop=True))
                    ev[("tck1", c)] = self.s_pe.n

                @blk.vector
                def _(v, c=c):
                    v.wait_ge(self.s_pe.h, ev[("tck1", c)])
                    self.s_dve.inc(v.tensor_copy(
                        out=self.Tc_all[:, c * 128:(c + 1) * 128],
                        in_=psS2[:, 128:256]))
                    ev[("tcev", c)] = self.s_dve.n

            for ft in range(1, 17):
                gp_prefetch(ft + 1)
                te_proj(ft)
                if ft <= 8:
                    g = ft - 1
                    ct = 64 + g
                    dst = self.B_fm[:, g * T:(g + 1) * T]
                else:
                    g = ft - 9
                    ct = 72 + g
                    dst = self.C_fm[:, g * T:(g + 1) * T]
                dve_conv(ft, ct)
                act_silu(ft, ct, dst)

            for ft in range(17, 81):
                k = ft - 17
                gp_prefetch(ft + 1)
                te_proj(ft)
                dve_conv(ft, 0 + k)
                act_silu(ft, 0 + k, self.hs_fm[:, (k % 2) * T:(k % 2) * T + T],
                         hs_k=k)
                if k >= 1:
                    self._p2_ssd_pair(blk, ev, psS1, psS2, k - 1)
                self._p2_hs_stage(blk, ev, k)
            self._p2_ssd_pair(blk, ev, psS1, psS2, 63)

            @blk.gpsimd
            def _(gp):
                gp.wait_ge(self.s_dve2.h, ev[("run", 63, 3)])
                self.c_ag.inc(gp.dma_start(out=self.h_agin[:, 0:INTER],
                                           in_=self.runbf[:, :]), 0)
                self.c_ag.inc(gp.dma_start(allow_slow_non_contiguous=True,
                                           out=self.h_agin[:, INTER:INTER + 1],
                                           in_=self.scr[:, 30:31]), 0)
                self.c_ag.wait(gp, 0)
                gp.collective_compute(
                    "AllGather", OP.bypass,
                    replica_groups=[[0, 1, 2, 3], [4, 5, 6, 7]],
                    ins=[self.h_agin.opt()], outs=[self.h_agout.opt()],
                ).then_inc(self.s_cc.h, 1)
                self.s_cc.n += 1

            for ft in range(81, 145):
                gt = ft - 81
                gp_prefetch(ft + 1)
                te_proj(ft)
                ps = psPA if ft % 2 == 0 else psPB

                @blk.scalar
                def _(act, ft=ft, gt=gt, ps=ps):
                    act.wait_ge(self.s_pe2.h, ev[("proj", ft)])
                    if gt >= 2:
                        self.c_gt.wait(act, gt % 2, at=ev[("gtsp", gt - 2)])
                    self.s_act2.inc(act.activation(
                        out=self.gatest[:, (gt % 2) * T:(gt % 2) * T + T],
                        in_=ps[:, 128:TH], func=AF.Silu))
                    ev[("psfree_a", ft)] = self.s_act2.n
                    ev[("gsilu", gt)] = self.s_act2.n
                ev[("psfree", ft)] = self.s_dve2.n

                @blk.sync
                def _(sync, gt=gt):
                    sync.wait_ge(self.s_act2.h, ev[("gsilu", gt)])
                    self.c_gt.inc(sync.dma_start(
                        out=self.h_gate[gt],
                        in_=self.gatest[:, (gt % 2) * T:(gt % 2) * T + T]), gt % 2)
                    ev[("gtsp", gt)] = self.c_gt.val(gt % 2)

    def _p2_hs_stage(self, blk, ev, k):
        ft = 17 + k

        @blk.scalar
        def _(act):
            act.wait_ge(self.s_act2.h, ev[("silu", ft)])
            if k >= 2:
                act.wait_ge(self.s_pe.h, ev[("ssdpe", k - 2)])
            srcb = self.hs_fm[:, (k % 2) * T:(k % 2) * T + T]
            dst = self.hs_tm[:, (k % 2) * T:(k % 2) * T + T]
            for tc in range(8):
                self.c_tp.inc(act.dma_start(out=dst[:, tc * 128:(tc + 1) * 128],
                                            in_=srcb[:, tc * 128:(tc + 1) * 128],
                                            transpose=True), k % 2)
            ev[("hstm", k)] = self.c_tp.val(k % 2)
            act.wait_ge(self.s_act.h, ev["edA"])
            st = self.sstage
            if ("edbc", k - 1, NCH - 1) in ev:
                act.wait_ge(self.s_pe.h, ev[("edbc", k - 1, NCH - 1)])
            self.c_st.inc(act.dma_start(
                allow_slow_non_contiguous=True,
                out=st[0:1, 0:T],
                in_=self.planes_bf[2 * k:2 * k + 1, 0:T]), 0)
            self.c_st.inc(act.dma_start(
                allow_slow_non_contiguous=True,
                out=st[0:1, T:2 * T],
                in_=self.planes_bf[2 * k + 1:2 * k + 2, 0:T]), 0)
            ev[("edrow", k)] = self.c_st.val(0)

        if k % 8 == 0:
            g = k // 8

            @blk.scalar
            def _(act):
                act.wait_ge(self.s_act2.h, ev[("silu", 1 + g)])
                if g >= 1:
                    act.wait_ge(self.s_pe.h, ev[("stmm", 8 * g - 1, NCH - 1)])
                srcb = self.B_fm[:, g * T:(g + 1) * T]
                for tc in range(8):
                    self.c_bt.inc(act.dma_start(
                        out=self.B_tm[:, tc * 128:(tc + 1) * 128],
                        in_=srcb[:, tc * 128:(tc + 1) * 128],
                        transpose=True), g % 2)
                ev[("btm", g)] = self.c_bt.val(g % 2)

    def _p2_ssd_pair(self, blk, ev, psS1, psS2, k):
        g = k // 8
        ft = 17 + k

        if k % 8 == 0:
            for c in range(NCH):
                @blk.tensor
                def _(te, c=c):
                    if c == 0:
                        te.wait_ge(self.s_act2.h, ev[("silu", 1 + g)])
                        te.wait_ge(self.s_act2.h, ev[("silu", 9 + g)])
                        if ("y1ev", k - 1, NCH - 1) in ev:
                            te.wait_ge(self.s_dve2.h, ev[("y1ev", k - 1, NCH - 1)])
                    else:
                        te.wait_ge(self.s_dve.h, ev[("rawSev", g, c - 1)])
                    Bg = self.B_fm[:, g * T:(g + 1) * T]
                    Cg = self.C_fm[:, g * T:(g + 1) * T]
                    te.matmul(out=psS2[:, 0:256],
                              lhsT=Bg[:, c * CS:c * CS + 128],
                              rhs=Cg[:, c * CS:(c + 1) * CS], start=True, stop=True)
                    self.s_pe.inc(te.matmul(out=psS2[:, 256:384],
                                            lhsT=Bg[:, c * CS + 128:(c + 1) * CS],
                                            rhs=Cg[:, c * CS + 128:(c + 1) * CS],
                                            start=True, stop=True))
                    ev[("rawSmm", g, c)] = self.s_pe.n

                @blk.vector
                def _(v, c=c):
                    v.wait_ge(self.s_pe.h, ev[("rawSmm", g, c)])
                    self.s_dve.inc(v.tensor_copy(
                        out=self.rawS[:, c * 384:c * 384 + 384],
                        in_=psS2[:, 0:384]))
                    ev[("rawSev", g, c)] = self.s_dve.n
            ev[("rawSdone", g)] = ev[("rawSev", g, NCH - 1)]

        for c in range(NCH):
            @blk.tensor
            def _(te, c=c):
                self.c_st.wait(te, 0, at=ev[("edrow", k)])
                if c == 0:
                    if k >= 1:
                        te.wait_ge(self.s_dve2.h, ev[("stev", k - 1, NCH - 1)])
                else:
                    te.wait_ge(self.s_dve2.h, ev[("stev", k, c - 1)])
                st = self.sstage
                te.matmul(out=psS1[:, 0:256],
                          lhsT=self.ones1[:, :],
                          rhs=st[0:1, c * CS:(c + 1) * CS],
                          start=True, stop=True)
                self.s_pe.inc(te.matmul(
                    out=psS1[:, 256:512], lhsT=self.ones1[:, :],
                    rhs=st[0:1, T + c * CS:T + (c + 1) * CS],
                    start=True, stop=True))
                ev[("edbc", k, c)] = self.s_pe.n

            @blk.vector
            def _(v, c=c):
                v.wait_ge(self.s_pe.h, ev[("edbc", k, c)])
                if ("y1ev", k, c - 1) in ev:
                    v.wait_ge(self.s_dve2.h, ev[("y1ev", k, c - 1)])
                for hp in range(2):
                    e0 = hp * 256
                    d0 = hp * 384
                    self.s_dve2.inc(v.tensor_tensor_scan(
                        out=self.d1b[:, d0:d0 + 256],
                        data0=psS1[:, e0:e0 + 256], data1=self.masks[:, 0:256],
                        initial=0.0, op0=OP.mult, op1=OP.add))
                    self.s_dve2.inc(v.tensor_tensor_scan(
                        out=self.d1b[:, d0 + 256:d0 + 384],
                        data0=psS1[:, e0 + 128:e0 + 256], data1=self.masks[:, 384:512],
                        initial=0.0, op0=OP.mult, op1=OP.add))
                self.s_dve2.wait(v)
                ev[("scan", k, c)] = self.s_dve2.n
                for hp in range(2):
                    h = 2 * k + hp
                    d0 = hp * 384
                    for jt in range(2):
                        tcol = self.tmcols[:, (2 * c + jt) * 128 + h:(2 * c + jt) * 128 + h + 1]
                        self.s_dve2.inc(v.scalar_tensor_tensor(
                            out=self.stb[:, d0 + jt * 256:d0 + jt * 256 + (256 if jt == 0 else 128)],
                            in0=self.d1b[:, d0 + jt * 256:d0 + jt * 256 + (256 if jt == 0 else 128)],
                            scalar=tcol,
                            in1=self.rawS[:, c * 384 + jt * 256:c * 384 + jt * 256 + (256 if jt == 0 else 128)],
                            op0=OP.mult, op1=OP.mult))
                self.s_dve2.wait(v)
                ev[("st", k, c)] = self.s_dve2.n

            @blk.tensor
            def _(te, c=c):
                te.wait_ge(self.s_dve2.h, ev[("st", k, c)])
                self.c_tp.wait(te, k % 2, at=ev[("hstm", k)])
                if k % 8 == 0 and c == 0:
                    te.wait_ge(self.s_dve.h, ev[("rawSdone", g)])
                if ("y1ev", k, c - 1) in ev:
                    te.wait_ge(self.s_dve2.h, ev[("y1ev", k, c - 1)])
                elif ("y1ev", k - 1, NCH - 1) in ev:
                    te.wait_ge(self.s_dve2.h, ev[("y1ev", k - 1, NCH - 1)])
                hst = self.hs_tm[:, (k % 2) * T:(k % 2) * T + T]
                for hp in range(2):
                    d0 = hp * 384
                    lt0 = hst[:, (2 * c) * 128 + hp * 64:(2 * c) * 128 + hp * 64 + 64]
                    lt1 = hst[:, (2 * c + 1) * 128 + hp * 64:(2 * c + 1) * 128 + hp * 64 + 64]
                    te.matmul(out=psS2[hp * 64:hp * 64 + 64, 0:256],
                              lhsT=lt0, rhs=self.stb[:, d0:d0 + 256],
                              start=True, stop=False)
                    mm = te.matmul(out=psS2[hp * 64:hp * 64 + 64, 128:256],
                                   lhsT=lt1, rhs=self.stb[:, d0 + 256:d0 + 384],
                                   start=False, stop=True)
                self.s_pe.inc(mm)
                ev[("ymm", k, c)] = self.s_pe.n

            @blk.vector
            def _(v, c=c):
                self.c_bt.wait(v, g % 2, at=ev[("btm", g)])
                self.c_tp.wait(v, k % 2, at=ev[("hstm", k)])
                v.wait_ge(self.s_pe.h, ev.get(("stmm", k, c - 1), ev.get(("stmm", k - 1, NCH - 1), 0)))
                hst = self.hs_tm[:, (k % 2) * T:(k % 2) * T + T]
                for hp in range(2):
                    h = 2 * k + hp
                    for jt in range(2):
                        wcol = self.tmcols[:, WT0 + (2 * c + jt) * 128 + h:WT0 + (2 * c + jt) * 128 + h + 1]
                        self.s_dve2.inc(v.tensor_scalar(
                            out=self.wbb[:, (hp * 2 + jt) * 64:(hp * 2 + jt) * 64 + 64],
                            in0=hst[:, (2 * c + jt) * 128 + hp * 64:(2 * c + jt) * 128 + hp * 64 + 64],
                            scalar1=wcol, scalar2=None, op0=OP.mult))
                self.s_dve2.wait(v)
                ev[("wb", k, c)] = self.s_dve2.n

            @blk.vector
            def _(v, c=c):
                v.wait_ge(self.s_pe.h, ev[("ymm", k, c)])
                if c == 0:
                    self.c_y1.wait(v, k % 2)
                self.s_dve2.inc(v.scalar_tensor_tensor(
                    out=self.y1st[:, (k % 2) * T + c * CS:(k % 2) * T + (c + 1) * CS],
                    in0=self.hs_fm[:, (k % 2) * T + c * CS:(k % 2) * T + (c + 1) * CS],
                    scalar=self.drep[:, k:k + 1], in1=psS2[:, 0:256],
                    op0=OP.mult, op1=OP.add))
                ev[("y1ev", k, c)] = self.s_dve2.n

            @blk.tensor
            def _(te, c=c):
                te.wait_ge(self.s_dve2.h, ev[("wb", k, c)])
                te.wait_ge(self.s_dve2.h, ev[("scan", k, c)])
                for hp in range(2):
                    te.matmul(out=psS1[:, hp * 64:hp * 64 + 64],
                              lhsT=self.B_tm[:, (2 * c) * 128:(2 * c) * 128 + 128],
                              rhs=self.wbb[:, (hp * 2) * 64:(hp * 2) * 64 + 64],
                              start=True, stop=False)
                    mm = te.matmul(out=psS1[:, hp * 64:hp * 64 + 64],
                                   lhsT=self.B_tm[:, (2 * c + 1) * 128:(2 * c + 1) * 128 + 128],
                                   rhs=self.wbb[:, (hp * 2 + 1) * 64:(hp * 2 + 1) * 64 + 64],
                                   start=False, stop=True)
                self.s_pe.inc(mm)
                ev[("stmm", k, c)] = self.s_pe.n
                if c == NCH - 1:
                    ev[("ssdpe", k)] = self.s_pe.n

            @blk.vector
            def _(v, c=c):
                v.wait_ge(self.s_pe.h, ev[("stmm", k, c)])
                slot = (k * NCH + c) % 2
                self.c_ss.wait(v, slot)
                self.s_dve2.inc(v.tensor_copy(
                    out=self.sst[:, slot * 128:slot * 128 + 128],
                    in_=psS1[:, 0:128]))
                self.s_dve2.wait(v)
                self.s_dve2.inc(v.tensor_tensor(
                    out=self.sscr[:, 0:128].rearrange("p (a b) -> p a b", a=2),
                    in0=self.runbf[:, k * 128:(k + 1) * 128].rearrange("p (a b) -> p a b", a=2),
                    in1=self.Tc_all[:, c * 128 + 2 * k:c * 128 + 2 * k + 2]
                        .unsqueeze(2).broadcast_to((128, 2, 64)),
                    op=OP.mult))
                self.s_dve2.wait(v)
                self.s_dve2.inc(v.tensor_add(
                    out=self.runbf[:, k * 128:(k + 1) * 128],
                    in0=self.sscr[:, 0:128],
                    in1=self.sst[:, slot * 128:slot * 128 + 128]))
                self.s_dve2.wait(v)
                ev[("stev", k, c)] = self.s_dve2.n
                if c == NCH - 1 and k == 63:
                    ev[("run", 63, 3)] = self.s_dve2.n
                else:
                    ev[("run", k, c)] = self.s_dve2.n

            @blk.sync
            def _(sync, c=c):
                slot = (k * NCH + c) % 2
                sync.wait_ge(self.s_dve2.h, ev[("stev", k, c)])
                self.c_ss.inc(sync.dma_start(
                    out=self.h_states[c][:, k * 128:(k + 1) * 128],
                    in_=self.sst[:, slot * 128:slot * 128 + 128]), slot)

        @blk.sync
        def _(sync):
            sync.wait_ge(self.s_dve2.h, ev[("y1ev", k, NCH - 1)])
            self.c_y1.inc(sync.dma_start(
                out=self.h_y1[k],
                in_=self.y1st[:, (k % 2) * T:(k % 2) * T + T]), k % 2)
            ev[("y1sp", k)] = self.c_y1.val(k % 2)

    def emit_pb(self):
        nc = self.nc
        ev = {}

        with (nc.psum_tensor("psB1", [128, 1536], F32) as psB1,
              nc.psum_tensor("psB2", [128, 1536], F32) as psB2,
              nc.psum_tensor("psN", [1, 512], F32) as psN,
              nc.psum_tensor("psRq", [128, 512], F32) as psRq,
              nc.Block("pb") as blk):
            psW = psB1

            @blk.gpsimd
            def _(gp):
                gp.wait_ge(self.s_cc.h, self.s_cc.n)
                for j in range(4):
                    self.c_ag.inc(gp.dma_start(
                        allow_slow_non_contiguous=True,
                        out=self.scr[:, 32 + j:33 + j],
                        in_=self.h_agout[128 * j:128 * (j + 1), INTER:INTER + 1]), 0)
                ev["glogs"] = self.c_ag.val(0)

            @blk.vector
            def _(v):
                self.c_ag.wait(v, 0, at=ev["glogs"])
                self.s_dve.inc(v.tensor_tensor(out=self.scr[:, 36:40],
                                               in0=self.scr[:, 32:36],
                                               in1=self.consts[:, 5:9], op=OP.mult))
                self.s_dve.wait(v)
                v.memset(self.scr[:, 43:44], 0.0)
                self.s_dve.inc(v.tensor_copy(out=self.scr[:, 42:43],
                                             in_=self.scr[:, 39:40]))
                self.s_dve.wait(v)
                self.s_dve.inc(v.tensor_add(out=self.scr[:, 41:42],
                                            in0=self.scr[:, 38:39],
                                            in1=self.scr[:, 42:43]))
                self.s_dve.wait(v)
                self.s_dve.inc(v.tensor_add(out=self.scr[:, 40:41],
                                            in0=self.scr[:, 37:38],
                                            in1=self.scr[:, 41:42]))
                ev["tj"] = self.s_dve.n

            @blk.scalar
            def _(act):
                act.wait_ge(self.s_dve.h, ev["tj"])
                self.s_act.inc(act.activation(out=self.scr[:, 44:48],
                                              in_=self.scr[:, 40:44], func=AF.Exp))
                ev["wexp"] = self.s_act.n

            @blk.vector
            def _(v):
                v.wait_ge(self.s_act.h, ev["wexp"])
                self.s_dve.inc(v.tensor_tensor(out=self.scr[:, 44:48],
                                               in0=self.scr[:, 44:48],
                                               in1=self.consts[:, 5:9], op=OP.mult))
                ev["wj"] = self.s_dve.n

            @blk.gpsimd
            def _(gp):
                gp.wait_ge(self.s_dve.h, ev["wj"])
                for j in range(4):
                    self.c_sg.inc(gp.dma_start(
                        allow_slow_non_contiguous=True,
                        out=self.wrow[0:1, 128 * j:128 * j + 128],
                        in_=self.scr[:, 44 + j:45 + j]), 1)
                ev["wrows"] = self.c_sg.val(1)

            for j in range(4):
                @blk.tensor
                def _(te, j=j):
                    self.c_sg.wait(te, 1, at=ev["wrows"])
                    if j > 0:
                        te.wait_ge(self.s_dve.h, ev[("sin", j - 1)])
                    self.s_pe.inc(te.matmul(out=psW[:, 0:128], lhsT=self.ones1[:, :],
                                            rhs=self.wrow[0:1, 128 * j:128 * j + 128],
                                            start=True, stop=True))
                    ev[("wk1", j)] = self.s_pe.n

                for pc in range(8):
                    @blk.sync
                    def _(sync, j=j, pc=pc):
                        if j == 0 and pc == 0:
                            self.wait_everything(sync)
                        slot = pc % 2
                        if (j * 8 + pc) >= 2:
                            self.s_dve.wait(sync, at=ev.get(("sacc", j, pc - 2),
                                            ev.get(("sacc", j - 1, 6 + pc), 0)))
                        self.c_ld.inc(sync.dma_start(
                            out=self.sjp[:, slot * 1024:slot * 1024 + 1024],
                            in_=self.h_agout[128 * j:128 * (j + 1),
                                             1024 * pc:1024 * (pc + 1)]), slot)
                        ev[("sj", j, pc)] = self.c_ld.val(slot)

                    @blk.vector
                    def _(v, j=j, pc=pc):
                        slot = pc % 2
                        self.c_ld.wait(v, slot, at=ev[("sj", j, pc)])
                        v.wait_ge(self.s_pe.h, ev[("wk1", j)])
                        if j == 0:
                            self.s_dve.inc(v.tensor_tensor(
                                out=self.prev[:, 1024 * pc:1024 * (pc + 1)]
                                    .rearrange("p (a b) -> p a b", a=16),
                                in0=self.sjp[:, slot * 1024:slot * 1024 + 1024]
                                    .rearrange("p (a b) -> p a b", a=16),
                                in1=psW[:, 16 * pc:16 * pc + 16].unsqueeze(2)
                                    .broadcast_to((128, 16, 64)),
                                op=OP.mult))
                        else:
                            self.s_dve.inc(v.tensor_tensor(
                                out=self.sjp[:, slot * 1024:slot * 1024 + 1024]
                                    .rearrange("p (a b) -> p a b", a=16),
                                in0=self.sjp[:, slot * 1024:slot * 1024 + 1024]
                                    .rearrange("p (a b) -> p a b", a=16),
                                in1=psW[:, 16 * pc:16 * pc + 16].unsqueeze(2)
                                    .broadcast_to((128, 16, 64)),
                                op=OP.mult))
                            self.s_dve.wait(v)
                            self.s_dve.inc(v.tensor_add(
                                out=self.prev[:, 1024 * pc:1024 * (pc + 1)],
                                in0=self.prev[:, 1024 * pc:1024 * (pc + 1)],
                                in1=self.sjp[:, slot * 1024:slot * 1024 + 1024]))
                        ev[("sacc", j, pc)] = self.s_dve.n
                        if pc == 7:
                            ev[("sin", j)] = self.s_dve.n
            ev["sin"] = ev[("sacc", 3, 7)]

            for c in range(NCH):
                self._pb_chunk(blk, ev, psB1, psB2, psN, psRq, c)

    def _pb_chunk(self, blk, ev, psB1, psB2, psN, psRq, c):
        nc = self.nc

        @blk.scalar
        def _(act):
            if c == 0:
                act.wait_ge(self.s_dve.h, ev["sin"])
            else:
                act.wait_ge(self.s_dve2.h, ev[("ydone", c - 1)])
            self.s_act.inc(act.activation(
                out=self.evpl[:, 0:CS],
                in_=self.planes[:, T + c * CS:T + (c + 1) * CS], func=AF.Exp))
            ev[("evpl", c)] = self.s_act.n

        @blk.vector
        def _(v):
            if c == 0:
                v.wait_ge(self.s_dve.h, ev["sin"])
            else:
                v.wait_ge(self.s_dve.h, ev[("supd", c - 1, 7)])
                v.wait_ge(self.s_pe.h, ev[("ymm2", c - 1, 63)])
            for q in range(8):
                self.s_dve2.inc(v.tensor_copy(
                    out=self.prunb[:, q * 1024:(q + 1) * 1024],
                    in_=self.prev[:, q * 1024:(q + 1) * 1024]))
            ev[("prunb", c)] = self.s_dve2.n

        for half in range(8):
            @blk.sync
            def _(sync, half=half):
                if c == 0 and half == 0:
                    sync.wait_ge(self.s_dve.h, ev["sin"])
                if ("ydone", c - 1) in ev:
                    sync.wait_ge(self.s_dve2.h, ev[("ydone", c - 1)])
                self.c_ld.inc(sync.dma_start(
                    out=self.yb[:, half * 2048:(half + 1) * 2048]
                        .rearrange("p (k t) -> p k t", k=8),
                    in_=self.h_y1[8 * half:8 * (half + 1), :, c * CS:(c + 1) * CS]
                        .rearrange("k p t -> p k t")), 2)
                self.c_ld.inc(sync.dma_start(
                    out=self.gateb[:, half * 2048:(half + 1) * 2048]
                        .rearrange("p (k t) -> p k t", k=8),
                    in_=self.h_gate[8 * half:8 * (half + 1), :, c * CS:(c + 1) * CS]
                        .rearrange("k p t -> p k t")), 3)
        ev[("reloadall", c)] = (self.c_ld.val(2), self.c_ld.val(3))

        for i in range(66):
            if i < 64 and i % 8 == 0:
                q = i // 8
                @blk.gpsimd
                def _(gp, q=q):
                    gp.wait_ge(self.s_act.h, ev[("evpl", c)])
                    if ("evk1", c, 8 * q - 2) in ev:
                        gp.wait_ge(self.s_pe.h, ev[("evk1", c, 8 * q - 2)])
                    self.c_sg.inc(gp.dma_start(
                        allow_slow_non_contiguous=True,
                        out=self.evst[0:1, (q % 2) * 4096:(q % 2) * 4096 + 4096]
                            .rearrange("o (h t) -> o h t", h=16),
                        in_=self.evpl[16 * q:16 * (q + 1), 0:CS]), 0)
                    ev[("evst", c, q)] = self.c_sg.val(0)
            if i < 64:
                self._pb_stage1(blk, ev, psB1, c, i)
            if 2 <= i <= 65:
                self._pb_stage3(blk, ev, psN, c, i - 2)
                if (i - 2) % 8 == 7:
                    self._pb_group_norm(blk, ev, psN, psRq, c, (i - 2) // 8)
            if 1 <= i <= 64:
                self._pb_stage2(blk, ev, psB1, psB2, c, i - 1)
        ev[("ydone", c)] = self.s_dve2.n

        for pc in range(8):
            @blk.sync
            def _(sync, pc=pc):
                sync.wait_ge(self.s_dve2.h, ev[("ydone", c)])
                slot = pc % 2
                if ("supd", c, pc - 2) in ev:
                    sync.wait_ge(self.s_dve.h, ev[("supd", c, pc - 2)])
                self.c_ld.inc(sync.dma_start(
                    out=self.sjp[:, slot * 1024:slot * 1024 + 1024],
                    in_=self.h_states[c][:, 1024 * pc:1024 * (pc + 1)]), slot)
                ev[("sld", c, pc)] = self.c_ld.val(slot)

            @blk.vector
            def _(v, pc=pc):
                slot = pc % 2
                self.c_ld.wait(v, slot, at=ev[("sld", c, pc)])
                self.s_dve.inc(v.tensor_tensor(
                    out=self.prev[:, 1024 * pc:1024 * (pc + 1)]
                        .rearrange("p (a b) -> p a b", a=16),
                    in0=self.prev[:, 1024 * pc:1024 * (pc + 1)]
                        .rearrange("p (a b) -> p a b", a=16),
                    in1=self.Tc_all[:, c * 128 + 16 * pc:c * 128 + 16 * pc + 16]
                        .unsqueeze(2).broadcast_to((128, 16, 64)),
                    op=OP.mult))
                self.s_dve.wait(v)
                self.s_dve.inc(v.tensor_add(
                    out=self.prev[:, 1024 * pc:1024 * (pc + 1)],
                    in0=self.prev[:, 1024 * pc:1024 * (pc + 1)],
                    in1=self.sjp[:, slot * 1024:slot * 1024 + 1024]))
                ev[("supd", c, pc)] = self.s_dve.n

    def _pb_stage1(self, blk, ev, psB1, c, k):
        g = k // 8
        q = k // 8
        sl = k % 3
        b1 = psB1[:, sl * 512:sl * 512 + 512]

        @blk.tensor
        def _(te):
            self.c_sg.wait(te, 0, at=ev[("evst", c, q)])
            if ("ctld", c, k - 3) in ev:
                te.wait_ge(self.s_dve2.h, ev[("ctld", c, k - 3)])
            elif c > 0 and ("ctld", c - 1, _prev_slot_pair(k)) in ev:
                te.wait_ge(self.s_dve2.h, ev[("ctld", c - 1, _prev_slot_pair(k))])
            r0 = (2 * k) % 16
            te.matmul(out=b1[:, 0:256], lhsT=self.ones1[:, :],
                      rhs=self.evst[0:1, (q % 2) * 4096 + r0 * CS:
                                    (q % 2) * 4096 + (r0 + 1) * CS],
                      start=True, stop=True)
            self.s_pe.inc(te.matmul(
                out=b1[:, 256:512], lhsT=self.ones1[:, :],
                rhs=self.evst[0:1, (q % 2) * 4096 + (r0 + 1) * CS:
                              (q % 2) * 4096 + (r0 + 2) * CS],
                start=True, stop=True))
            ev[("evk1", c, k)] = self.s_pe.n

        @blk.vector
        def _(v):
            v.wait_ge(self.s_pe.h, ev[("evk1", c, k)])
            v.wait_ge(self.s_dve2.h, ev[("prunb", c)])
            if ("ymm2", c, k - 3) in ev:
                v.wait_ge(self.s_pe.h, ev[("ymm2", c, k - 3)])
            elif c > 0 and ("ymm2", c - 1, _prev_slot_pair(k)) in ev:
                v.wait_ge(self.s_pe.h, ev[("ymm2", c - 1, _prev_slot_pair(k))])
            for hp in range(2):
                self.s_dve2.inc(v.tensor_tensor(
                    out=self.ctld[:, sl * 512 + hp * CS:sl * 512 + (hp + 1) * CS],
                    in0=self.C_fm[:, g * T + c * CS:g * T + (c + 1) * CS],
                    in1=b1[:, hp * 256:hp * 256 + 256], op=OP.mult))
            self.s_dve2.wait(v)
            ev[("ctld", c, k)] = self.s_dve2.n

    def _pb_stage2(self, blk, ev, psB1, psB2, c, k):
        sl = k % 3
        b2 = psB2[:, sl * 512:sl * 512 + 256]

        @blk.tensor
        def _(te):
            te.wait_ge(self.s_dve2.h, ev[("ctld", c, k)])
            if ("gev", c, k - 3) in ev:
                te.wait_ge(self.s_dve2.h, ev[("gev", c, k - 3)])
            elif c > 0 and ("gev", c - 1, _prev_slot_pair(k)) in ev:
                te.wait_ge(self.s_dve2.h, ev[("gev", c - 1, _prev_slot_pair(k))])
            for hp in range(2):
                mm = te.matmul(out=b2[hp * 64:hp * 64 + 64, 0:256],
                               lhsT=self.prunb[:, k * 128 + hp * 64:k * 128 + hp * 64 + 64],
                               rhs=self.ctld[:, sl * 512 + hp * CS:sl * 512 + (hp + 1) * CS],
                               start=True, stop=True)
            self.s_pe.inc(mm)
            ev[("ymm2", c, k)] = self.s_pe.n

        @blk.vector
        def _(v):
            v.wait_ge(self.s_pe.h, ev[("ymm2", c, k)])
            rl = ev[("reloadall", c)]
            self.c_ld.wait(v, 2, at=rl[0])
            self.c_ld.wait(v, 3, at=rl[1])
            half = k // 8
            kk = k % 8
            ybs = self.yb[:, half * 2048 + kk * CS:half * 2048 + (kk + 1) * CS]
            gbs = self.gateb[:, half * 2048 + kk * CS:half * 2048 + (kk + 1) * CS]
            self.s_dve2.inc(v.tensor_add(out=self.gy[:, sl * CS:(sl + 1) * CS],
                                         in0=ybs, in1=b2))
            self.s_dve2.wait(v)
            self.s_dve2.inc(v.tensor_tensor(
                out=self.gpre[:, (k % 8) * CS:(k % 8 + 1) * CS],
                in0=self.gy[:, sl * CS:(sl + 1) * CS], in1=gbs, op=OP.mult))
            self.s_dve2.wait(v)
            ev[("gev", c, k)] = self.s_dve2.n

        @blk.scalar
        def _(act):
            act.wait_ge(self.s_dve2.h, ev[("gev", c, k)])
            self.s_act2.inc(act.activation(
                out=self.gsq[:, sl * CS:(sl + 1) * CS],
                in_=self.gpre[:, (k % 8) * CS:(k % 8 + 1) * CS], func=AF.Square))
            ev[("gsq", c, k)] = self.s_act2.n

    def _pb_stage3(self, blk, ev, psN, c, k):
        sl = k % 3

        @blk.tensor
        def _(te):
            te.wait_ge(self.s_act2.h, ev[("gsq", c, k)])
            if k % 8 == 0:
                if ("nsq", c, k // 8 - 1) in ev:
                    te.wait_ge(self.s_act.h, ev[("nsq", c, k // 8 - 1)])
                elif c > 0 and ("nsq", c - 1, 7) in ev:
                    te.wait_ge(self.s_act.h, ev[("nsq", c - 1, 7)])
            self.s_pe.inc(te.matmul(out=psN[0:1, 0:256], lhsT=self.onescol[:, :],
                                    rhs=self.gsq[:, sl * CS:(sl + 1) * CS],
                                    start=(k % 8 == 0), stop=(k % 8 == 7)))
            ev[("nmm", c, k)] = self.s_pe.n

    def _pb_group_norm(self, blk, ev, psN, psRq, c, g):
        @blk.scalar
        def _(act):
            act.wait_ge(self.s_pe.h, ev[("nmm", c, 8 * g + 7)])
            self.s_act.inc(act.activation(out=self.nrow[0:1, 0:CS],
                                          in_=psN[0:1, 0:256], func=AF.Sqrt,
                                          scale=np.float32(1.0 / 1024.0),
                                          bias=np.float32(EPS)))
            ev[("nsq", c, g)] = self.s_act.n

        @blk.vector
        def _(v):
            v.wait_ge(self.s_act.h, ev[("nsq", c, g)])
            self.s_dve.inc(v.reciprocal(out=self.nrow[0:1, 0:CS],
                                        in_=self.nrow[0:1, 0:CS]))
            self.s_dve.wait(v)
            self.s_dve.inc(v.tensor_copy(out=self.nrow_bf[0:1, 0:CS],
                                         in_=self.nrow[0:1, 0:CS]))
            ev[("rqrow", c, g)] = self.s_dve.n

        @blk.tensor
        def _(te):
            te.wait_ge(self.s_dve.h, ev[("rqrow", c, g)])
            if ("gnorm", c, g - 1) in ev:
                te.wait_ge(self.s_dve2.h, ev[("gnorm", c, g - 1)])
            elif c > 0:
                te.wait_ge(self.s_dve2.h, ev[("gnorm", c - 1, 7)])
            self.s_pe.inc(te.matmul(
                out=psRq[:, 0:256], lhsT=self.ones1[:, :],
                rhs=self.nrow_bf[0:1, 0:CS],
                start=True, stop=True))
            ev[("rqk1", c, g)] = self.s_pe.n

        @blk.vector
        def _(v):
            v.wait_ge(self.s_pe.h, ev[("rqk1", c, g)])
            self.c_gt.wait(v, 0)
            self.c_gt.wait(v, 1)
            for kk in range(8):
                k = 8 * g + kk
                self.s_dve2.inc(v.scalar_tensor_tensor(
                    out=self.gst[:, kk * CS:(kk + 1) * CS],
                    in0=psRq[:, 0:256], scalar=self.nwrep[:, k:k + 1],
                    in1=self.gpre[:, kk * CS:(kk + 1) * CS],
                    op0=OP.mult, op1=OP.mult))
            self.s_dve2.wait(v)
            ev[("gnorm", c, g)] = self.s_dve2.n

        @blk.sync
        def _(sync):
            sync.wait_ge(self.s_dve2.h, ev[("gnorm", c, g)])
            self.c_gt.inc(sync.dma_start(
                out=self.h_g[8 * g:8 * (g + 1), :, c * CS:(c + 1) * CS]
                    .rearrange("k p t -> p k t"),
                in_=self.gst[:, 0:2048].rearrange("p (k t) -> p k t", k=8)), g % 2)
            ev[("gsp", c, g)] = self.c_gt.val(g % 2)

    def emit_p3a(self):
        nc = self.nc
        ev = {}
        with nc.Block("gl") as blk:
            @blk.sync
            def _(sync):
                self.wait_everything(sync)
                for q in range(4):
                    self.c_ld.inc(sync.dma_start(
                        out=self.g_all[:, q * 16384:(q + 1) * 16384]
                            .rearrange("p (k t) -> p k t", k=16),
                        in_=self.h_g[16 * q:16 * (q + 1)]
                            .rearrange("k p t -> p k t")), q % 2)

        bws = [self.c_w.val(j) for j in range(2)]
        bxs = [self.c_x.val(j) for j in range(2)]
        bsp = [self.c_sp.val(j) for j in range(2)]
        bp2, bd2 = self.s_pe2.n, self.s_dve2.n
        w_done = [bws[dt % 2] + 16 * (dt // 2 + 1) for dt in range(32)]
        x_done = [bxs[dt % 2] + 16 * (dt // 2 + 1) for dt in range(32)]
        sp_done = [bsp[dt % 2] + 16 * (dt // 2 + 1) for dt in range(32)]
        pe_done = [bp2 + (dt + 1) for dt in range(32)]
        ev_done = [bd2 + (dt + 1) for dt in range(32)]

        with (nc.psum_tensor("psO_A", [128, 1024], F32) as psA,
              nc.psum_tensor("psO_B", [128, 1024], F32) as psB,
              nc.Block("p3a") as blk):
            @blk.gpsimd
            def _(gp):
                self.wait_everything(gp)
                for dt in range(32):
                    if dt >= 2:
                        gp.wait_ge(self.s_pe2.h, pe_done[dt - 2])
                    slab = self.wslab3[:, (dt % 2) * 8192:(dt % 2) * 8192 + 8192]
                    self.c_w.inc(gp.dma_start(out=slab, in_=self.i_wout[dt]), dt % 2)

            @blk.sync
            def _(sync):
                for dt in range(32):
                    if dt >= 2:
                        sync.wait_ge(self.s_dve2.h, ev_done[dt - 2])
                    buf = self.xtile[:, (dt % 2) * TH:(dt % 2) * TH + T]
                    self.c_x.inc(sync.dma_start(out=buf,
                                                in_=self.h_r1[dt][:, HALO:TH]), dt % 2)
                    if dt >= 1:
                        sync.wait_ge(self.s_dve2.h, ev_done[dt - 1])
                        eb = self.evt[:, ((dt - 1) % 2) * TH:((dt - 1) % 2) * TH + T]
                        self.c_sp.inc(sync.dma_start(out=self.h_r2[dt - 1],
                                                     in_=eb), (dt - 1) % 2)
                sync.wait_ge(self.s_dve2.h, ev_done[31])
                eb = self.evt[:, (31 % 2) * TH:(31 % 2) * TH + T]
                self.c_sp.inc(sync.dma_start(out=self.h_r2[31], in_=eb), 31 % 2)
                self.c_sp.wait(sync, 0)
                self.c_sp.wait(sync, 1)

            @blk.tensor
            def _(te):
                self.c_ld.wait(te, 0)
                self.c_ld.wait(te, 1)
                for dt in range(32):
                    self.c_w.wait(te, dt % 2, at=w_done[dt])
                    if dt >= 2:
                        te.wait_ge(self.s_dve2.h, ev_done[dt - 2])
                    ps = psA if dt % 2 == 0 else psB
                    slab = self.wslab3[:, (dt % 2) * 8192:(dt % 2) * 8192 + 8192]
                    for f in range(64):
                        lt = slab[:, f * 128:(f + 1) * 128]
                        for (c0, c1) in ((0, 512), (512, 1024)):
                            mm = te.matmul(out=ps[:, c0:c1], lhsT=lt,
                                           rhs=self.g_all[:, f * T + c0:f * T + c1],
                                           start=(f == 0), stop=(f == 63))
                    self.s_pe2.inc(mm)

            @blk.vector
            def _(v):
                for dt in range(32):
                    v.wait_ge(self.s_pe2.h, pe_done[dt])
                    self.c_x.wait(v, dt % 2, at=x_done[dt])
                    if dt >= 2:
                        self.c_sp.wait(v, dt % 2, at=sp_done[dt - 2])
                    ps = psA if dt % 2 == 0 else psB
                    eb = self.evt[:, (dt % 2) * TH:(dt % 2) * TH + T]
                    self.s_dve2.inc(v.tensor_add(
                        out=eb, in0=ps[:, 0:T],
                        in1=self.xtile[:, (dt % 2) * TH:(dt % 2) * TH + T]))

    def build(self):
        self.emit_init()
        self.push("C_fm", [128, 8 * T], BF16)
        self.push("planes", [128, 3 * T], F32)
        self.push("planes_bf", [128, T + 512], BF16)
        self.push("tmcols", [128, 2 * 8 * 128], F32)
        self.push("Tc_all", [128, 4 * 128], F32)
        self.push("runbf", [128, INTER], BF16)
        self.push("xn", [128, 32 * TH], BF16)
        self.xst = TH
        self.push("rows", [1, 3 * TH], F32)
        self.push("rows_bf", [1, 2 * TH], BF16)
        self.push("xtile", [128, 2 * TH], F32)
        self.push("sqb", [128, 2 * TH], BF16)
        self.push("evt", [128, 2 * TH], F32)
        self.push("wslab", [128, 2 * 4224], BF16)
        self.emit_ln("ln1", "x", TH, mask_halo=True)
        self.emit_ffn("ffn1", TH, "x", self.h_r1)
        self.emit_ln("ln2", "r1", TH, mask_halo=True)
        self.pop("wslab", "evt", "sqb", "xtile", "rows_bf", "rows")
        self.emit_barrier()
        self.push("wslab2", [128, 2 * 4096], BF16)
        self.push("B_fm", [128, INTER], BF16)
        self.push("B_tm", [128, 1024], BF16)
        self.push("hs_fm", [128, 2 * T], BF16)
        self.push("hs_tm", [128, 2 * T], BF16)
        self.push("rawS", [128, 4 * 384], BF16)
        self.push("d1b", [128, 2 * 384], BF16)
        self.push("stb", [128, 2 * 384], BF16)
        self.push("y1st", [128, 2 * T], BF16)
        self.push("gatest", [128, 2 * T], BF16)
        self.push("cacc", [128, T], F32)
        self.push("wbb", [128, 4 * 64], BF16)
        self.push("sst", [128, 2 * 128], F32)
        self.push("sscr", [128, 128], F32)
        self.push("sstage", [1, 2 * T], BF16)
        self.emit_p2()
        self.pop("sstage", "sscr", "sst", "wbb", "cacc", "gatest", "y1st",
                 "stb", "d1b", "rawS", "hs_tm", "hs_fm", "B_tm", "B_fm",
                 "wslab2", "xn")
        self.pop("runbf")
        self.emit_barrier()
        self.push("prev", [128, INTER], F32)
        self.push("sjp", [128, 2 * 1024], F32)
        self.push("yb", [128, 64 * CS], BF16)
        self.push("gateb", [128, 64 * CS], BF16)
        self.push("gpre", [128, 8 * CS], BF16)
        self.push("gst", [128, 8 * CS], BF16)
        self.push("gy", [128, 3 * CS], F32)
        self.push("gsq", [128, 3 * CS], BF16)
        self.push("ctld", [128, 3 * 512], BF16)
        self.push("prunb", [128, INTER], BF16)
        self.push("evpl", [128, CS], BF16)
        self.push("evst", [1, 2 * 4096], BF16)
        self.push("nrow", [1, CS], F32)
        self.push("nrow_bf", [1, CS], BF16)
        self.push("wrow", [1, 512], BF16)
        self.emit_pb()
        self.pop("wrow", "nrow_bf", "nrow", "evst", "evpl", "prunb", "ctld", "gsq", "gy", "gst",
                 "gpre", "gateb", "yb", "sjp", "prev")
        self.pop("Tc_all", "tmcols", "planes_bf", "planes", "C_fm")
        self.emit_barrier()
        self.push("g_all", [128, 64 * T], BF16)
        self.push("wslab3", [128, 2 * 8192], BF16)
        self.push("xtile", [128, 2 * TH], F32)
        self.push("evt", [128, 2 * TH], F32)
        self.emit_p3a()
        self.pop("evt", "xtile", "wslab3", "g_all")
        self.emit_barrier()
        self.push("xn3", [128, 32 * T], BF16)
        self.xn = self.xn3
        self.xst = T
        self.push("rows", [1, 3 * TH], F32)
        self.push("rows_bf", [1, 2 * TH], BF16)
        self.push("xtile", [128, 2 * TH], F32)
        self.push("sqb", [128, 2 * TH], BF16)
        self.push("evt", [128, 2 * TH], F32)
        self.push("wslab", [128, 2 * 4224], BF16)
        self.emit_ln("ln3", "r2", T, mask_halo=False)
        self.emit_ffn("ffn2", T, "r2", self.o_y)
        self.finish()
        return self.nc

    def emit_barrier(self):
        self._barid = getattr(self, "_barid", 0) + 1
        with self.nc.Block(f"bar{self._barid}") as blk:
            self.barrier(blk)

    def finish(self):
        with self.nc.Block("fin") as blk:
            self.barrier(blk)


def kernel(**inputs):
    in_maps = make_inputs(inputs)
    kb = KB()
    nc = kb.build()
    res = run_bass_kernel_spmd(nc, in_maps, core_ids=list(range(8)))
    out = np.empty((2, 4096, 4096), np.float32)
    for k in range(8):
        y = res.results[k]["y_fm"].reshape(4096, T)
        out[k // 4, T * (k % 4):T * (k % 4 + 1), :] = y.T
    return out


# version 08
# speedup vs baseline: 1.0409x; 1.0307x over previous
import numpy as np
import ml_dtypes
import concourse.bass as bass
import concourse.mybir as mybir
from concourse.bass_utils import run_bass_kernel_spmd

F32 = mybir.dt.float32
BF16 = mybir.dt.bfloat16
I32 = mybir.dt.int32
AF = mybir.ActivationFunctionType
OP = mybir.AluOpType
AX = mybir.AxisListType.X

D = 4096
T = 1024
HALO = 128
TH = T + HALO
H, P, N, G, CK, CS = 128, 64, 128, 8, 4, 256
INTER = H * P
NCH = T // CS
EPS = 1e-5

N_DT, N_B, N_C, N_HS, N_GATE = 1, 8, 8, 64, 64
WT0 = 8 * 128
N_FT = N_DT + N_B + N_C + N_HS + N_GATE


def _proj_tile_order():
    tiles = [18432 // 128]
    tiles += [16384 // 128 + i for i in range(8)]
    tiles += [17408 // 128 + i for i in range(8)]
    tiles += [8192 // 128 + i for i in range(64)]
    tiles += list(range(64))
    return tiles


def _panels(wt, n_out_tiles, pad_rows=None):
    if pad_rows is not None and wt.shape[0] < pad_rows:
        wt = np.concatenate([wt, np.zeros((pad_rows - wt.shape[0], wt.shape[1]),
                                          wt.dtype)], 0)
    d_in, d_out = wt.shape
    nci = d_in // 128
    a = wt.reshape(nci, 128, n_out_tiles, 128).transpose(2, 1, 0, 3)
    return np.ascontiguousarray(a.reshape(n_out_tiles, 128, nci * 128))


def make_inputs(full):
    x = np.asarray(full["L0_attn_input_states"])
    wproj = np.asarray(full["L0_attn_in_proj_w"])
    conv_w = np.asarray(full["L0_attn_conv_w"]).reshape(10240, CK)
    conv_b = np.asarray(full["L0_attn_conv_b"])
    dt_bias = np.asarray(full["L0_attn_dt_bias"])
    a_log = np.asarray(full["L0_attn_A_log"])
    d_vec = np.asarray(full["L0_attn_D"])
    norm_w = np.asarray(full["L0_attn_norm_w"])
    out_w = np.asarray(full["L0_attn_out_w"])
    wffn = np.asarray(full["L0_ffn_weight"])
    ffn_b = np.asarray(full["L0_ffn_bias"])
    lora_a = np.asarray(full["L0_ffn_lora_A"])
    lora_b = np.asarray(full["L0_ffn_lora_B"])
    lsi = np.asarray(full["L0_ffn_lora_start_indices"]).astype(np.int32)

    lb_t = lora_b.transpose(0, 2, 1).reshape(48, 4096)
    wffn_aug = np.concatenate([wffn.T, lb_t,
                               np.zeros((80, 4096), wffn.dtype)], 0)
    wffn_p = _panels(np.ascontiguousarray(wffn_aug), 32)
    wproj_all = _panels(np.ascontiguousarray(wproj.T), 18560 // 128)
    wproj_p = np.ascontiguousarray(wproj_all[_proj_tile_order()])
    wout_p = _panels(np.ascontiguousarray(out_w.T), 32)
    la_t = np.ascontiguousarray(lora_a.reshape(48, 4096).T)

    shared = {
        "wffn_p": wffn_p, "wproj_p": wproj_p, "wout_p": wout_p, "la_t": la_t,
        "conv_w_cols": np.ascontiguousarray(conv_w.reshape(80, 128, CK)),
        "conv_b_cols": np.ascontiguousarray(conv_b.reshape(80, 128)),
        "ffn_b_cols": np.ascontiguousarray(ffn_b.reshape(32, 128)),
        "d_rep": np.ascontiguousarray(np.repeat(d_vec, P).reshape(64, 128)),
        "nw_rep": np.ascontiguousarray(norm_w.reshape(64, 128)),
        "dtb_row": np.ascontiguousarray(dt_bias.reshape(1, 128)),
        "alog_row": np.ascontiguousarray(a_log.reshape(1, 128)),
        "ident_f32": np.eye(128, dtype=np.float32),
        "lsi_rep": np.ascontiguousarray(np.broadcast_to(lsi[None, :], (128, 4))),
    }
    in_maps = []
    for k in range(8):
        b, p = k // 4, k % 4
        t0 = T * p
        xs = x[b, t0:t0 + T, :]
        halo = (np.zeros((HALO, D), np.float32) if p == 0
                else x[b, t0 - HALO:t0, :])
        meta = np.zeros((128, 8), np.float32)
        meta[:, 0] = float(b)
        meta[:, 1] = 0.0 if p == 0 else 1.0
        for j in range(4):
            meta[:, 2 + j] = 1.0 if j < p else 0.0
        m = dict(shared)
        m["x_fm"] = np.ascontiguousarray(np.concatenate([halo, xs], 0).T)
        m["meta"] = np.ascontiguousarray(meta)
        in_maps.append(m)
    return in_maps



def _prev4(k):
    return [x for x in range(60, 64) if x % 4 == k % 4][0]

class Sem:
    def __init__(self, nc, name, step=1):
        self.h = nc.alloc_semaphore(name)
        self.step = step
        self.n = 0

    def inc(self, instr, k=1):
        self.n += self.step * k
        return instr.then_inc(self.h, self.step * k)

    def wait(self, eng, at=None):
        eng.wait_ge(self.h, self.n if at is None else at)


class Chan:

    def __init__(self, nc, name, nslots, step=16):
        self.sems = [nc.alloc_semaphore(f"{name}{j}") for j in range(nslots)]
        self.n = [0] * nslots
        self.step = step

    def inc(self, instr, slot):
        self.n[slot] += self.step
        return instr.then_inc(self.sems[slot], self.step)

    def val(self, slot):
        return self.n[slot]

    def wait(self, eng, slot, at=None):
        eng.wait_ge(self.sems[slot], self.n[slot] if at is None else at)

    def wait_all(self, eng):
        for j, s in enumerate(self.sems):
            if self.n[j]:
                eng.wait_ge(s, self.n[j])


class KB:
    def __init__(self, taps=()):
        self.taps = set(taps)
        nc = self.nc = bass.Bass("TRN2", target_bir_lowering=False)
        dram = nc.dram_tensor
        self.i_x = dram("x_fm", [D, TH], F32, kind="ExternalInput").ap()
        self.i_wffn = dram("wffn_p", [32, 128, 4224], F32, kind="ExternalInput").ap()
        self.i_wproj = dram("wproj_p", [N_FT, 128, 4096], F32, kind="ExternalInput").ap()
        self.i_wout = dram("wout_p", [32, 128, 8192], F32, kind="ExternalInput").ap()
        self.i_la = dram("la_t", [D, 48], F32, kind="ExternalInput").ap()
        self.i_convw = dram("conv_w_cols", [80, 128, CK], F32, kind="ExternalInput").ap()
        self.i_convb = dram("conv_b_cols", [80, 128], F32, kind="ExternalInput").ap()
        self.i_fb = dram("ffn_b_cols", [32, 128], F32, kind="ExternalInput").ap()
        self.i_drep = dram("d_rep", [64, 128], F32, kind="ExternalInput").ap()
        self.i_nwrep = dram("nw_rep", [64, 128], F32, kind="ExternalInput").ap()
        self.i_dtb = dram("dtb_row", [1, 128], F32, kind="ExternalInput").ap()
        self.i_alog = dram("alog_row", [1, 128], F32, kind="ExternalInput").ap()
        self.i_idf32 = dram("ident_f32", [128, 128], F32, kind="ExternalInput").ap()
        self.i_lsi = dram("lsi_rep", [128, 4], I32, kind="ExternalInput").ap()
        self.i_meta = dram("meta", [128, 8], F32, kind="ExternalInput").ap()
        self.o_y = dram("y_fm", [32, 128, T], F32, kind="ExternalOutput").ap()

        self.h_r1 = dram("r1_sp", [32, 128, TH], F32).ap()
        self.h_r2 = dram("r2_sp", [32, 128, T], F32).ap()
        self.h_gate = dram("gate_sp", [64, 128, T], BF16).ap()
        self.h_y1 = dram("y1_sp", [64, 128, T], BF16).ap()
        self.h_states = dram("states_sp", [NCH, 128, INTER], F32).ap()
        self.h_g = dram("g_sp", [64, 128, T], BF16).ap()
        self.h_agin = dram("ag_in", [128, INTER + 1], F32).ap()
        self.h_agout = dram("ag_out", [512, INTER + 1], F32).ap()

        sb = self._sb_perm = {}
        def L(name, shape, dt):
            t = nc.alloc_sbuf_tensor(name, shape, dt, side="left")
            sb[name] = t
            return t
        self.consts = L("consts", [128, 16], F32)
        self.convw = L("convw", [128, 80 * CK], F32)
        self.convb = L("convb", [128, 80], F32)
        self.drep = L("drep", [128, 64], F32)
        self.nwrep = L("nwrep", [128, 64], F32)
        self.hbcol = L("hbcol", [128, 32], F32)
        self.idf32 = L("idf32", [128, 128], F32)
        self.la = L("la", [128, 32 * 48], BF16)
        self.xa = L("xa", [128, TH], BF16)
        self.scol = L("scol", [48, 1], F32)
        self.ones1 = L("ones1", [1, 128], BF16)
        self.onescol = L("onescol", [128, 1], BF16)
        self.masks = L("masks", [128, 2 * CS], F32)
        self.scr = L("scr", [128, 64], F32)

        self._guards = {}

        self.s_in = Sem(nc, "s_in", 16)
        self.s_ing = Sem(nc, "s_ing", 16)
        self.c_w = Chan(nc, "c_w", 2)
        self.c_x = Chan(nc, "c_x", 2)
        self.c_sp = Chan(nc, "c_sp", 2)
        self.c_ld = Chan(nc, "c_ld", 4)
        self.c_tp = Chan(nc, "c_tp", 2)
        self.c_st = Chan(nc, "c_st", 2)
        self.c_y1 = Chan(nc, "c_y1", 2)
        self.c_gt = Chan(nc, "c_gt", 2)
        self.c_ss = Chan(nc, "c_ss", 2)
        self.c_bt = Chan(nc, "c_bt", 2)
        self.c_ag = Chan(nc, "c_ag", 1)
        self.c_sg = Chan(nc, "c_sg", 2)
        self.s_pe = Sem(nc, "s_pe", 1)
        self.s_pe2 = Sem(nc, "s_pe2", 1)
        self.s_dve = Sem(nc, "s_dve", 1)
        self.s_dve2 = Sem(nc, "s_dve2", 1)
        self.s_act = Sem(nc, "s_act", 1)
        self.s_act2 = Sem(nc, "s_act2", 1)
        self.s_gp = Sem(nc, "s_gp", 1)
        self.s_cc = Sem(nc, "s_cc", 1)
        self.all_sems = [self.s_in, self.s_ing, self.s_pe, self.s_pe2,
                         self.s_dve, self.s_dve2, self.s_act, self.s_act2,
                         self.s_gp, self.s_cc]
        self.all_chans = [self.c_w, self.c_x, self.c_sp, self.c_ld, self.c_tp,
                          self.c_st, self.c_y1, self.c_gt, self.c_ss, self.c_bt,
                          self.c_ag, self.c_sg]
        self.dbg = {}

    def push(self, name, shape, dt):
        self._pushid = getattr(self, "_pushid", 0) + 1
        g = self.nc.sbuf_tensor(f"{name}_{self._pushid}", shape, dt, side="right")
        t = g.__enter__()
        self._guards[name] = g
        setattr(self, name, t)
        return t

    def pop(self, *names):
        for name in names:
            self._guards.pop(name).__exit__(None, None, None)
            delattr(self, name)

    def tap(self, name, shape, dtype=F32):
        if name not in self.taps:
            return None
        ap = self.nc.dram_tensor("dbg_" + name, shape, dtype, kind="ExternalOutput").ap()
        self.dbg[name] = ap
        return ap

    def barrier(self, blk, engines=("sync", "vector", "scalar", "tensor", "gpsimd")):
        for en in engines:
            @getattr(blk, en)
            def _(eng, _en=en):
                for s in self.all_sems:
                    if s.n:
                        eng.wait_ge(s.h, s.n)
                for c in self.all_chans:
                    c.wait_all(eng)

    def wait_everything(self, eng):
        for s in self.all_sems:
            if s.n:
                eng.wait_ge(s.h, s.n)
        for c in self.all_chans:
            c.wait_all(eng)

    def emit_init(self):
        nc = self.nc
        with nc.Block("init") as blk:
            @blk.sync
            def _(sync):
                s = self.s_in
                s.inc(sync.dma_start(out=self.convw[:, :].rearrange("p (t k) -> p t k", t=80),
                                     in_=self.i_convw.rearrange("t p k -> p t k")))
                s.inc(sync.dma_start(allow_slow_non_contiguous=True, out=self.convb[:, :],
                                     in_=self.i_convb.rearrange("t p -> p t")))
                s.inc(sync.dma_start(allow_slow_non_contiguous=True, out=self.drep[:, :],
                                     in_=self.i_drep.rearrange("t p -> p t")))
                s.inc(sync.dma_start(allow_slow_non_contiguous=True, out=self.nwrep[:, :],
                                     in_=self.i_nwrep.rearrange("t p -> p t")))
                s.inc(sync.dma_start(allow_slow_non_contiguous=True, out=self.hbcol[:, :],
                                     in_=self.i_fb.rearrange("t p -> p t")))
                s.inc(sync.dma_start(out=self.idf32[:, :], in_=self.i_idf32[:, :]))
                s.inc(sync.dma_start(allow_slow_non_contiguous=True, out=self.consts[:, 0:1],
                                     in_=self.i_dtb.rearrange("o p -> p o")))
                s.inc(sync.dma_start(allow_slow_non_contiguous=True, out=self.consts[:, 1:2],
                                     in_=self.i_alog.rearrange("o p -> p o")))
                s.inc(sync.dma_start(out=self.consts[:, 3:11], in_=self.i_meta[:, :]))

            @blk.gpsimd
            def _(gp):
                s = self.s_ing
                s.inc(gp.dma_start(out=self.scr[:, 0:4].bitcast(I32), in_=self.i_lsi[:, :]))
                s.inc(gp.dma_start(
                    out=self.la[:, :].rearrange("p (c r) -> p c r", c=32),
                    in_=self.i_la.rearrange("(c p) r -> p c r", p=128)))
                self.s_gp.inc(gp.iota(self.masks[:, 0:CS], [[1, CS]],
                                      channel_multiplier=0,
                                      allow_small_or_imprecise_dtypes=True))
                self.s_gp.inc(gp.iota(self.scr[:, 16:17], [[0, 1]],
                                      channel_multiplier=1,
                                      allow_small_or_imprecise_dtypes=True))

            @blk.vector
            def _(v):
                self.s_in.wait(v)
                self.s_ing.wait(v)
                self.s_gp.wait(v)
                v.memset(self.ones1[:, :], 1.0)
                v.memset(self.onescol[:, :], 1.0)
                v.memset(self.xa[:, :], 0.0)
                self.s_dve.inc(v.tensor_copy(out=self.scr[:, 4:8],
                                             in_=self.scr[:, 0:4].bitcast(I32)))
                self.s_dve.inc(v.tensor_scalar(out=self.masks[:, CS:2 * CS],
                                               in0=self.masks[:, 0:CS],
                                               scalar1=self.scr[:, 16:17], scalar2=128.0,
                                               op0=OP.subtract, op1=OP.is_equal))
                self.s_dve.wait(v)
                self.s_dve.inc(v.tensor_scalar(out=self.masks[:, 0:CS],
                                               in0=self.masks[:, 0:CS],
                                               scalar1=self.scr[:, 16:17], scalar2=0.0,
                                               op0=OP.subtract, op1=OP.is_equal))
                self.s_dve.inc(v.tensor_scalar(out=self.scr[:, 8:12], in0=self.scr[:, 4:8],
                                               scalar1=-1.0, scalar2=self.consts[:, 3:4],
                                               op0=OP.mult, op1=OP.add))
                self.s_dve.wait(v)
                self.s_dve.inc(v.tensor_scalar(out=self.scr[:, 8:12], in0=self.scr[:, 8:12],
                                               scalar1=1.0, scalar2=0.0,
                                               op0=OP.add, op1=OP.max))
                self.s_dve.wait(v)
                self.s_dve.inc(v.tensor_scalar(out=self.scr[:, 8:12], in0=self.scr[:, 8:12],
                                               scalar1=1.0, scalar2=None, op0=OP.min))
                self.s_dve.wait(v)
                self.s_dve.inc(v.reduce_sum(out=self.scr[:, 12:13], in_=self.scr[:, 8:12],
                                            axis=AX))
                self.s_dve.wait(v)
                self.s_dve.inc(v.tensor_scalar(out=self.scr[:, 13:14], in0=self.scr[:, 12:13],
                                               scalar1=-1.0, scalar2=None, op0=OP.add))
                self.s_dve.wait(v)
                self.s_dve.inc(v.tensor_scalar(out=self.scr[:, 14:15], in0=self.scr[:, 13:14],
                                               scalar1=0.0, scalar2=1.0,
                                               op0=OP.max, op1=OP.min))
                self.s_dve.inc(v.tensor_scalar(out=self.scr[:, 15:16], in0=self.scr[:, 13:14],
                                               scalar1=-1.0, scalar2=0.0,
                                               op0=OP.add, op1=OP.max))
                self.s_dve.wait(v)
                self.s_dve.inc(v.tensor_scalar(out=self.scr[:, 15:16], in0=self.scr[:, 15:16],
                                               scalar1=2.0, scalar2=None, op0=OP.min))
                v.memset(self.scr[:, 24:25], 0.0)
                v.memset(self.scr[:, 25:26], 1.0)
                self.s_dve.inc(v.memset(self.scr[:, 26:27], 2.0))
                self.s_dve.wait(v)
                self.s_dve.inc(v.tensor_scalar(out=self.scr[:, 27:30], in0=self.scr[:, 24:27],
                                               scalar1=-1.0, scalar2=self.scr[:, 15:16],
                                               op0=OP.mult, op1=OP.add))
                self.s_dve.wait(v)
                self.s_dve.inc(v.tensor_tensor(out=self.scr[:, 27:30], in0=self.scr[:, 27:30],
                                               in1=self.scr[:, 27:30], op=OP.mult))
                self.s_dve.wait(v)
                self.s_dve.inc(v.tensor_scalar(out=self.scr[:, 27:30], in0=self.scr[:, 27:30],
                                               scalar1=-1.0, scalar2=1.0,
                                               op0=OP.mult, op1=OP.add))
                self.s_dve.wait(v)
                self.s_dve.inc(v.tensor_scalar(out=self.scr[:, 27:30], in0=self.scr[:, 27:30],
                                               scalar1=0.0, scalar2=self.scr[:, 14:15],
                                               op0=OP.max, op1=OP.mult))
                self.s_dve.wait(v)
                self.s_dve.inc(v.tensor_scalar(out=self.hbcol[:, :], in0=self.hbcol[:, :],
                                               scalar1=0.5, scalar2=None, op0=OP.mult))

            @blk.sync
            def _(sync):
                self.s_dve.wait(sync)
                for kk in range(3):
                    self.s_in.inc(sync.dma_start(
                        allow_slow_non_contiguous=True,
                        out=self.scol[16 * kk:16 * (kk + 1), 0:1],
                        in_=self.scr[16 * kk:16 * (kk + 1), 27 + kk:28 + kk]))

            @blk.scalar
            def _(act):
                self.s_in.wait(act)
                self.s_act.inc(act.activation(out=self.consts[:, 2:3],
                                              in_=self.consts[:, 1:2], func=AF.Exp))

            @blk.vector
            def _(v):
                self.s_act.wait(v)
                self.s_dve.inc(v.tensor_scalar(out=self.consts[:, 2:3],
                                               in0=self.consts[:, 2:3],
                                               scalar1=-1.0, scalar2=None, op0=OP.mult))

    def emit_ln(self, tag, src_kind, ncols, mask_halo):
        nc = self.nc
        src = {"x": None, "r1": self.h_r1, "r2": self.h_r2}[src_kind]
        n_mm = (ncols + 511) // 512
        inv_d = 1.0 / D
        xst = self.xst

        bxs = [self.c_x.val(j) for j in range(2)]
        bd, ba = self.s_dve.n, self.s_act.n
        x_done = [bxs[dt % 2] + 16 * (dt // 2 + 1) for dt in range(32)]
        cast_done = [bd + (dt + 1) for dt in range(32)]
        sq_done = [ba + (dt + 1) for dt in range(32)]
        bp = self.s_pe.n

        with (nc.psum_tensor("ps_sum_" + tag, [1, 1536], F32) as ps_sum,
              nc.psum_tensor("ps_sq_" + tag, [1, 1536], F32) as ps_sq,
              nc.Block("lnA_" + tag) as blk):
            @blk.sync
            def _(sync):
                for dt in range(32):
                    if dt >= 2:
                        sync.wait_ge(self.s_dve.h, cast_done[dt - 2])
                    buf = self.xtile[:, (dt % 2) * TH:(dt % 2) * TH + ncols]
                    if src_kind == "x":
                        ap = self.i_x[128 * dt:128 * (dt + 1), 0:ncols]
                    else:
                        ap = src[dt][:, 0:ncols]
                    self.c_x.inc(sync.dma_start(out=buf, in_=ap), dt % 2)

            @blk.vector
            def _(v):
                for dt in range(32):
                    self.c_x.wait(v, dt % 2, at=x_done[dt])
                    buf = self.xtile[:, (dt % 2) * TH:(dt % 2) * TH + ncols]
                    self.s_dve.inc(v.tensor_copy(
                        out=self.xn[:, dt * xst:dt * xst + ncols], in_=buf))

            @blk.scalar
            def _(act):
                for dt in range(32):
                    act.wait_ge(self.s_dve.h, cast_done[dt])
                    if dt >= 2:
                        act.wait_ge(self.s_pe.h, bp + dt - 1)
                    sq = self.sqb[:, (dt % 2) * TH:(dt % 2) * TH + ncols]
                    self.s_act.inc(act.activation(
                        out=sq, in_=self.xn[:, dt * xst:dt * xst + ncols],
                        func=AF.Square))

            @blk.tensor
            def _(te):
                for dt in range(32):
                    te.wait_ge(self.s_act.h, sq_done[dt])
                    xb = self.xn[:, dt * xst:dt * xst + ncols]
                    sq = self.sqb[:, (dt % 2) * TH:(dt % 2) * TH + ncols]
                    for j in range(n_mm):
                        c0 = 512 * j
                        c1 = min(512 * (j + 1), ncols)
                        te.matmul(out=ps_sum[0:1, c0:c1], lhsT=self.onescol[:, :],
                                  rhs=xb[:, c0:c1], start=(dt == 0), stop=(dt == 31))
                        mm = te.matmul(out=ps_sq[0:1, c0:c1], lhsT=self.onescol[:, :],
                                       rhs=sq[:, c0:c1], start=(dt == 0), stop=(dt == 31))
                    self.s_pe.inc(mm)

            @blk.scalar
            def _(act):
                self.s_pe.wait(act)
                self.s_act.inc(act.activation(out=self.rows[0:1, 0:ncols],
                                              in_=ps_sum[0:1, 0:ncols],
                                              func=AF.Copy, scale=np.float32(inv_d)))
                act.wait_ge(self.s_act.h, self.s_act.n)
                self.s_act.inc(act.activation(out=self.rows[0:1, TH:TH + ncols],
                                              in_=ps_sq[0:1, 0:ncols],
                                              func=AF.Copy, scale=np.float32(inv_d)))

            @blk.vector
            def _(v):
                self.s_act.wait(v)
                m = self.rows[0:1, 0:ncols]
                e2 = self.rows[0:1, TH:TH + ncols]
                c = self.rows[0:1, 2 * TH:2 * TH + ncols]
                self.s_dve.inc(v.tensor_tensor(out=c, in0=m, in1=m, op=OP.mult))
                self.s_dve.wait(v)
                self.s_dve.inc(v.tensor_tensor(out=e2, in0=e2, in1=c, op=OP.subtract))

            @blk.scalar
            def _(act):
                self.s_dve.wait(act)
                self.s_act.inc(act.activation(out=self.rows[0:1, 2 * TH:2 * TH + ncols],
                                              in_=self.rows[0:1, TH:TH + ncols],
                                              func=AF.Sqrt, bias=np.float32(EPS)))

            @blk.vector
            def _(v):
                self.s_act.wait(v)
                rs = self.rows[0:1, 2 * TH:2 * TH + ncols]
                m = self.rows[0:1, 0:ncols]
                self.s_dve.inc(v.reciprocal(out=rs, in_=rs))
                self.s_dve.wait(v)
                self.s_dve.inc(v.tensor_copy(out=self.rows_bf[0:1, 0:ncols], in_=rs))
                self.s_dve.inc(v.tensor_tensor(out=m, in0=m, in1=rs, op=OP.mult))
                self.s_dve.wait(v)
                self.s_dve.inc(v.tensor_scalar(out=m, in0=m, scalar1=-1.0,
                                               scalar2=None, op0=OP.mult))
                self.s_dve.wait(v)
                self.s_dve.inc(v.tensor_copy(out=self.rows_bf[0:1, TH:TH + ncols],
                                             in_=m))

        with (nc.psum_tensor("ps_rs_" + tag, [128, 1536], F32) as ps_rs,
              nc.psum_tensor("ps_mrs_" + tag, [128, 1536], F32) as ps_mrs,
              nc.Block("lnB_" + tag) as blk):
            @blk.tensor
            def _(te):
                self.s_dve.wait(te)
                for j in range(n_mm):
                    c0, c1 = 512 * j, min(512 * (j + 1), ncols)
                    te.matmul(out=ps_rs[:, c0:c1], lhsT=self.ones1[:, :],
                              rhs=self.rows_bf[0:1, c0:c1], start=True, stop=True)
                    mm = te.matmul(out=ps_mrs[:, c0:c1], lhsT=self.ones1[:, :],
                                   rhs=self.rows_bf[0:1, TH + c0:TH + c1],
                                   start=True, stop=True)
                self.s_pe.inc(mm)

            @blk.vector
            def _(v):
                self.s_pe.wait(v)
                for dt in range(32):
                    xb = self.xn[:, dt * xst:dt * xst + ncols]
                    tmp = self.evt[:, (dt % 2) * TH:(dt % 2) * TH + ncols]
                    self.s_dve.inc(v.tensor_tensor(out=tmp, in0=xb,
                                                   in1=ps_rs[:, 0:ncols], op=OP.mult))
                    self.s_dve.wait(v)
                    if mask_halo:
                        self.s_dve.inc(v.tensor_tensor(out=tmp, in0=tmp,
                                                       in1=ps_mrs[:, 0:ncols], op=OP.add))
                        self.s_dve.wait(v)
                        self.s_dve.inc(v.tensor_scalar(
                            out=self.xn[:, dt * xst:dt * xst + HALO], in0=tmp[:, 0:HALO],
                            scalar1=self.consts[:, 4:5], scalar2=None, op0=OP.mult))
                        self.s_dve.inc(v.tensor_copy(
                            out=self.xn[:, dt * xst + HALO:dt * xst + ncols],
                            in_=tmp[:, HALO:ncols]))
                        self.s_dve.wait(v)
                    else:
                        self.s_dve.inc(v.tensor_tensor(out=xb, in0=tmp,
                                                       in1=ps_mrs[:, 0:ncols], op=OP.add))
                        self.s_dve.wait(v)

    def emit_ffn(self, tag, ncols, res_src, dst):
        nc = self.nc
        n_mm = (ncols + 511) // 512
        xst = self.xst

        with (nc.psum_tensor("ps_lo_" + tag, [48, 1536], F32) as ps_lo,
              nc.Block("lora_" + tag) as blk):
            @blk.tensor
            def _(te):
                self.wait_everything(te)
                for d in range(32):
                    lt = self.la[:, d * 48:(d + 1) * 48]
                    for j in range(n_mm):
                        c0, c1 = 512 * j, min(512 * (j + 1), ncols)
                        mm = te.matmul(out=ps_lo[0:48, c0:c1], lhsT=lt,
                                       rhs=self.xn[:, d * xst + c0:d * xst + c1],
                                       start=(d == 0), stop=(d == 31))
                self.s_pe.inc(mm)

            @blk.vector
            def _(v):
                self.s_pe.wait(v)
                self.s_dve.inc(v.tensor_scalar(out=self.xa[0:48, 0:ncols],
                                               in0=ps_lo[0:48, 0:ncols],
                                               scalar1=self.scol[:, 0:1], scalar2=None,
                                               op0=OP.mult))

        bws = [self.c_w.val(j) for j in range(2)]
        bxs = [self.c_x.val(j) for j in range(2)]
        bsp = [self.c_sp.val(j) for j in range(2)]
        bp2, bd2 = self.s_pe2.n, self.s_dve2.n
        w_done = [bws[dt % 2] + 16 * (dt // 2 + 1) for dt in range(32)]
        x_done = [bxs[dt % 2] + 16 * (dt // 2 + 1) for dt in range(32)]
        sp_done = [bsp[dt % 2] + 16 * (dt // 2 + 1) for dt in range(32)]
        pe_done = [bp2 + (dt + 1) for dt in range(32)]
        ev_done = [bd2 + 2 * (dt + 1) for dt in range(32)]

        with (nc.psum_tensor("ps_mA_" + tag, [128, 1536], F32) as psA,
              nc.psum_tensor("ps_mB_" + tag, [128, 1536], F32) as psB,
              nc.Block("ffn_" + tag) as blk):
            @blk.gpsimd
            def _(gp):
                for dt in range(32):
                    if dt >= 2:
                        gp.wait_ge(self.s_pe2.h, pe_done[dt - 2])
                    slab = self.wslab[:, (dt % 2) * 4224:(dt % 2) * 4224 + 4224]
                    self.c_w.inc(gp.dma_start(out=slab, in_=self.i_wffn[dt]), dt % 2)

            @blk.sync
            def _(sync):
                for dt in range(32):
                    if dt >= 2:
                        sync.wait_ge(self.s_dve2.h, ev_done[dt - 2])
                    buf = self.xtile[:, (dt % 2) * TH:(dt % 2) * TH + ncols]
                    if res_src == "x":
                        ap = self.i_x[128 * dt:128 * (dt + 1), 0:ncols]
                    else:
                        ap = self.h_r2[dt][:, 0:ncols]
                    self.c_x.inc(sync.dma_start(out=buf, in_=ap), dt % 2)
                    if dt >= 1:
                        sync.wait_ge(self.s_dve2.h, ev_done[dt - 1])
                        eb = self.evt[:, ((dt - 1) % 2) * TH:((dt - 1) % 2) * TH + ncols]
                        self.c_sp.inc(sync.dma_start(out=dst[dt - 1][:, 0:ncols],
                                                     in_=eb), (dt - 1) % 2)
                sync.wait_ge(self.s_dve2.h, ev_done[31])
                eb = self.evt[:, (31 % 2) * TH:(31 % 2) * TH + ncols]
                self.c_sp.inc(sync.dma_start(out=dst[31][:, 0:ncols], in_=eb), 31 % 2)
                self.c_sp.wait(sync, 0)
                self.c_sp.wait(sync, 1)

            @blk.tensor
            def _(te):
                self.s_dve.wait(te)
                for dt in range(32):
                    self.c_w.wait(te, dt % 2, at=w_done[dt])
                    if dt >= 2:
                        te.wait_ge(self.s_dve2.h, ev_done[dt - 2])
                    ps = psA if dt % 2 == 0 else psB
                    slab = self.wslab[:, (dt % 2) * 4224:(dt % 2) * 4224 + 4224]
                    for d in range(33):
                        lt = slab[:, d * 128:(d + 1) * 128]
                        rhs_base = (self.xn[:, d * xst:d * xst + ncols] if d < 32
                                    else self.xa[:, 0:ncols])
                        for j in range(n_mm):
                            c0, c1 = 512 * j, min(512 * (j + 1), ncols)
                            mm = te.matmul(out=ps[:, c0:c1], lhsT=lt,
                                           rhs=rhs_base[:, c0:c1],
                                           start=(d == 0), stop=(d == 32))
                    self.s_pe2.inc(mm)

            @blk.vector
            def _(v):
                for dt in range(32):
                    v.wait_ge(self.s_pe2.h, pe_done[dt])
                    self.c_x.wait(v, dt % 2, at=x_done[dt])
                    if dt >= 2:
                        self.c_sp.wait(v, dt % 2, at=sp_done[dt - 2])
                    ps = psA if dt % 2 == 0 else psB
                    eb = self.evt[:, (dt % 2) * TH:(dt % 2) * TH + ncols]
                    self.s_dve2.inc(v.tensor_scalar(out=eb, in0=ps[:, 0:ncols],
                                                    scalar1=0.5,
                                                    scalar2=self.hbcol[:, dt:dt + 1],
                                                    op0=OP.mult, op1=OP.add))
                    self.s_dve2.wait(v)
                    self.s_dve2.inc(v.tensor_tensor(
                        out=eb, in0=eb,
                        in1=self.xtile[:, (dt % 2) * TH:(dt % 2) * TH + ncols],
                        op=OP.add))

    def emit_p2(self):
        nc = self.nc
        ev = {}
        DT0, CU0, SC0 = 0, T, 2 * T

        def cw_col(ct, kk):
            return self.convw[:, ct * CK + kk:ct * CK + kk + 1]

        with (nc.psum_tensor("psP_A", [128, 1536], F32) as psPA,
              nc.psum_tensor("psP_B", [128, 1536], F32) as psPB,
              nc.psum_tensor("psS1", [128, 512], F32) as psS1,
              nc.psum_tensor("psS2", [128, 512], F32) as psS2,
              nc.Block("p2") as blk):

            self.barrier(blk)

            @blk.vector
            def _(v):
                self.s_dve2.inc(v.memset(self.runbf[:, :], 0.0))

            def gp_prefetch(ft):
                if ft >= N_FT:
                    return
                @blk.gpsimd
                def _(gp):
                    if ft >= 2:
                        gp.wait_ge(self.s_pe2.h, ev[("proj", ft - 2)])
                        gp.wait_ge(self.s_dve2.h, ev[("psfree", ft - 2)])
                    slab = self.wslab2[:, (ft % 2) * 4096:(ft % 2) * 4096 + 4096]
                    self.c_w.inc(gp.dma_start(out=slab, in_=self.i_wproj[ft]), ft % 2)
                    ev[("slab", ft)] = self.c_w.val(ft % 2)

            def te_proj(ft):
                @blk.tensor
                def _(te):
                    self.c_w.wait(te, ft % 2, at=ev[("slab", ft)])
                    if ft >= 2:
                        te.wait_ge(self.s_dve2.h, ev[("psfree", ft - 2)])
                        te.wait_ge(self.s_act2.h, ev[("psfree_a", ft - 2)])
                    ps = psPA if ft % 2 == 0 else psPB
                    slab = self.wslab2[:, (ft % 2) * 4096:(ft % 2) * 4096 + 4096]
                    for d in range(32):
                        lt = slab[:, d * 128:(d + 1) * 128]
                        for (c0, c1) in ((0, 512), (512, 1024), (1024, TH)):
                            mm = te.matmul(out=ps[:, c0:c1], lhsT=lt,
                                           rhs=self.xn[:, d * TH + c0:d * TH + c1],
                                           start=(d == 0), stop=(d == 31))
                    self.s_pe2.inc(mm)
                    ev[("proj", ft)] = self.s_pe2.n

            def dve_conv(ft, ct):
                ps = psPA if ft % 2 == 0 else psPB
                @blk.vector
                def _(v):
                    cbuf = (self.cacc[:, 0:T] if ft % 2 == 0
                            else self.planes[:, SC0:SC0 + T])
                    v.wait_ge(self.s_pe2.h, ev[("proj", ft)])
                    if ft == 1:
                        v.wait_ge(self.s_dve.h, ev[("tmev", 7)])
                        v.wait_ge(self.s_dve.h, ev[("wglog", )] if False else ev["wglog"])
                    if ("siluref", ft - 2) in ev:
                        v.wait_ge(self.s_act2.h, ev[("siluref", ft - 2)])
                    self.s_dve2.inc(v.tensor_scalar(
                        out=cbuf, in0=ps[:, 125:125 + T],
                        scalar1=cw_col(ct, 0), scalar2=None, op0=OP.mult))
                    for kk in range(1, CK):
                        self.s_dve2.wait(v)
                        self.s_dve2.inc(v.scalar_tensor_tensor(
                            out=cbuf, in0=ps[:, 125 + kk:125 + kk + T],
                            scalar=cw_col(ct, kk), in1=cbuf,
                            op0=OP.mult, op1=OP.add))
                    ev[("psfree", ft)] = self.s_dve2.n
                    ev[("conv", ft)] = self.s_dve2.n

            def act_silu(ft, ct, dst_ap, hs_k=None):
                @blk.scalar
                def _(act):
                    act.wait_ge(self.s_dve2.h, ev[("conv", ft)])
                    if hs_k is not None and ("y1ev", hs_k - 2, NCH - 1) in ev:
                        act.wait_ge(self.s_dve2.h, ev[("y1ev", hs_k - 2, NCH - 1)])
                    cbuf = (self.cacc[:, 0:T] if ft % 2 == 0
                            else self.planes[:, SC0:SC0 + T])
                    self.s_act2.inc(act.activation(
                        out=dst_ap, in_=cbuf,
                        func=AF.Silu, bias=self.convb[:, ct:ct + 1]))
                    ev[("silu", ft)] = self.s_act2.n
                    ev[("siluref", ft)] = self.s_act2.n
                    ev[("psfree_a", ft)] = self.s_act2.n

            gp_prefetch(0)
            gp_prefetch(1)
            te_proj(0)

            @blk.vector
            def _(v):
                v.wait_ge(self.s_pe2.h, ev[("proj", 0)])
                self.s_dve2.inc(v.tensor_copy(out=self.planes[:, DT0:DT0 + T],
                                              in_=psPA[:, 128:TH]))
                ev[("psfree", 0)] = self.s_dve2.n
                ev[("dtraw", 0)] = self.s_dve2.n
            ev[("psfree_a", 0)] = 0

            @blk.scalar
            def _(act):
                act.wait_ge(self.s_dve2.h, ev[("dtraw", 0)])
                self.s_act.inc(act.activation(out=self.planes[:, SC0:SC0 + T],
                                              in_=self.planes[:, DT0:DT0 + T],
                                              func=AF.Abs, bias=self.consts[:, 0:1]))
                self.s_act.wait(act)
                self.s_act.inc(act.activation(out=self.planes[:, SC0:SC0 + T],
                                              in_=self.planes[:, SC0:SC0 + T],
                                              func=AF.Exp, scale=np.float32(-1.0)))
                self.s_act.wait(act)
                self.s_act.inc(act.activation(out=self.planes[:, SC0:SC0 + T],
                                              in_=self.planes[:, SC0:SC0 + T],
                                              func=AF.Ln, bias=np.float32(1.0)))
                self.s_act.wait(act)
                self.s_act.inc(act.activation(out=self.cacc[:, 0:T],
                                              in_=self.planes[:, DT0:DT0 + T],
                                              func=AF.Relu, bias=self.consts[:, 0:1]))
                ev["sp_parts"] = self.s_act.n

            @blk.vector
            def _(v):
                v.wait_ge(self.s_act.h, ev["sp_parts"])
                self.s_dve.inc(v.tensor_add(out=self.planes[:, DT0:DT0 + T],
                                            in0=self.cacc[:, 0:T],
                                            in1=self.planes[:, SC0:SC0 + T]))
                self.s_dve.wait(v)
                self.s_dve.inc(v.tensor_scalar(out=self.planes[:, SC0:SC0 + T],
                                               in0=self.planes[:, DT0:DT0 + T],
                                               scalar1=self.consts[:, 2:3], scalar2=None,
                                               op0=OP.mult))
                self.s_dve.wait(v)
                for c in range(NCH):
                    s0 = SC0 + c * CS
                    self.s_dve.inc(v.tensor_tensor_scan(
                        out=self.planes[:, CU0 + c * CS:CU0 + (c + 1) * CS],
                        data0=self.planes[:, s0:s0 + CS],
                        data1=self.planes[:, s0:s0 + CS],
                        initial=0.0, op0=OP.add, op1=OP.bypass))
                ev["cum"] = self.s_dve.n

            @blk.scalar
            def _(act):
                act.wait_ge(self.s_dve.h, ev["cum"])
                self.s_act.inc(act.activation(out=self.planes_bf[:, 0:T],
                                              in_=self.planes[:, SC0:SC0 + T],
                                              func=AF.Exp))
                ev["edA"] = self.s_act.n
                for c in range(NCH):
                    act.wait_ge(self.s_act.h, self.s_act.n)
                    self.s_act.inc(act.activation(
                        out=self.cacc[:, c * CS:(c + 1) * CS],
                        in_=self.planes[:, CU0 + c * CS:CU0 + (c + 1) * CS],
                        func=AF.Exp, scale=np.float32(-1.0),
                        bias=self.planes[:, CU0 + (c + 1) * CS - 1:CU0 + (c + 1) * CS]))
                ev["wexp"] = self.s_act.n

            @blk.vector
            def _(v):
                v.wait_ge(self.s_act.h, ev["wexp"])
                self.s_dve.inc(v.tensor_tensor(out=self.planes[:, SC0:SC0 + T],
                                               in0=self.cacc[:, 0:T],
                                               in1=self.planes[:, DT0:DT0 + T],
                                               op=OP.mult))
                self.s_dve.wait(v)
                self.s_dve.inc(v.tensor_copy(out=self.scr[:, 30:31],
                                             in_=self.planes[:, CU0 + CS - 1:CU0 + CS]))
                for c in range(1, NCH):
                    self.s_dve.wait(v)
                    self.s_dve.inc(v.tensor_add(
                        out=self.scr[:, 30:31], in0=self.scr[:, 30:31],
                        in1=self.planes[:, CU0 + (c + 1) * CS - 1:CU0 + (c + 1) * CS]))
                ev["wglog"] = self.s_dve.n

            for tc in range(8):
                @blk.tensor
                def _(te, tc=tc):
                    te.wait_ge(self.s_dve.h, ev["wglog"] if tc == 0 else ev[("tmev", tc - 1)])
                    te.transpose(out=psS2[:, 0:128],
                                 in_=self.planes[:, DT0 + tc * 128:DT0 + (tc + 1) * 128],
                                 identity=self.idf32[:, :])
                    self.s_pe.inc(te.transpose(
                        out=psS2[:, 128:256],
                        in_=self.planes[:, SC0 + tc * 128:SC0 + (tc + 1) * 128],
                        identity=self.idf32[:, :]))
                    ev[("tmtp", tc)] = self.s_pe.n

                @blk.vector
                def _(v, tc=tc):
                    v.wait_ge(self.s_pe.h, ev[("tmtp", tc)])
                    self.s_dve.inc(v.tensor_copy(
                        out=self.tmcols[:, tc * 128:(tc + 1) * 128],
                        in_=psS2[:, 0:128]))
                    self.s_dve.inc(v.tensor_copy(
                        out=self.tmcols[:, WT0 + tc * 128:WT0 + (tc + 1) * 128],
                        in_=psS2[:, 128:256]))
                    ev[("tmev", tc)] = self.s_dve.n

            for c in range(NCH):
                @blk.tensor
                def _(te, c=c):
                    te.wait_ge(self.s_dve.h, ev[("tmev", 7)] if c == 0 else ev[("tcev", c - 1)])
                    self.s_pe.inc(te.transpose(
                        out=psS2[0:1, 0:128],
                        in_=self.planes[:, CU0 + (c + 1) * CS - 1:CU0 + (c + 1) * CS],
                        identity=self.idf32[:, :]))
                    ev[("cdtp", c)] = self.s_pe.n

                @blk.scalar
                def _(act, c=c):
                    act.wait_ge(self.s_pe.h, ev[("cdtp", c)])
                    self.s_act.inc(act.activation(
                        out=self.planes_bf[0:1, T + c * 128:T + (c + 1) * 128],
                        in_=psS2[0:1, 0:128], func=AF.Exp))
                    ev[("cdrow", c)] = self.s_act.n

                @blk.tensor
                def _(te, c=c):
                    te.wait_ge(self.s_act.h, ev[("cdrow", c)])
                    self.s_pe.inc(te.matmul(
                        out=psS2[:, 128:256],
                        lhsT=self.ones1[:, :],
                        rhs=self.planes_bf[0:1, T + c * 128:T + (c + 1) * 128],
                        start=True, stop=True))
                    ev[("tck1", c)] = self.s_pe.n

                @blk.vector
                def _(v, c=c):
                    v.wait_ge(self.s_pe.h, ev[("tck1", c)])
                    self.s_dve.inc(v.tensor_copy(
                        out=self.Tc_all[:, c * 128:(c + 1) * 128],
                        in_=psS2[:, 128:256]))
                    ev[("tcev", c)] = self.s_dve.n

            for ft in range(1, 17):
                gp_prefetch(ft + 1)
                te_proj(ft)
                if ft <= 8:
                    g = ft - 1
                    ct = 64 + g
                    dst = self.B_fm[:, g * T:(g + 1) * T]
                else:
                    g = ft - 9
                    ct = 72 + g
                    dst = self.C_fm[:, g * T:(g + 1) * T]
                dve_conv(ft, ct)
                act_silu(ft, ct, dst)

            for ft in range(17, 81):
                k = ft - 17
                gp_prefetch(ft + 1)
                te_proj(ft)
                dve_conv(ft, 0 + k)
                act_silu(ft, 0 + k, self.hs_fm[:, (k % 2) * T:(k % 2) * T + T],
                         hs_k=k)
                if k >= 1:
                    self._p2_ssd_pair(blk, ev, psS1, psS2, k - 1)
                self._p2_hs_stage(blk, ev, k)
            self._p2_ssd_pair(blk, ev, psS1, psS2, 63)

            @blk.gpsimd
            def _(gp):
                gp.wait_ge(self.s_dve2.h, ev[("run", 63, 3)])
                self.c_ag.inc(gp.dma_start(out=self.h_agin[:, 0:INTER],
                                           in_=self.runbf[:, :]), 0)
                self.c_ag.inc(gp.dma_start(allow_slow_non_contiguous=True,
                                           out=self.h_agin[:, INTER:INTER + 1],
                                           in_=self.scr[:, 30:31]), 0)
                self.c_ag.wait(gp, 0)
                gp.collective_compute(
                    "AllGather", OP.bypass,
                    replica_groups=[[0, 1, 2, 3], [4, 5, 6, 7]],
                    ins=[self.h_agin.opt()], outs=[self.h_agout.opt()],
                ).then_inc(self.s_cc.h, 1)
                self.s_cc.n += 1

            for ft in range(81, 145):
                gt = ft - 81
                gp_prefetch(ft + 1)
                te_proj(ft)
                ps = psPA if ft % 2 == 0 else psPB

                @blk.scalar
                def _(act, ft=ft, gt=gt, ps=ps):
                    act.wait_ge(self.s_pe2.h, ev[("proj", ft)])
                    if gt >= 2:
                        self.c_gt.wait(act, gt % 2, at=ev[("gtsp", gt - 2)])
                    self.s_act2.inc(act.activation(
                        out=self.gatest[:, (gt % 2) * T:(gt % 2) * T + T],
                        in_=ps[:, 128:TH], func=AF.Silu))
                    ev[("psfree_a", ft)] = self.s_act2.n
                    ev[("gsilu", gt)] = self.s_act2.n
                ev[("psfree", ft)] = self.s_dve2.n

                @blk.sync
                def _(sync, gt=gt):
                    sync.wait_ge(self.s_act2.h, ev[("gsilu", gt)])
                    self.c_gt.inc(sync.dma_start(
                        out=self.h_gate[gt],
                        in_=self.gatest[:, (gt % 2) * T:(gt % 2) * T + T]), gt % 2)
                    ev[("gtsp", gt)] = self.c_gt.val(gt % 2)

    def _p2_hs_stage(self, blk, ev, k):
        ft = 17 + k

        @blk.scalar
        def _(act):
            act.wait_ge(self.s_act2.h, ev[("silu", ft)])
            if k >= 2:
                act.wait_ge(self.s_pe.h, ev[("ssdpe", k - 2)])
            srcb = self.hs_fm[:, (k % 2) * T:(k % 2) * T + T]
            dst = self.hs_tm[:, (k % 2) * T:(k % 2) * T + T]
            for tc in range(8):
                self.c_tp.inc(act.dma_start(out=dst[:, tc * 128:(tc + 1) * 128],
                                            in_=srcb[:, tc * 128:(tc + 1) * 128],
                                            transpose=True), k % 2)
            ev[("hstm", k)] = self.c_tp.val(k % 2)
            act.wait_ge(self.s_act.h, ev["edA"])
            st = self.sstage
            if ("edbc", k - 1, NCH - 1) in ev:
                act.wait_ge(self.s_pe.h, ev[("edbc", k - 1, NCH - 1)])
            self.c_st.inc(act.dma_start(
                allow_slow_non_contiguous=True,
                out=st[0:1, 0:T],
                in_=self.planes_bf[2 * k:2 * k + 1, 0:T]), 0)
            self.c_st.inc(act.dma_start(
                allow_slow_non_contiguous=True,
                out=st[0:1, T:2 * T],
                in_=self.planes_bf[2 * k + 1:2 * k + 2, 0:T]), 0)
            ev[("edrow", k)] = self.c_st.val(0)

        if k % 8 == 0:
            g = k // 8

            @blk.scalar
            def _(act):
                act.wait_ge(self.s_act2.h, ev[("silu", 1 + g)])
                if g >= 1:
                    act.wait_ge(self.s_pe.h, ev[("stmm", 8 * g - 1, NCH - 1)])
                srcb = self.B_fm[:, g * T:(g + 1) * T]
                for tc in range(8):
                    self.c_bt.inc(act.dma_start(
                        out=self.B_tm[:, tc * 128:(tc + 1) * 128],
                        in_=srcb[:, tc * 128:(tc + 1) * 128],
                        transpose=True), g % 2)
                ev[("btm", g)] = self.c_bt.val(g % 2)

    def _p2_ssd_pair(self, blk, ev, psS1, psS2, k):
        g = k // 8
        ft = 17 + k

        if k % 8 == 0:
            for c in range(NCH):
                @blk.tensor
                def _(te, c=c):
                    if c == 0:
                        te.wait_ge(self.s_act2.h, ev[("silu", 1 + g)])
                        te.wait_ge(self.s_act2.h, ev[("silu", 9 + g)])
                        if ("y1ev", k - 1, NCH - 1) in ev:
                            te.wait_ge(self.s_dve2.h, ev[("y1ev", k - 1, NCH - 1)])
                    else:
                        te.wait_ge(self.s_dve.h, ev[("rawSev", g, c - 1)])
                    Bg = self.B_fm[:, g * T:(g + 1) * T]
                    Cg = self.C_fm[:, g * T:(g + 1) * T]
                    te.matmul(out=psS2[:, 0:256],
                              lhsT=Bg[:, c * CS:c * CS + 128],
                              rhs=Cg[:, c * CS:(c + 1) * CS], start=True, stop=True)
                    self.s_pe.inc(te.matmul(out=psS2[:, 256:384],
                                            lhsT=Bg[:, c * CS + 128:(c + 1) * CS],
                                            rhs=Cg[:, c * CS + 128:(c + 1) * CS],
                                            start=True, stop=True))
                    ev[("rawSmm", g, c)] = self.s_pe.n

                @blk.vector
                def _(v, c=c):
                    v.wait_ge(self.s_pe.h, ev[("rawSmm", g, c)])
                    self.s_dve.inc(v.tensor_copy(
                        out=self.rawS[:, c * 384:c * 384 + 384],
                        in_=psS2[:, 0:384]))
                    ev[("rawSev", g, c)] = self.s_dve.n
            ev[("rawSdone", g)] = ev[("rawSev", g, NCH - 1)]

        for c in range(NCH):
            @blk.tensor
            def _(te, c=c):
                self.c_st.wait(te, 0, at=ev[("edrow", k)])
                if c == 0:
                    if k >= 1:
                        te.wait_ge(self.s_dve2.h, ev[("stev", k - 1, NCH - 1)])
                else:
                    te.wait_ge(self.s_dve2.h, ev[("stev", k, c - 1)])
                st = self.sstage
                te.matmul(out=psS1[:, 0:256],
                          lhsT=self.ones1[:, :],
                          rhs=st[0:1, c * CS:(c + 1) * CS],
                          start=True, stop=True)
                self.s_pe.inc(te.matmul(
                    out=psS1[:, 256:512], lhsT=self.ones1[:, :],
                    rhs=st[0:1, T + c * CS:T + (c + 1) * CS],
                    start=True, stop=True))
                ev[("edbc", k, c)] = self.s_pe.n

            @blk.vector
            def _(v, c=c):
                v.wait_ge(self.s_pe.h, ev[("edbc", k, c)])
                if ("y1ev", k, c - 1) in ev:
                    v.wait_ge(self.s_dve2.h, ev[("y1ev", k, c - 1)])
                for hp in range(2):
                    e0 = hp * 256
                    d0 = hp * 384
                    self.s_dve2.inc(v.tensor_tensor_scan(
                        out=self.d1b[:, d0:d0 + 256],
                        data0=psS1[:, e0:e0 + 256], data1=self.masks[:, 0:256],
                        initial=0.0, op0=OP.mult, op1=OP.add))
                    self.s_dve2.inc(v.tensor_tensor_scan(
                        out=self.d1b[:, d0 + 256:d0 + 384],
                        data0=psS1[:, e0 + 128:e0 + 256], data1=self.masks[:, 384:512],
                        initial=0.0, op0=OP.mult, op1=OP.add))
                self.s_dve2.wait(v)
                ev[("scan", k, c)] = self.s_dve2.n
                for hp in range(2):
                    h = 2 * k + hp
                    d0 = hp * 384
                    for jt in range(2):
                        tcol = self.tmcols[:, (2 * c + jt) * 128 + h:(2 * c + jt) * 128 + h + 1]
                        self.s_dve2.inc(v.scalar_tensor_tensor(
                            out=self.stb[:, d0 + jt * 256:d0 + jt * 256 + (256 if jt == 0 else 128)],
                            in0=self.d1b[:, d0 + jt * 256:d0 + jt * 256 + (256 if jt == 0 else 128)],
                            scalar=tcol,
                            in1=self.rawS[:, c * 384 + jt * 256:c * 384 + jt * 256 + (256 if jt == 0 else 128)],
                            op0=OP.mult, op1=OP.mult))
                self.s_dve2.wait(v)
                ev[("st", k, c)] = self.s_dve2.n

            @blk.tensor
            def _(te, c=c):
                te.wait_ge(self.s_dve2.h, ev[("st", k, c)])
                self.c_tp.wait(te, k % 2, at=ev[("hstm", k)])
                if k % 8 == 0 and c == 0:
                    te.wait_ge(self.s_dve.h, ev[("rawSdone", g)])
                if ("y1ev", k, c - 1) in ev:
                    te.wait_ge(self.s_dve2.h, ev[("y1ev", k, c - 1)])
                elif ("y1ev", k - 1, NCH - 1) in ev:
                    te.wait_ge(self.s_dve2.h, ev[("y1ev", k - 1, NCH - 1)])
                hst = self.hs_tm[:, (k % 2) * T:(k % 2) * T + T]
                for hp in range(2):
                    d0 = hp * 384
                    lt0 = hst[:, (2 * c) * 128 + hp * 64:(2 * c) * 128 + hp * 64 + 64]
                    lt1 = hst[:, (2 * c + 1) * 128 + hp * 64:(2 * c + 1) * 128 + hp * 64 + 64]
                    te.matmul(out=psS2[hp * 64:hp * 64 + 64, 0:256],
                              lhsT=lt0, rhs=self.stb[:, d0:d0 + 256],
                              start=True, stop=False)
                    mm = te.matmul(out=psS2[hp * 64:hp * 64 + 64, 128:256],
                                   lhsT=lt1, rhs=self.stb[:, d0 + 256:d0 + 384],
                                   start=False, stop=True)
                self.s_pe.inc(mm)
                ev[("ymm", k, c)] = self.s_pe.n

            @blk.vector
            def _(v, c=c):
                self.c_bt.wait(v, g % 2, at=ev[("btm", g)])
                self.c_tp.wait(v, k % 2, at=ev[("hstm", k)])
                v.wait_ge(self.s_pe.h, ev.get(("stmm", k, c - 1), ev.get(("stmm", k - 1, NCH - 1), 0)))
                hst = self.hs_tm[:, (k % 2) * T:(k % 2) * T + T]
                for hp in range(2):
                    h = 2 * k + hp
                    for jt in range(2):
                        wcol = self.tmcols[:, WT0 + (2 * c + jt) * 128 + h:WT0 + (2 * c + jt) * 128 + h + 1]
                        self.s_dve2.inc(v.tensor_scalar(
                            out=self.wbb[:, (hp * 2 + jt) * 64:(hp * 2 + jt) * 64 + 64],
                            in0=hst[:, (2 * c + jt) * 128 + hp * 64:(2 * c + jt) * 128 + hp * 64 + 64],
                            scalar1=wcol, scalar2=None, op0=OP.mult))
                self.s_dve2.wait(v)
                ev[("wb", k, c)] = self.s_dve2.n

            @blk.vector
            def _(v, c=c):
                v.wait_ge(self.s_pe.h, ev[("ymm", k, c)])
                if c == 0:
                    self.c_y1.wait(v, k % 2)
                self.s_dve2.inc(v.scalar_tensor_tensor(
                    out=self.y1st[:, (k % 2) * T + c * CS:(k % 2) * T + (c + 1) * CS],
                    in0=self.hs_fm[:, (k % 2) * T + c * CS:(k % 2) * T + (c + 1) * CS],
                    scalar=self.drep[:, k:k + 1], in1=psS2[:, 0:256],
                    op0=OP.mult, op1=OP.add))
                ev[("y1ev", k, c)] = self.s_dve2.n

            @blk.tensor
            def _(te, c=c):
                te.wait_ge(self.s_dve2.h, ev[("wb", k, c)])
                te.wait_ge(self.s_dve2.h, ev[("scan", k, c)])
                for hp in range(2):
                    te.matmul(out=psS1[:, hp * 64:hp * 64 + 64],
                              lhsT=self.B_tm[:, (2 * c) * 128:(2 * c) * 128 + 128],
                              rhs=self.wbb[:, (hp * 2) * 64:(hp * 2) * 64 + 64],
                              start=True, stop=False)
                    mm = te.matmul(out=psS1[:, hp * 64:hp * 64 + 64],
                                   lhsT=self.B_tm[:, (2 * c + 1) * 128:(2 * c + 1) * 128 + 128],
                                   rhs=self.wbb[:, (hp * 2 + 1) * 64:(hp * 2 + 1) * 64 + 64],
                                   start=False, stop=True)
                self.s_pe.inc(mm)
                ev[("stmm", k, c)] = self.s_pe.n
                if c == NCH - 1:
                    ev[("ssdpe", k)] = self.s_pe.n

            @blk.vector
            def _(v, c=c):
                v.wait_ge(self.s_pe.h, ev[("stmm", k, c)])
                slot = (k * NCH + c) % 2
                self.c_ss.wait(v, slot)
                self.s_dve2.inc(v.tensor_copy(
                    out=self.sst[:, slot * 128:slot * 128 + 128],
                    in_=psS1[:, 0:128]))
                self.s_dve2.wait(v)
                self.s_dve2.inc(v.tensor_tensor(
                    out=self.sscr[:, 0:128].rearrange("p (a b) -> p a b", a=2),
                    in0=self.runbf[:, k * 128:(k + 1) * 128].rearrange("p (a b) -> p a b", a=2),
                    in1=self.Tc_all[:, c * 128 + 2 * k:c * 128 + 2 * k + 2]
                        .unsqueeze(2).broadcast_to((128, 2, 64)),
                    op=OP.mult))
                self.s_dve2.wait(v)
                self.s_dve2.inc(v.tensor_add(
                    out=self.runbf[:, k * 128:(k + 1) * 128],
                    in0=self.sscr[:, 0:128],
                    in1=self.sst[:, slot * 128:slot * 128 + 128]))
                self.s_dve2.wait(v)
                ev[("stev", k, c)] = self.s_dve2.n
                if c == NCH - 1 and k == 63:
                    ev[("run", 63, 3)] = self.s_dve2.n
                else:
                    ev[("run", k, c)] = self.s_dve2.n

            @blk.sync
            def _(sync, c=c):
                slot = (k * NCH + c) % 2
                sync.wait_ge(self.s_dve2.h, ev[("stev", k, c)])
                self.c_ss.inc(sync.dma_start(
                    out=self.h_states[c][:, k * 128:(k + 1) * 128],
                    in_=self.sst[:, slot * 128:slot * 128 + 128]), slot)

        @blk.sync
        def _(sync):
            sync.wait_ge(self.s_dve2.h, ev[("y1ev", k, NCH - 1)])
            self.c_y1.inc(sync.dma_start(
                out=self.h_y1[k],
                in_=self.y1st[:, (k % 2) * T:(k % 2) * T + T]), k % 2)
            ev[("y1sp", k)] = self.c_y1.val(k % 2)

    def emit_pb(self):
        nc = self.nc
        ev = {}

        with (nc.psum_tensor("psB1", [128, 1024], F32) as psB1,
              nc.psum_tensor("psB2", [128, 1024], F32) as psB2,
              nc.psum_tensor("psN", [1, 512], F32) as psN,
              nc.psum_tensor("psRq", [128, 512], F32) as psRq,
              nc.psum_tensor("psW", [128, 512], F32) as psW,
              nc.Block("pb") as blk):

            @blk.gpsimd
            def _(gp):
                gp.wait_ge(self.s_cc.h, self.s_cc.n)
                for j in range(4):
                    self.c_ag.inc(gp.dma_start(
                        allow_slow_non_contiguous=True,
                        out=self.scr[:, 32 + j:33 + j],
                        in_=self.h_agout[128 * j:128 * (j + 1), INTER:INTER + 1]), 0)
                ev["glogs"] = self.c_ag.val(0)

            @blk.vector
            def _(v):
                self.c_ag.wait(v, 0, at=ev["glogs"])
                self.s_dve.inc(v.tensor_tensor(out=self.scr[:, 36:40],
                                               in0=self.scr[:, 32:36],
                                               in1=self.consts[:, 5:9], op=OP.mult))
                self.s_dve.wait(v)
                v.memset(self.scr[:, 43:44], 0.0)
                self.s_dve.inc(v.tensor_copy(out=self.scr[:, 42:43],
                                             in_=self.scr[:, 39:40]))
                self.s_dve.wait(v)
                self.s_dve.inc(v.tensor_add(out=self.scr[:, 41:42],
                                            in0=self.scr[:, 38:39],
                                            in1=self.scr[:, 42:43]))
                self.s_dve.wait(v)
                self.s_dve.inc(v.tensor_add(out=self.scr[:, 40:41],
                                            in0=self.scr[:, 37:38],
                                            in1=self.scr[:, 41:42]))
                ev["tj"] = self.s_dve.n

            @blk.scalar
            def _(act):
                act.wait_ge(self.s_dve.h, ev["tj"])
                self.s_act.inc(act.activation(out=self.scr[:, 44:48],
                                              in_=self.scr[:, 40:44], func=AF.Exp))
                ev["wexp"] = self.s_act.n

            @blk.vector
            def _(v):
                v.wait_ge(self.s_act.h, ev["wexp"])
                self.s_dve.inc(v.tensor_tensor(out=self.scr[:, 44:48],
                                               in0=self.scr[:, 44:48],
                                               in1=self.consts[:, 5:9], op=OP.mult))
                ev["wj"] = self.s_dve.n

            @blk.gpsimd
            def _(gp):
                gp.wait_ge(self.s_dve.h, ev["wj"])
                for j in range(4):
                    self.c_sg.inc(gp.dma_start(
                        allow_slow_non_contiguous=True,
                        out=self.wrow[0:1, 128 * j:128 * j + 128],
                        in_=self.scr[:, 44 + j:45 + j]), 1)
                ev["wrows"] = self.c_sg.val(1)

            for j in range(4):
                @blk.tensor
                def _(te, j=j):
                    self.c_sg.wait(te, 1, at=ev["wrows"])
                    if j > 0:
                        te.wait_ge(self.s_dve.h, ev[("sin", j - 1)])
                    self.s_pe.inc(te.matmul(out=psW[:, 0:128], lhsT=self.ones1[:, :],
                                            rhs=self.wrow[0:1, 128 * j:128 * j + 128],
                                            start=True, stop=True))
                    ev[("wk1", j)] = self.s_pe.n

                for pc in range(8):
                    @blk.sync
                    def _(sync, j=j, pc=pc):
                        if j == 0 and pc == 0:
                            self.wait_everything(sync)
                        slot = pc % 2
                        if (j * 8 + pc) >= 2:
                            self.s_dve.wait(sync, at=ev.get(("sacc", j, pc - 2),
                                            ev.get(("sacc", j - 1, 6 + pc), 0)))
                        self.c_ld.inc(sync.dma_start(
                            out=self.sjp[:, slot * 1024:slot * 1024 + 1024],
                            in_=self.h_agout[128 * j:128 * (j + 1),
                                             1024 * pc:1024 * (pc + 1)]), slot)
                        ev[("sj", j, pc)] = self.c_ld.val(slot)

                    @blk.vector
                    def _(v, j=j, pc=pc):
                        slot = pc % 2
                        self.c_ld.wait(v, slot, at=ev[("sj", j, pc)])
                        v.wait_ge(self.s_pe.h, ev[("wk1", j)])
                        if j == 0:
                            self.s_dve.inc(v.tensor_tensor(
                                out=self.prev[:, 1024 * pc:1024 * (pc + 1)]
                                    .rearrange("p (a b) -> p a b", a=16),
                                in0=self.sjp[:, slot * 1024:slot * 1024 + 1024]
                                    .rearrange("p (a b) -> p a b", a=16),
                                in1=psW[:, 16 * pc:16 * pc + 16].unsqueeze(2)
                                    .broadcast_to((128, 16, 64)),
                                op=OP.mult))
                        else:
                            self.s_dve.inc(v.tensor_tensor(
                                out=self.sjp[:, slot * 1024:slot * 1024 + 1024]
                                    .rearrange("p (a b) -> p a b", a=16),
                                in0=self.sjp[:, slot * 1024:slot * 1024 + 1024]
                                    .rearrange("p (a b) -> p a b", a=16),
                                in1=psW[:, 16 * pc:16 * pc + 16].unsqueeze(2)
                                    .broadcast_to((128, 16, 64)),
                                op=OP.mult))
                            self.s_dve.wait(v)
                            self.s_dve.inc(v.tensor_add(
                                out=self.prev[:, 1024 * pc:1024 * (pc + 1)],
                                in0=self.prev[:, 1024 * pc:1024 * (pc + 1)],
                                in1=self.sjp[:, slot * 1024:slot * 1024 + 1024]))
                        ev[("sacc", j, pc)] = self.s_dve.n
                        if pc == 7:
                            ev[("sin", j)] = self.s_dve.n
            ev["sin"] = ev[("sacc", 3, 7)]

            for c in range(NCH):
                self._pb_chunk(blk, ev, psB1, psB2, psN, psRq, c)

    def _pb_chunk(self, blk, ev, psB1, psB2, psN, psRq, c):
        nc = self.nc

        @blk.scalar
        def _(act):
            if c == 0:
                act.wait_ge(self.s_dve.h, ev["sin"])
            else:
                act.wait_ge(self.s_dve2.h, ev[("ydone", c - 1)])
            self.s_act.inc(act.activation(
                out=self.evpl[:, 0:CS],
                in_=self.planes[:, T + c * CS:T + (c + 1) * CS], func=AF.Exp))
            ev[("evpl", c)] = self.s_act.n

        @blk.vector
        def _(v):
            if c == 0:
                v.wait_ge(self.s_dve.h, ev["sin"])
            else:
                v.wait_ge(self.s_dve.h, ev[("supd", c - 1, 7)])
                v.wait_ge(self.s_pe.h, ev[("ymm2", c - 1, 63)])
            for q in range(8):
                self.s_dve2.inc(v.tensor_copy(
                    out=self.prunb[:, q * 1024:(q + 1) * 1024],
                    in_=self.prev[:, q * 1024:(q + 1) * 1024]))
            ev[("prunb", c)] = self.s_dve2.n

        for half in range(8):
            @blk.sync
            def _(sync, half=half):
                if c == 0 and half == 0:
                    sync.wait_ge(self.s_dve.h, ev["sin"])
                if ("ydone", c - 1) in ev:
                    sync.wait_ge(self.s_dve2.h, ev[("ydone", c - 1)])
                self.c_ld.inc(sync.dma_start(
                    out=self.yb[:, half * 2048:(half + 1) * 2048]
                        .rearrange("p (k t) -> p k t", k=8),
                    in_=self.h_y1[8 * half:8 * (half + 1), :, c * CS:(c + 1) * CS]
                        .rearrange("k p t -> p k t")), 2)
                self.c_ld.inc(sync.dma_start(
                    out=self.gateb[:, half * 2048:(half + 1) * 2048]
                        .rearrange("p (k t) -> p k t", k=8),
                    in_=self.h_gate[8 * half:8 * (half + 1), :, c * CS:(c + 1) * CS]
                        .rearrange("k p t -> p k t")), 3)
        ev[("reloadall", c)] = (self.c_ld.val(2), self.c_ld.val(3))

        for ib in range(0, 68, 2):
            for j in (ib, ib + 1):
                if j < 64 and j % 8 == 0:
                    q = j // 8
                    @blk.gpsimd
                    def _(gp, q=q):
                        gp.wait_ge(self.s_act.h, ev[("evpl", c)])
                        if ("evk1", c, 8 * q - 2) in ev:
                            gp.wait_ge(self.s_pe.h, ev[("evk1", c, 8 * q - 2)])
                        self.c_sg.inc(gp.dma_start(
                            allow_slow_non_contiguous=True,
                            out=self.evst[0:1, (q % 2) * 4096:(q % 2) * 4096 + 4096]
                                .rearrange("o (h t) -> o h t", h=16),
                            in_=self.evpl[16 * q:16 * (q + 1), 0:CS]), 0)
                        ev[("evst", c, q)] = self.c_sg.val(0)
                if j < 64:
                    self._pb_stage1(blk, ev, psB1, c, j)
            for j in (ib - 4, ib - 3):
                if 0 <= j < 64:
                    self._pb_stage3(blk, ev, psN, c, j)
                    if j % 8 == 7:
                        self._pb_group_norm(blk, ev, psN, psRq, c, j // 8)
            for j in (ib - 2, ib - 1):
                if 0 <= j < 64:
                    self._pb_stage2(blk, ev, psB1, psB2, c, j)
        ev[("ydone", c)] = self.s_dve2.n

        for pc in range(8):
            @blk.sync
            def _(sync, pc=pc):
                sync.wait_ge(self.s_dve2.h, ev[("ydone", c)])
                slot = pc % 2
                if ("supd", c, pc - 2) in ev:
                    sync.wait_ge(self.s_dve.h, ev[("supd", c, pc - 2)])
                self.c_ld.inc(sync.dma_start(
                    out=self.sjp[:, slot * 1024:slot * 1024 + 1024],
                    in_=self.h_states[c][:, 1024 * pc:1024 * (pc + 1)]), slot)
                ev[("sld", c, pc)] = self.c_ld.val(slot)

            @blk.vector
            def _(v, pc=pc):
                slot = pc % 2
                self.c_ld.wait(v, slot, at=ev[("sld", c, pc)])
                self.s_dve.inc(v.tensor_tensor(
                    out=self.prev[:, 1024 * pc:1024 * (pc + 1)]
                        .rearrange("p (a b) -> p a b", a=16),
                    in0=self.prev[:, 1024 * pc:1024 * (pc + 1)]
                        .rearrange("p (a b) -> p a b", a=16),
                    in1=self.Tc_all[:, c * 128 + 16 * pc:c * 128 + 16 * pc + 16]
                        .unsqueeze(2).broadcast_to((128, 16, 64)),
                    op=OP.mult))
                self.s_dve.wait(v)
                self.s_dve.inc(v.tensor_add(
                    out=self.prev[:, 1024 * pc:1024 * (pc + 1)],
                    in0=self.prev[:, 1024 * pc:1024 * (pc + 1)],
                    in1=self.sjp[:, slot * 1024:slot * 1024 + 1024]))
                ev[("supd", c, pc)] = self.s_dve.n

    def _pb_stage1(self, blk, ev, psB1, c, k):
        g = k // 8
        q = k // 8
        sl = k % 2
        b1 = psB1[:, sl * 512:sl * 512 + 512]

        @blk.tensor
        def _(te):
            self.c_sg.wait(te, 0, at=ev[("evst", c, q)])
            if ("ctld", c, k - 2) in ev:
                te.wait_ge(self.s_dve2.h, ev[("ctld", c, k - 2)])
            elif c > 0 and ("ctld", c - 1, 62 + (k % 2)) in ev:
                te.wait_ge(self.s_dve2.h, ev[("ctld", c - 1, 62 + (k % 2))])
            r0 = (2 * k) % 16
            te.matmul(out=b1[:, 0:256], lhsT=self.ones1[:, :],
                      rhs=self.evst[0:1, (q % 2) * 4096 + r0 * CS:
                                    (q % 2) * 4096 + (r0 + 1) * CS],
                      start=True, stop=True)
            self.s_pe.inc(te.matmul(
                out=b1[:, 256:512], lhsT=self.ones1[:, :],
                rhs=self.evst[0:1, (q % 2) * 4096 + (r0 + 1) * CS:
                              (q % 2) * 4096 + (r0 + 2) * CS],
                start=True, stop=True))
            ev[("evk1", c, k)] = self.s_pe.n

        @blk.vector
        def _(v):
            v.wait_ge(self.s_pe.h, ev[("evk1", c, k)])
            v.wait_ge(self.s_dve2.h, ev[("prunb", c)])
            if ("ymm2", c, k - 4) in ev:
                v.wait_ge(self.s_pe.h, ev[("ymm2", c, k - 4)])
            elif c > 0 and ("ymm2", c - 1, _prev4(k)) in ev:
                v.wait_ge(self.s_pe.h, ev[("ymm2", c - 1, _prev4(k))])
            sl4 = k % 4
            for hp in range(2):
                self.s_dve2.inc(v.tensor_tensor(
                    out=self.ctld[:, sl4 * 512 + hp * CS:sl4 * 512 + (hp + 1) * CS],
                    in0=self.C_fm[:, g * T + c * CS:g * T + (c + 1) * CS],
                    in1=b1[:, hp * 256:hp * 256 + 256], op=OP.mult))
            self.s_dve2.wait(v)
            ev[("ctld", c, k)] = self.s_dve2.n

    def _pb_stage2(self, blk, ev, psB1, psB2, c, k):
        sl = k % 2
        b2 = psB2[:, sl * 512:sl * 512 + 256]

        @blk.tensor
        def _(te):
            te.wait_ge(self.s_dve2.h, ev[("ctld", c, k)])
            if ("gev", c, k - 2) in ev:
                te.wait_ge(self.s_dve2.h, ev[("gev", c, k - 2)])
            elif c > 0 and ("gev", c - 1, 62 + (k % 2)) in ev:
                te.wait_ge(self.s_dve2.h, ev[("gev", c - 1, 62 + (k % 2))])
            sl4 = k % 4
            for hp in range(2):
                mm = te.matmul(out=b2[hp * 64:hp * 64 + 64, 0:256],
                               lhsT=self.prunb[:, k * 128 + hp * 64:k * 128 + hp * 64 + 64],
                               rhs=self.ctld[:, sl4 * 512 + hp * CS:sl4 * 512 + (hp + 1) * CS],
                               start=True, stop=True)
            self.s_pe.inc(mm)
            ev[("ymm2", c, k)] = self.s_pe.n

        @blk.vector
        def _(v):
            v.wait_ge(self.s_pe.h, ev[("ymm2", c, k)])
            rl = ev[("reloadall", c)]
            self.c_ld.wait(v, 2, at=rl[0])
            self.c_ld.wait(v, 3, at=rl[1])
            half = k // 8
            kk = k % 8
            ybs = self.yb[:, half * 2048 + kk * CS:half * 2048 + (kk + 1) * CS]
            gbs = self.gateb[:, half * 2048 + kk * CS:half * 2048 + (kk + 1) * CS]
            self.s_dve2.inc(v.tensor_add(out=self.gy[:, sl * CS:(sl + 1) * CS],
                                         in0=ybs, in1=b2))
            self.s_dve2.wait(v)
            self.s_dve2.inc(v.tensor_tensor(
                out=self.gpre[:, (k % 8) * CS:(k % 8 + 1) * CS],
                in0=self.gy[:, sl * CS:(sl + 1) * CS], in1=gbs, op=OP.mult))
            self.s_dve2.wait(v)
            ev[("gev", c, k)] = self.s_dve2.n

        @blk.scalar
        def _(act):
            act.wait_ge(self.s_dve2.h, ev[("gev", c, k)])
            self.s_act2.inc(act.activation(
                out=self.gsq[:, sl * CS:(sl + 1) * CS],
                in_=self.gpre[:, (k % 8) * CS:(k % 8 + 1) * CS], func=AF.Square))
            ev[("gsq", c, k)] = self.s_act2.n

    def _pb_stage3(self, blk, ev, psN, c, k):
        sl = k % 2

        @blk.tensor
        def _(te):
            te.wait_ge(self.s_act2.h, ev[("gsq", c, k)])
            if k % 8 == 0:
                if ("nsq", c, k // 8 - 1) in ev:
                    te.wait_ge(self.s_act.h, ev[("nsq", c, k // 8 - 1)])
                elif c > 0 and ("nsq", c - 1, 7) in ev:
                    te.wait_ge(self.s_act.h, ev[("nsq", c - 1, 7)])
            self.s_pe.inc(te.matmul(out=psN[0:1, 0:256], lhsT=self.onescol[:, :],
                                    rhs=self.gsq[:, sl * CS:(sl + 1) * CS],
                                    start=(k % 8 == 0), stop=(k % 8 == 7)))
            ev[("nmm", c, k)] = self.s_pe.n

    def _pb_group_norm(self, blk, ev, psN, psRq, c, g):
        @blk.scalar
        def _(act):
            act.wait_ge(self.s_pe.h, ev[("nmm", c, 8 * g + 7)])
            self.s_act.inc(act.activation(out=self.nrow[0:1, 0:CS],
                                          in_=psN[0:1, 0:256], func=AF.Sqrt,
                                          scale=np.float32(1.0 / 1024.0),
                                          bias=np.float32(EPS)))
            ev[("nsq", c, g)] = self.s_act.n

        @blk.vector
        def _(v):
            v.wait_ge(self.s_act.h, ev[("nsq", c, g)])
            self.s_dve.inc(v.reciprocal(out=self.nrow[0:1, 0:CS],
                                        in_=self.nrow[0:1, 0:CS]))
            self.s_dve.wait(v)
            self.s_dve.inc(v.tensor_copy(out=self.nrow_bf[0:1, 0:CS],
                                         in_=self.nrow[0:1, 0:CS]))
            ev[("rqrow", c, g)] = self.s_dve.n

        @blk.tensor
        def _(te):
            te.wait_ge(self.s_dve.h, ev[("rqrow", c, g)])
            if ("gnorm", c, g - 1) in ev:
                te.wait_ge(self.s_dve2.h, ev[("gnorm", c, g - 1)])
            elif c > 0:
                te.wait_ge(self.s_dve2.h, ev[("gnorm", c - 1, 7)])
            self.s_pe.inc(te.matmul(
                out=psRq[:, 0:256], lhsT=self.ones1[:, :],
                rhs=self.nrow_bf[0:1, 0:CS],
                start=True, stop=True))
            ev[("rqk1", c, g)] = self.s_pe.n

        @blk.vector
        def _(v):
            v.wait_ge(self.s_pe.h, ev[("rqk1", c, g)])
            self.c_gt.wait(v, 0)
            self.c_gt.wait(v, 1)
            for kk in range(8):
                k = 8 * g + kk
                self.s_dve2.inc(v.scalar_tensor_tensor(
                    out=self.gst[:, kk * CS:(kk + 1) * CS],
                    in0=psRq[:, 0:256], scalar=self.nwrep[:, k:k + 1],
                    in1=self.gpre[:, kk * CS:(kk + 1) * CS],
                    op0=OP.mult, op1=OP.mult))
            self.s_dve2.wait(v)
            ev[("gnorm", c, g)] = self.s_dve2.n

        @blk.sync
        def _(sync):
            sync.wait_ge(self.s_dve2.h, ev[("gnorm", c, g)])
            self.c_gt.inc(sync.dma_start(
                out=self.h_g[8 * g:8 * (g + 1), :, c * CS:(c + 1) * CS]
                    .rearrange("k p t -> p k t"),
                in_=self.gst[:, 0:2048].rearrange("p (k t) -> p k t", k=8)), g % 2)
            ev[("gsp", c, g)] = self.c_gt.val(g % 2)

    def emit_p3a(self):
        nc = self.nc
        ev = {}
        with nc.Block("gl") as blk:
            @blk.sync
            def _(sync):
                self.wait_everything(sync)
                for q in range(4):
                    self.c_ld.inc(sync.dma_start(
                        out=self.g_all[:, q * 16384:(q + 1) * 16384]
                            .rearrange("p (k t) -> p k t", k=16),
                        in_=self.h_g[16 * q:16 * (q + 1)]
                            .rearrange("k p t -> p k t")), q % 2)

        bws = [self.c_w.val(j) for j in range(2)]
        bxs = [self.c_x.val(j) for j in range(2)]
        bsp = [self.c_sp.val(j) for j in range(2)]
        bp2, bd2 = self.s_pe2.n, self.s_dve2.n
        w_done = [bws[dt % 2] + 16 * (dt // 2 + 1) for dt in range(32)]
        x_done = [bxs[dt % 2] + 16 * (dt // 2 + 1) for dt in range(32)]
        sp_done = [bsp[dt % 2] + 16 * (dt // 2 + 1) for dt in range(32)]
        pe_done = [bp2 + (dt + 1) for dt in range(32)]
        ev_done = [bd2 + (dt + 1) for dt in range(32)]

        with (nc.psum_tensor("psO_A", [128, 1024], F32) as psA,
              nc.psum_tensor("psO_B", [128, 1024], F32) as psB,
              nc.Block("p3a") as blk):
            @blk.gpsimd
            def _(gp):
                self.wait_everything(gp)
                for dt in range(32):
                    if dt >= 2:
                        gp.wait_ge(self.s_pe2.h, pe_done[dt - 2])
                    slab = self.wslab3[:, (dt % 2) * 8192:(dt % 2) * 8192 + 8192]
                    self.c_w.inc(gp.dma_start(out=slab, in_=self.i_wout[dt]), dt % 2)

            @blk.sync
            def _(sync):
                for dt in range(32):
                    if dt >= 2:
                        sync.wait_ge(self.s_dve2.h, ev_done[dt - 2])
                    buf = self.xtile[:, (dt % 2) * TH:(dt % 2) * TH + T]
                    self.c_x.inc(sync.dma_start(out=buf,
                                                in_=self.h_r1[dt][:, HALO:TH]), dt % 2)
                    if dt >= 1:
                        sync.wait_ge(self.s_dve2.h, ev_done[dt - 1])
                        eb = self.evt[:, ((dt - 1) % 2) * TH:((dt - 1) % 2) * TH + T]
                        self.c_sp.inc(sync.dma_start(out=self.h_r2[dt - 1],
                                                     in_=eb), (dt - 1) % 2)
                sync.wait_ge(self.s_dve2.h, ev_done[31])
                eb = self.evt[:, (31 % 2) * TH:(31 % 2) * TH + T]
                self.c_sp.inc(sync.dma_start(out=self.h_r2[31], in_=eb), 31 % 2)
                self.c_sp.wait(sync, 0)
                self.c_sp.wait(sync, 1)

            @blk.tensor
            def _(te):
                self.c_ld.wait(te, 0)
                self.c_ld.wait(te, 1)
                for dt in range(32):
                    self.c_w.wait(te, dt % 2, at=w_done[dt])
                    if dt >= 2:
                        te.wait_ge(self.s_dve2.h, ev_done[dt - 2])
                    ps = psA if dt % 2 == 0 else psB
                    slab = self.wslab3[:, (dt % 2) * 8192:(dt % 2) * 8192 + 8192]
                    for f in range(64):
                        lt = slab[:, f * 128:(f + 1) * 128]
                        for (c0, c1) in ((0, 512), (512, 1024)):
                            mm = te.matmul(out=ps[:, c0:c1], lhsT=lt,
                                           rhs=self.g_all[:, f * T + c0:f * T + c1],
                                           start=(f == 0), stop=(f == 63))
                    self.s_pe2.inc(mm)

            @blk.vector
            def _(v):
                for dt in range(32):
                    v.wait_ge(self.s_pe2.h, pe_done[dt])
                    self.c_x.wait(v, dt % 2, at=x_done[dt])
                    if dt >= 2:
                        self.c_sp.wait(v, dt % 2, at=sp_done[dt - 2])
                    ps = psA if dt % 2 == 0 else psB
                    eb = self.evt[:, (dt % 2) * TH:(dt % 2) * TH + T]
                    self.s_dve2.inc(v.tensor_add(
                        out=eb, in0=ps[:, 0:T],
                        in1=self.xtile[:, (dt % 2) * TH:(dt % 2) * TH + T]))

    def build(self):
        self.emit_init()
        self.push("C_fm", [128, 8 * T], BF16)
        self.push("planes", [128, 3 * T], F32)
        self.push("planes_bf", [128, T + 512], BF16)
        self.push("tmcols", [128, 2 * 8 * 128], F32)
        self.push("Tc_all", [128, 4 * 128], F32)
        self.push("runbf", [128, INTER], BF16)
        self.push("xn", [128, 32 * TH], BF16)
        self.xst = TH
        self.push("rows", [1, 3 * TH], F32)
        self.push("rows_bf", [1, 2 * TH], BF16)
        self.push("xtile", [128, 2 * TH], F32)
        self.push("sqb", [128, 2 * TH], BF16)
        self.push("evt", [128, 2 * TH], F32)
        self.push("wslab", [128, 2 * 4224], BF16)
        self.emit_ln("ln1", "x", TH, mask_halo=True)
        self.emit_ffn("ffn1", TH, "x", self.h_r1)
        self.emit_ln("ln2", "r1", TH, mask_halo=True)
        self.pop("wslab", "evt", "sqb", "xtile", "rows_bf", "rows")
        self.emit_barrier()
        self.push("wslab2", [128, 2 * 4096], BF16)
        self.push("B_fm", [128, INTER], BF16)
        self.push("B_tm", [128, 1024], BF16)
        self.push("hs_fm", [128, 2 * T], BF16)
        self.push("hs_tm", [128, 2 * T], BF16)
        self.push("rawS", [128, 4 * 384], BF16)
        self.push("d1b", [128, 2 * 384], BF16)
        self.push("stb", [128, 2 * 384], BF16)
        self.push("y1st", [128, 2 * T], BF16)
        self.push("gatest", [128, 2 * T], BF16)
        self.push("cacc", [128, T], F32)
        self.push("wbb", [128, 4 * 64], BF16)
        self.push("sst", [128, 2 * 128], F32)
        self.push("sscr", [128, 128], F32)
        self.push("sstage", [1, 2 * T], BF16)
        self.emit_p2()
        self.pop("sstage", "sscr", "sst", "wbb", "cacc", "gatest", "y1st",
                 "stb", "d1b", "rawS", "hs_tm", "hs_fm", "B_tm", "B_fm",
                 "wslab2", "xn")
        self.pop("runbf")
        self.emit_barrier()
        self.push("prev", [128, INTER], F32)
        self.push("sjp", [128, 2 * 1024], F32)
        self.push("yb", [128, 64 * CS], BF16)
        self.push("gateb", [128, 64 * CS], BF16)
        self.push("gpre", [128, 8 * CS], BF16)
        self.push("gst", [128, 8 * CS], BF16)
        self.push("gy", [128, 2 * CS], F32)
        self.push("gsq", [128, 2 * CS], BF16)
        self.push("ctld", [128, 4 * 512], BF16)
        self.push("prunb", [128, INTER], BF16)
        self.push("evpl", [128, CS], BF16)
        self.push("evst", [1, 2 * 4096], BF16)
        self.push("nrow", [1, CS], F32)
        self.push("nrow_bf", [1, CS], BF16)
        self.push("wrow", [1, 512], BF16)
        self.emit_pb()
        self.pop("wrow", "nrow_bf", "nrow", "evst", "evpl", "prunb", "ctld", "gsq", "gy", "gst",
                 "gpre", "gateb", "yb", "sjp", "prev")
        self.pop("Tc_all", "tmcols", "planes_bf", "planes", "C_fm")
        self.emit_barrier()
        self.push("g_all", [128, 64 * T], BF16)
        self.push("wslab3", [128, 2 * 8192], BF16)
        self.push("xtile", [128, 2 * TH], F32)
        self.push("evt", [128, 2 * TH], F32)
        self.emit_p3a()
        self.pop("evt", "xtile", "wslab3", "g_all")
        self.emit_barrier()
        self.push("xn3", [128, 32 * T], BF16)
        self.xn = self.xn3
        self.xst = T
        self.push("rows", [1, 3 * TH], F32)
        self.push("rows_bf", [1, 2 * TH], BF16)
        self.push("xtile", [128, 2 * TH], F32)
        self.push("sqb", [128, 2 * TH], BF16)
        self.push("evt", [128, 2 * TH], F32)
        self.push("wslab", [128, 2 * 4224], BF16)
        self.emit_ln("ln3", "r2", T, mask_halo=False)
        self.emit_ffn("ffn2", T, "r2", self.o_y)
        self.finish()
        return self.nc

    def emit_barrier(self):
        self._barid = getattr(self, "_barid", 0) + 1
        with self.nc.Block(f"bar{self._barid}") as blk:
            self.barrier(blk)

    def finish(self):
        with self.nc.Block("fin") as blk:
            self.barrier(blk)


def kernel(**inputs):
    in_maps = make_inputs(inputs)
    kb = KB()
    nc = kb.build()
    res = run_bass_kernel_spmd(nc, in_maps, core_ids=list(range(8)))
    out = np.empty((2, 4096, 4096), np.float32)
    for k in range(8):
        y = res.results[k]["y_fm"].reshape(4096, T)
        out[k // 4, T * (k % 4):T * (k % 4 + 1), :] = y.T
    return out


# version 10
# speedup vs baseline: 1.0533x; 1.0120x over previous
import numpy as np
import ml_dtypes
import concourse.bass as bass
import concourse.mybir as mybir
from concourse.bass_utils import run_bass_kernel_spmd

F32 = mybir.dt.float32
BF16 = mybir.dt.bfloat16
I32 = mybir.dt.int32
AF = mybir.ActivationFunctionType
OP = mybir.AluOpType
AX = mybir.AxisListType.X

D = 4096
T = 1024
HALO = 128
TH = T + HALO
H, P, N, G, CK, CS = 128, 64, 128, 8, 4, 256
INTER = H * P
NCH = T // CS
EPS = 1e-5

N_DT, N_B, N_C, N_HS, N_GATE = 1, 8, 8, 64, 64
WT0 = 8 * 128
N_FT = N_DT + N_B + N_C + N_HS + N_GATE


def _proj_tile_order():
    tiles = [18432 // 128]
    tiles += [16384 // 128 + i for i in range(8)]
    tiles += [17408 // 128 + i for i in range(8)]
    tiles += [8192 // 128 + i for i in range(64)]
    tiles += list(range(64))
    return tiles


def _panels(wt, n_out_tiles, pad_rows=None):
    if pad_rows is not None and wt.shape[0] < pad_rows:
        wt = np.concatenate([wt, np.zeros((pad_rows - wt.shape[0], wt.shape[1]),
                                          wt.dtype)], 0)
    d_in, d_out = wt.shape
    nci = d_in // 128
    a = wt.reshape(nci, 128, n_out_tiles, 128).transpose(2, 1, 0, 3)
    return np.ascontiguousarray(a.reshape(n_out_tiles, 128, nci * 128))


def make_inputs(full):
    x = np.asarray(full["L0_attn_input_states"])
    wproj = np.asarray(full["L0_attn_in_proj_w"])
    conv_w = np.asarray(full["L0_attn_conv_w"]).reshape(10240, CK)
    conv_b = np.asarray(full["L0_attn_conv_b"])
    dt_bias = np.asarray(full["L0_attn_dt_bias"])
    a_log = np.asarray(full["L0_attn_A_log"])
    d_vec = np.asarray(full["L0_attn_D"])
    norm_w = np.asarray(full["L0_attn_norm_w"])
    out_w = np.asarray(full["L0_attn_out_w"])
    wffn = np.asarray(full["L0_ffn_weight"])
    ffn_b = np.asarray(full["L0_ffn_bias"])
    lora_a = np.asarray(full["L0_ffn_lora_A"])
    lora_b = np.asarray(full["L0_ffn_lora_B"])
    lsi = np.asarray(full["L0_ffn_lora_start_indices"]).astype(np.int32)

    lb_t = lora_b.transpose(0, 2, 1).reshape(48, 4096)
    wffn_aug = np.concatenate([wffn.T, lb_t,
                               np.zeros((80, 4096), wffn.dtype)], 0)
    wffn_p = _panels(np.ascontiguousarray(wffn_aug), 32)
    wproj_all = _panels(np.ascontiguousarray(wproj.T), 18560 // 128)
    wproj_p = np.ascontiguousarray(wproj_all[_proj_tile_order()])
    wout_p = _panels(np.ascontiguousarray(out_w.T), 32)
    la_t = np.ascontiguousarray(lora_a.reshape(48, 4096).T)

    shared = {
        "wffn_p": wffn_p, "wproj_p": wproj_p, "wout_p": wout_p, "la_t": la_t,
        "conv_w_cols": np.ascontiguousarray(conv_w.reshape(80, 128, CK)),
        "conv_b_cols": np.ascontiguousarray(conv_b.reshape(80, 128)),
        "ffn_b_cols": np.ascontiguousarray(ffn_b.reshape(32, 128)),
        "d_rep": np.ascontiguousarray(np.repeat(d_vec, P).reshape(64, 128)),
        "nw_rep": np.ascontiguousarray(norm_w.reshape(64, 128)),
        "dtb_row": np.ascontiguousarray(dt_bias.reshape(1, 128)),
        "alog_row": np.ascontiguousarray(a_log.reshape(1, 128)),
        "ident_f32": np.eye(128, dtype=np.float32),
        "lsi_rep": np.ascontiguousarray(np.broadcast_to(lsi[None, :], (128, 4))),
    }
    in_maps = []
    for k in range(8):
        b, p = k // 4, k % 4
        t0 = T * p
        xs = x[b, t0:t0 + T, :]
        halo = (np.zeros((HALO, D), np.float32) if p == 0
                else x[b, t0 - HALO:t0, :])
        meta = np.zeros((128, 8), np.float32)
        meta[:, 0] = float(b)
        meta[:, 1] = 0.0 if p == 0 else 1.0
        for j in range(4):
            meta[:, 2 + j] = 1.0 if j < p else 0.0
        m = dict(shared)
        m["x_fm"] = np.ascontiguousarray(np.concatenate([halo, xs], 0).T)
        m["meta"] = np.ascontiguousarray(meta)
        in_maps.append(m)
    return in_maps



def _prev4(k):
    return [x for x in range(60, 64) if x % 4 == k % 4][0]


def _prev8(k):
    return 56 + (k % 8)

class Sem:
    def __init__(self, nc, name, step=1):
        self.h = nc.alloc_semaphore(name)
        self.step = step
        self.n = 0

    def inc(self, instr, k=1):
        self.n += self.step * k
        return instr.then_inc(self.h, self.step * k)

    def wait(self, eng, at=None):
        eng.wait_ge(self.h, self.n if at is None else at)


class Chan:

    def __init__(self, nc, name, nslots, step=16):
        self.sems = [nc.alloc_semaphore(f"{name}{j}") for j in range(nslots)]
        self.n = [0] * nslots
        self.step = step

    def inc(self, instr, slot):
        self.n[slot] += self.step
        return instr.then_inc(self.sems[slot], self.step)

    def val(self, slot):
        return self.n[slot]

    def wait(self, eng, slot, at=None):
        eng.wait_ge(self.sems[slot], self.n[slot] if at is None else at)

    def wait_all(self, eng):
        for j, s in enumerate(self.sems):
            if self.n[j]:
                eng.wait_ge(s, self.n[j])


class KB:
    def __init__(self, taps=()):
        self.taps = set(taps)
        nc = self.nc = bass.Bass("TRN2", target_bir_lowering=False)
        dram = nc.dram_tensor
        self.i_x = dram("x_fm", [D, TH], F32, kind="ExternalInput").ap()
        self.i_wffn = dram("wffn_p", [32, 128, 4224], F32, kind="ExternalInput").ap()
        self.i_wproj = dram("wproj_p", [N_FT, 128, 4096], F32, kind="ExternalInput").ap()
        self.i_wout = dram("wout_p", [32, 128, 8192], F32, kind="ExternalInput").ap()
        self.i_la = dram("la_t", [D, 48], F32, kind="ExternalInput").ap()
        self.i_convw = dram("conv_w_cols", [80, 128, CK], F32, kind="ExternalInput").ap()
        self.i_convb = dram("conv_b_cols", [80, 128], F32, kind="ExternalInput").ap()
        self.i_fb = dram("ffn_b_cols", [32, 128], F32, kind="ExternalInput").ap()
        self.i_drep = dram("d_rep", [64, 128], F32, kind="ExternalInput").ap()
        self.i_nwrep = dram("nw_rep", [64, 128], F32, kind="ExternalInput").ap()
        self.i_dtb = dram("dtb_row", [1, 128], F32, kind="ExternalInput").ap()
        self.i_alog = dram("alog_row", [1, 128], F32, kind="ExternalInput").ap()
        self.i_idf32 = dram("ident_f32", [128, 128], F32, kind="ExternalInput").ap()
        self.i_lsi = dram("lsi_rep", [128, 4], I32, kind="ExternalInput").ap()
        self.i_meta = dram("meta", [128, 8], F32, kind="ExternalInput").ap()
        self.o_y = dram("y_fm", [32, 128, T], F32, kind="ExternalOutput").ap()

        self.h_r1 = dram("r1_sp", [32, 128, TH], F32).ap()
        self.h_r2 = dram("r2_sp", [32, 128, T], F32).ap()
        self.h_gate = dram("gate_sp", [64, 128, T], BF16).ap()
        self.h_y1 = dram("y1_sp", [64, 128, T], BF16).ap()
        self.h_states = dram("states_sp", [NCH, 128, INTER], F32).ap()
        self.h_g = dram("g_sp", [64, 128, T], BF16).ap()
        self.h_agin = dram("ag_in", [128, INTER + 1], F32).ap()
        self.h_agout = dram("ag_out", [512, INTER + 1], F32).ap()

        sb = self._sb_perm = {}
        def L(name, shape, dt):
            t = nc.alloc_sbuf_tensor(name, shape, dt, side="left")
            sb[name] = t
            return t
        self.consts = L("consts", [128, 16], F32)
        self.convw = L("convw", [128, 80 * CK], F32)
        self.convb = L("convb", [128, 80], F32)
        self.drep = L("drep", [128, 64], F32)
        self.nwrep = L("nwrep", [128, 64], F32)
        self.hbcol = L("hbcol", [128, 32], F32)
        self.idf32 = L("idf32", [128, 128], F32)
        self.la = L("la", [128, 32 * 48], BF16)
        self.xa = L("xa", [128, TH], BF16)
        self.scol = L("scol", [48, 1], F32)
        self.ones1 = L("ones1", [1, 128], BF16)
        self.onescol = L("onescol", [128, 1], BF16)
        self.masks = L("masks", [128, 2 * CS], F32)
        self.scr = L("scr", [128, 64], F32)

        self._guards = {}

        self.s_in = Sem(nc, "s_in", 16)
        self.s_ing = Sem(nc, "s_ing", 16)
        self.c_w = Chan(nc, "c_w", 2)
        self.c_x = Chan(nc, "c_x", 2)
        self.c_sp = Chan(nc, "c_sp", 2)
        self.c_ld = Chan(nc, "c_ld", 4)
        self.c_tp = Chan(nc, "c_tp", 2)
        self.c_st = Chan(nc, "c_st", 2)
        self.c_y1 = Chan(nc, "c_y1", 2)
        self.c_gt = Chan(nc, "c_gt", 2)
        self.c_ss = Chan(nc, "c_ss", 2)
        self.c_bt = Chan(nc, "c_bt", 2)
        self.c_ag = Chan(nc, "c_ag", 1)
        self.c_sg = Chan(nc, "c_sg", 2)
        self.s_pe = Sem(nc, "s_pe", 1)
        self.s_pe2 = Sem(nc, "s_pe2", 1)
        self.s_dve = Sem(nc, "s_dve", 1)
        self.s_dve2 = Sem(nc, "s_dve2", 1)
        self.s_act = Sem(nc, "s_act", 1)
        self.s_act2 = Sem(nc, "s_act2", 1)
        self.s_gp = Sem(nc, "s_gp", 1)
        self.s_cc = Sem(nc, "s_cc", 1)
        self.all_sems = [self.s_in, self.s_ing, self.s_pe, self.s_pe2,
                         self.s_dve, self.s_dve2, self.s_act, self.s_act2,
                         self.s_gp, self.s_cc]
        self.all_chans = [self.c_w, self.c_x, self.c_sp, self.c_ld, self.c_tp,
                          self.c_st, self.c_y1, self.c_gt, self.c_ss, self.c_bt,
                          self.c_ag, self.c_sg]
        self.dbg = {}

    def push(self, name, shape, dt):
        self._pushid = getattr(self, "_pushid", 0) + 1
        g = self.nc.sbuf_tensor(f"{name}_{self._pushid}", shape, dt, side="right")
        t = g.__enter__()
        self._guards[name] = g
        setattr(self, name, t)
        return t

    def pop(self, *names):
        for name in names:
            self._guards.pop(name).__exit__(None, None, None)
            delattr(self, name)

    def tap(self, name, shape, dtype=F32):
        if name not in self.taps:
            return None
        ap = self.nc.dram_tensor("dbg_" + name, shape, dtype, kind="ExternalOutput").ap()
        self.dbg[name] = ap
        return ap

    def barrier(self, blk, engines=("sync", "vector", "scalar", "tensor", "gpsimd")):
        for en in engines:
            @getattr(blk, en)
            def _(eng, _en=en):
                for s in self.all_sems:
                    if s.n:
                        eng.wait_ge(s.h, s.n)
                for c in self.all_chans:
                    c.wait_all(eng)

    def wait_everything(self, eng):
        for s in self.all_sems:
            if s.n:
                eng.wait_ge(s.h, s.n)
        for c in self.all_chans:
            c.wait_all(eng)

    def emit_init(self):
        nc = self.nc
        with nc.Block("init") as blk:
            @blk.sync
            def _(sync):
                s = self.s_in
                s.inc(sync.dma_start(out=self.convw[:, :].rearrange("p (t k) -> p t k", t=80),
                                     in_=self.i_convw.rearrange("t p k -> p t k")))
                s.inc(sync.dma_start(allow_slow_non_contiguous=True, out=self.convb[:, :],
                                     in_=self.i_convb.rearrange("t p -> p t")))
                s.inc(sync.dma_start(allow_slow_non_contiguous=True, out=self.drep[:, :],
                                     in_=self.i_drep.rearrange("t p -> p t")))
                s.inc(sync.dma_start(allow_slow_non_contiguous=True, out=self.nwrep[:, :],
                                     in_=self.i_nwrep.rearrange("t p -> p t")))
                s.inc(sync.dma_start(allow_slow_non_contiguous=True, out=self.hbcol[:, :],
                                     in_=self.i_fb.rearrange("t p -> p t")))
                s.inc(sync.dma_start(out=self.idf32[:, :], in_=self.i_idf32[:, :]))
                s.inc(sync.dma_start(allow_slow_non_contiguous=True, out=self.consts[:, 0:1],
                                     in_=self.i_dtb.rearrange("o p -> p o")))
                s.inc(sync.dma_start(allow_slow_non_contiguous=True, out=self.consts[:, 1:2],
                                     in_=self.i_alog.rearrange("o p -> p o")))
                s.inc(sync.dma_start(out=self.consts[:, 3:11], in_=self.i_meta[:, :]))

            @blk.gpsimd
            def _(gp):
                s = self.s_ing
                s.inc(gp.dma_start(out=self.scr[:, 0:4].bitcast(I32), in_=self.i_lsi[:, :]))
                s.inc(gp.dma_start(
                    out=self.la[:, :].rearrange("p (c r) -> p c r", c=32),
                    in_=self.i_la.rearrange("(c p) r -> p c r", p=128)))
                self.s_gp.inc(gp.iota(self.masks[:, 0:CS], [[1, CS]],
                                      channel_multiplier=0,
                                      allow_small_or_imprecise_dtypes=True))
                self.s_gp.inc(gp.iota(self.scr[:, 16:17], [[0, 1]],
                                      channel_multiplier=1,
                                      allow_small_or_imprecise_dtypes=True))

            @blk.vector
            def _(v):
                self.s_in.wait(v)
                self.s_ing.wait(v)
                self.s_gp.wait(v)
                v.memset(self.ones1[:, :], 1.0)
                v.memset(self.onescol[:, :], 1.0)
                v.memset(self.xa[:, :], 0.0)
                self.s_dve.inc(v.tensor_copy(out=self.scr[:, 4:8],
                                             in_=self.scr[:, 0:4].bitcast(I32)))
                self.s_dve.inc(v.tensor_scalar(out=self.masks[:, CS:2 * CS],
                                               in0=self.masks[:, 0:CS],
                                               scalar1=self.scr[:, 16:17], scalar2=128.0,
                                               op0=OP.subtract, op1=OP.is_equal))
                self.s_dve.wait(v)
                self.s_dve.inc(v.tensor_scalar(out=self.masks[:, 0:CS],
                                               in0=self.masks[:, 0:CS],
                                               scalar1=self.scr[:, 16:17], scalar2=0.0,
                                               op0=OP.subtract, op1=OP.is_equal))
                self.s_dve.inc(v.tensor_scalar(out=self.scr[:, 8:12], in0=self.scr[:, 4:8],
                                               scalar1=-1.0, scalar2=self.consts[:, 3:4],
                                               op0=OP.mult, op1=OP.add))
                self.s_dve.wait(v)
                self.s_dve.inc(v.tensor_scalar(out=self.scr[:, 8:12], in0=self.scr[:, 8:12],
                                               scalar1=1.0, scalar2=0.0,
                                               op0=OP.add, op1=OP.max))
                self.s_dve.wait(v)
                self.s_dve.inc(v.tensor_scalar(out=self.scr[:, 8:12], in0=self.scr[:, 8:12],
                                               scalar1=1.0, scalar2=None, op0=OP.min))
                self.s_dve.wait(v)
                self.s_dve.inc(v.reduce_sum(out=self.scr[:, 12:13], in_=self.scr[:, 8:12],
                                            axis=AX))
                self.s_dve.wait(v)
                self.s_dve.inc(v.tensor_scalar(out=self.scr[:, 13:14], in0=self.scr[:, 12:13],
                                               scalar1=-1.0, scalar2=None, op0=OP.add))
                self.s_dve.wait(v)
                self.s_dve.inc(v.tensor_scalar(out=self.scr[:, 14:15], in0=self.scr[:, 13:14],
                                               scalar1=0.0, scalar2=1.0,
                                               op0=OP.max, op1=OP.min))
                self.s_dve.inc(v.tensor_scalar(out=self.scr[:, 15:16], in0=self.scr[:, 13:14],
                                               scalar1=-1.0, scalar2=0.0,
                                               op0=OP.add, op1=OP.max))
                self.s_dve.wait(v)
                self.s_dve.inc(v.tensor_scalar(out=self.scr[:, 15:16], in0=self.scr[:, 15:16],
                                               scalar1=2.0, scalar2=None, op0=OP.min))
                v.memset(self.scr[:, 24:25], 0.0)
                v.memset(self.scr[:, 25:26], 1.0)
                self.s_dve.inc(v.memset(self.scr[:, 26:27], 2.0))
                self.s_dve.wait(v)
                self.s_dve.inc(v.tensor_scalar(out=self.scr[:, 27:30], in0=self.scr[:, 24:27],
                                               scalar1=-1.0, scalar2=self.scr[:, 15:16],
                                               op0=OP.mult, op1=OP.add))
                self.s_dve.wait(v)
                self.s_dve.inc(v.tensor_tensor(out=self.scr[:, 27:30], in0=self.scr[:, 27:30],
                                               in1=self.scr[:, 27:30], op=OP.mult))
                self.s_dve.wait(v)
                self.s_dve.inc(v.tensor_scalar(out=self.scr[:, 27:30], in0=self.scr[:, 27:30],
                                               scalar1=-1.0, scalar2=1.0,
                                               op0=OP.mult, op1=OP.add))
                self.s_dve.wait(v)
                self.s_dve.inc(v.tensor_scalar(out=self.scr[:, 27:30], in0=self.scr[:, 27:30],
                                               scalar1=0.0, scalar2=self.scr[:, 14:15],
                                               op0=OP.max, op1=OP.mult))
                self.s_dve.wait(v)
                self.s_dve.inc(v.tensor_scalar(out=self.hbcol[:, :], in0=self.hbcol[:, :],
                                               scalar1=0.5, scalar2=None, op0=OP.mult))

            @blk.sync
            def _(sync):
                self.s_dve.wait(sync)
                for kk in range(3):
                    self.s_in.inc(sync.dma_start(
                        allow_slow_non_contiguous=True,
                        out=self.scol[16 * kk:16 * (kk + 1), 0:1],
                        in_=self.scr[16 * kk:16 * (kk + 1), 27 + kk:28 + kk]))

            @blk.scalar
            def _(act):
                self.s_in.wait(act)
                self.s_act.inc(act.activation(out=self.consts[:, 2:3],
                                              in_=self.consts[:, 1:2], func=AF.Exp))

            @blk.vector
            def _(v):
                self.s_act.wait(v)
                self.s_dve.inc(v.tensor_scalar(out=self.consts[:, 2:3],
                                               in0=self.consts[:, 2:3],
                                               scalar1=-1.0, scalar2=None, op0=OP.mult))

    def emit_ln(self, tag, src_kind, ncols, mask_halo):
        nc = self.nc
        src = {"x": None, "r1": self.h_r1, "r2": self.h_r2}[src_kind]
        n_mm = (ncols + 511) // 512
        inv_d = 1.0 / D
        xst = self.xst

        bxs = [self.c_x.val(j) for j in range(2)]
        bd, ba = self.s_dve.n, self.s_act.n
        x_done = [bxs[dt % 2] + 16 * (dt // 2 + 1) for dt in range(32)]
        cast_done = [bd + (dt + 1) for dt in range(32)]
        sq_done = [ba + (dt + 1) for dt in range(32)]
        bp = self.s_pe.n

        with (nc.psum_tensor("ps_sum_" + tag, [1, 1536], F32) as ps_sum,
              nc.psum_tensor("ps_sq_" + tag, [1, 1536], F32) as ps_sq,
              nc.Block("lnA_" + tag) as blk):
            @blk.sync
            def _(sync):
                for dt in range(32):
                    if dt >= 2:
                        sync.wait_ge(self.s_dve.h, cast_done[dt - 2])
                    buf = self.xtile[:, (dt % 2) * TH:(dt % 2) * TH + ncols]
                    if src_kind == "x":
                        ap = self.i_x[128 * dt:128 * (dt + 1), 0:ncols]
                    else:
                        ap = src[dt][:, 0:ncols]
                    self.c_x.inc(sync.dma_start(out=buf, in_=ap), dt % 2)

            @blk.vector
            def _(v):
                for dt in range(32):
                    self.c_x.wait(v, dt % 2, at=x_done[dt])
                    buf = self.xtile[:, (dt % 2) * TH:(dt % 2) * TH + ncols]
                    self.s_dve.inc(v.tensor_copy(
                        out=self.xn[:, dt * xst:dt * xst + ncols], in_=buf))

            @blk.scalar
            def _(act):
                for dt in range(32):
                    act.wait_ge(self.s_dve.h, cast_done[dt])
                    if dt >= 2:
                        act.wait_ge(self.s_pe.h, bp + dt - 1)
                    sq = self.sqb[:, (dt % 2) * TH:(dt % 2) * TH + ncols]
                    self.s_act.inc(act.activation(
                        out=sq, in_=self.xn[:, dt * xst:dt * xst + ncols],
                        func=AF.Square))

            @blk.tensor
            def _(te):
                for dt in range(32):
                    te.wait_ge(self.s_act.h, sq_done[dt])
                    xb = self.xn[:, dt * xst:dt * xst + ncols]
                    sq = self.sqb[:, (dt % 2) * TH:(dt % 2) * TH + ncols]
                    for j in range(n_mm):
                        c0 = 512 * j
                        c1 = min(512 * (j + 1), ncols)
                        te.matmul(out=ps_sum[0:1, c0:c1], lhsT=self.onescol[:, :],
                                  rhs=xb[:, c0:c1], start=(dt == 0), stop=(dt == 31))
                        mm = te.matmul(out=ps_sq[0:1, c0:c1], lhsT=self.onescol[:, :],
                                       rhs=sq[:, c0:c1], start=(dt == 0), stop=(dt == 31))
                    self.s_pe.inc(mm)

            @blk.scalar
            def _(act):
                self.s_pe.wait(act)
                self.s_act.inc(act.activation(out=self.rows[0:1, 0:ncols],
                                              in_=ps_sum[0:1, 0:ncols],
                                              func=AF.Copy, scale=np.float32(inv_d)))
                act.wait_ge(self.s_act.h, self.s_act.n)
                self.s_act.inc(act.activation(out=self.rows[0:1, TH:TH + ncols],
                                              in_=ps_sq[0:1, 0:ncols],
                                              func=AF.Copy, scale=np.float32(inv_d)))

            @blk.vector
            def _(v):
                self.s_act.wait(v)
                m = self.rows[0:1, 0:ncols]
                e2 = self.rows[0:1, TH:TH + ncols]
                c = self.rows[0:1, 2 * TH:2 * TH + ncols]
                self.s_dve.inc(v.tensor_tensor(out=c, in0=m, in1=m, op=OP.mult))
                self.s_dve.wait(v)
                self.s_dve.inc(v.tensor_tensor(out=e2, in0=e2, in1=c, op=OP.subtract))

            @blk.scalar
            def _(act):
                self.s_dve.wait(act)
                self.s_act.inc(act.activation(out=self.rows[0:1, 2 * TH:2 * TH + ncols],
                                              in_=self.rows[0:1, TH:TH + ncols],
                                              func=AF.Sqrt, bias=np.float32(EPS)))

            @blk.vector
            def _(v):
                self.s_act.wait(v)
                rs = self.rows[0:1, 2 * TH:2 * TH + ncols]
                m = self.rows[0:1, 0:ncols]
                self.s_dve.inc(v.reciprocal(out=rs, in_=rs))
                self.s_dve.wait(v)
                self.s_dve.inc(v.tensor_copy(out=self.rows_bf[0:1, 0:ncols], in_=rs))
                self.s_dve.inc(v.tensor_tensor(out=m, in0=m, in1=rs, op=OP.mult))
                self.s_dve.wait(v)
                self.s_dve.inc(v.tensor_scalar(out=m, in0=m, scalar1=-1.0,
                                               scalar2=None, op0=OP.mult))
                self.s_dve.wait(v)
                self.s_dve.inc(v.tensor_copy(out=self.rows_bf[0:1, TH:TH + ncols],
                                             in_=m))

        with (nc.psum_tensor("ps_rs_" + tag, [128, 1536], F32) as ps_rs,
              nc.psum_tensor("ps_mrs_" + tag, [128, 1536], F32) as ps_mrs,
              nc.Block("lnB_" + tag) as blk):
            @blk.tensor
            def _(te):
                self.s_dve.wait(te)
                for j in range(n_mm):
                    c0, c1 = 512 * j, min(512 * (j + 1), ncols)
                    te.matmul(out=ps_rs[:, c0:c1], lhsT=self.ones1[:, :],
                              rhs=self.rows_bf[0:1, c0:c1], start=True, stop=True)
                    mm = te.matmul(out=ps_mrs[:, c0:c1], lhsT=self.ones1[:, :],
                                   rhs=self.rows_bf[0:1, TH + c0:TH + c1],
                                   start=True, stop=True)
                self.s_pe.inc(mm)

            @blk.vector
            def _(v):
                self.s_pe.wait(v)
                for dt in range(32):
                    xb = self.xn[:, dt * xst:dt * xst + ncols]
                    tmp = self.evt[:, (dt % 2) * TH:(dt % 2) * TH + ncols]
                    self.s_dve.inc(v.tensor_tensor(out=tmp, in0=xb,
                                                   in1=ps_rs[:, 0:ncols], op=OP.mult))
                    self.s_dve.wait(v)
                    if mask_halo:
                        self.s_dve.inc(v.tensor_tensor(out=tmp, in0=tmp,
                                                       in1=ps_mrs[:, 0:ncols], op=OP.add))
                        self.s_dve.wait(v)
                        self.s_dve.inc(v.tensor_scalar(
                            out=self.xn[:, dt * xst:dt * xst + HALO], in0=tmp[:, 0:HALO],
                            scalar1=self.consts[:, 4:5], scalar2=None, op0=OP.mult))
                        self.s_dve.inc(v.tensor_copy(
                            out=self.xn[:, dt * xst + HALO:dt * xst + ncols],
                            in_=tmp[:, HALO:ncols]))
                        self.s_dve.wait(v)
                    else:
                        self.s_dve.inc(v.tensor_tensor(out=xb, in0=tmp,
                                                       in1=ps_mrs[:, 0:ncols], op=OP.add))
                        self.s_dve.wait(v)

    def emit_ffn(self, tag, ncols, res_src, dst):
        nc = self.nc
        n_mm = (ncols + 511) // 512
        xst = self.xst

        with (nc.psum_tensor("ps_lo_" + tag, [48, 1536], F32) as ps_lo,
              nc.Block("lora_" + tag) as blk):
            @blk.tensor
            def _(te):
                self.wait_everything(te)
                for d in range(32):
                    lt = self.la[:, d * 48:(d + 1) * 48]
                    for j in range(n_mm):
                        c0, c1 = 512 * j, min(512 * (j + 1), ncols)
                        mm = te.matmul(out=ps_lo[0:48, c0:c1], lhsT=lt,
                                       rhs=self.xn[:, d * xst + c0:d * xst + c1],
                                       start=(d == 0), stop=(d == 31))
                self.s_pe.inc(mm)

            @blk.vector
            def _(v):
                self.s_pe.wait(v)
                self.s_dve.inc(v.tensor_scalar(out=self.xa[0:48, 0:ncols],
                                               in0=ps_lo[0:48, 0:ncols],
                                               scalar1=self.scol[:, 0:1], scalar2=None,
                                               op0=OP.mult))

        bws = [self.c_w.val(j) for j in range(2)]
        bxs = [self.c_x.val(j) for j in range(2)]
        bsp = [self.c_sp.val(j) for j in range(2)]
        bp2, bd2 = self.s_pe2.n, self.s_dve2.n
        w_done = [bws[dt % 2] + 16 * (dt // 2 + 1) for dt in range(32)]
        x_done = [bxs[dt % 2] + 16 * (dt // 2 + 1) for dt in range(32)]
        sp_done = [bsp[dt % 2] + 16 * (dt // 2 + 1) for dt in range(32)]
        pe_done = [bp2 + (dt + 1) for dt in range(32)]
        ev_done = [bd2 + 2 * (dt + 1) for dt in range(32)]

        with (nc.psum_tensor("ps_mA_" + tag, [128, 1536], F32) as psA,
              nc.psum_tensor("ps_mB_" + tag, [128, 1536], F32) as psB,
              nc.Block("ffn_" + tag) as blk):
            @blk.gpsimd
            def _(gp):
                for dt in range(32):
                    if dt >= 2:
                        gp.wait_ge(self.s_pe2.h, pe_done[dt - 2])
                    slab = self.wslab[:, (dt % 2) * 4224:(dt % 2) * 4224 + 4224]
                    self.c_w.inc(gp.dma_start(out=slab, in_=self.i_wffn[dt]), dt % 2)

            @blk.sync
            def _(sync):
                for dt in range(32):
                    if dt >= 2:
                        sync.wait_ge(self.s_dve2.h, ev_done[dt - 2])
                    buf = self.xtile[:, (dt % 2) * TH:(dt % 2) * TH + ncols]
                    if res_src == "x":
                        ap = self.i_x[128 * dt:128 * (dt + 1), 0:ncols]
                    else:
                        ap = self.h_r2[dt][:, 0:ncols]
                    self.c_x.inc(sync.dma_start(out=buf, in_=ap), dt % 2)
                    if dt >= 1:
                        sync.wait_ge(self.s_dve2.h, ev_done[dt - 1])
                        eb = self.evt[:, ((dt - 1) % 2) * TH:((dt - 1) % 2) * TH + ncols]
                        self.c_sp.inc(sync.dma_start(out=dst[dt - 1][:, 0:ncols],
                                                     in_=eb), (dt - 1) % 2)
                sync.wait_ge(self.s_dve2.h, ev_done[31])
                eb = self.evt[:, (31 % 2) * TH:(31 % 2) * TH + ncols]
                self.c_sp.inc(sync.dma_start(out=dst[31][:, 0:ncols], in_=eb), 31 % 2)
                self.c_sp.wait(sync, 0)
                self.c_sp.wait(sync, 1)

            @blk.tensor
            def _(te):
                self.s_dve.wait(te)
                for dt in range(32):
                    self.c_w.wait(te, dt % 2, at=w_done[dt])
                    if dt >= 2:
                        te.wait_ge(self.s_dve2.h, ev_done[dt - 2])
                    ps = psA if dt % 2 == 0 else psB
                    slab = self.wslab[:, (dt % 2) * 4224:(dt % 2) * 4224 + 4224]
                    for d in range(33):
                        lt = slab[:, d * 128:(d + 1) * 128]
                        rhs_base = (self.xn[:, d * xst:d * xst + ncols] if d < 32
                                    else self.xa[:, 0:ncols])
                        for j in range(n_mm):
                            c0, c1 = 512 * j, min(512 * (j + 1), ncols)
                            mm = te.matmul(out=ps[:, c0:c1], lhsT=lt,
                                           rhs=rhs_base[:, c0:c1],
                                           start=(d == 0), stop=(d == 32))
                    self.s_pe2.inc(mm)

            @blk.vector
            def _(v):
                for dt in range(32):
                    v.wait_ge(self.s_pe2.h, pe_done[dt])
                    self.c_x.wait(v, dt % 2, at=x_done[dt])
                    if dt >= 2:
                        self.c_sp.wait(v, dt % 2, at=sp_done[dt - 2])
                    ps = psA if dt % 2 == 0 else psB
                    eb = self.evt[:, (dt % 2) * TH:(dt % 2) * TH + ncols]
                    self.s_dve2.inc(v.tensor_scalar(out=eb, in0=ps[:, 0:ncols],
                                                    scalar1=0.5,
                                                    scalar2=self.hbcol[:, dt:dt + 1],
                                                    op0=OP.mult, op1=OP.add))
                    self.s_dve2.wait(v)
                    self.s_dve2.inc(v.tensor_tensor(
                        out=eb, in0=eb,
                        in1=self.xtile[:, (dt % 2) * TH:(dt % 2) * TH + ncols],
                        op=OP.add))

    def emit_p2(self):
        nc = self.nc
        ev = {}
        DT0, CU0, SC0 = 0, T, 2 * T

        def cw_col(ct, kk):
            return self.convw[:, ct * CK + kk:ct * CK + kk + 1]

        with (nc.psum_tensor("psP_A", [128, 1536], F32) as psPA,
              nc.psum_tensor("psP_B", [128, 1536], F32) as psPB,
              nc.psum_tensor("psS1", [128, 512], F32) as psS1,
              nc.psum_tensor("psS2", [128, 512], F32) as psS2,
              nc.Block("p2") as blk):

            self.barrier(blk)

            @blk.vector
            def _(v):
                self.s_dve2.inc(v.memset(self.runbf[:, :], 0.0))

            def gp_prefetch(ft):
                if ft >= N_FT:
                    return
                @blk.gpsimd
                def _(gp):
                    if ft >= 2:
                        gp.wait_ge(self.s_pe2.h, ev[("proj", ft - 2)])
                        gp.wait_ge(self.s_dve2.h, ev[("psfree", ft - 2)])
                    slab = self.wslab2[:, (ft % 2) * 4096:(ft % 2) * 4096 + 4096]
                    self.c_w.inc(gp.dma_start(out=slab, in_=self.i_wproj[ft]), ft % 2)
                    ev[("slab", ft)] = self.c_w.val(ft % 2)

            def te_proj(ft):
                @blk.tensor
                def _(te):
                    self.c_w.wait(te, ft % 2, at=ev[("slab", ft)])
                    if ft >= 2:
                        te.wait_ge(self.s_dve2.h, ev[("psfree", ft - 2)])
                        te.wait_ge(self.s_act2.h, ev[("psfree_a", ft - 2)])
                    ps = psPA if ft % 2 == 0 else psPB
                    slab = self.wslab2[:, (ft % 2) * 4096:(ft % 2) * 4096 + 4096]
                    for d in range(32):
                        lt = slab[:, d * 128:(d + 1) * 128]
                        for (c0, c1) in ((0, 512), (512, 1024), (1024, TH)):
                            mm = te.matmul(out=ps[:, c0:c1], lhsT=lt,
                                           rhs=self.xn[:, d * TH + c0:d * TH + c1],
                                           start=(d == 0), stop=(d == 31))
                    self.s_pe2.inc(mm)
                    ev[("proj", ft)] = self.s_pe2.n

            def dve_conv(ft, ct):
                ps = psPA if ft % 2 == 0 else psPB
                @blk.vector
                def _(v):
                    cbuf = (self.cacc[:, 0:T] if ft % 2 == 0
                            else self.planes[:, SC0:SC0 + T])
                    v.wait_ge(self.s_pe2.h, ev[("proj", ft)])
                    if ft == 1:
                        v.wait_ge(self.s_dve.h, ev[("tmev", 7)])
                        v.wait_ge(self.s_dve.h, ev[("wglog", )] if False else ev["wglog"])
                    if ("siluref", ft - 2) in ev:
                        v.wait_ge(self.s_act2.h, ev[("siluref", ft - 2)])
                    self.s_dve2.inc(v.tensor_scalar(
                        out=cbuf, in0=ps[:, 125:125 + T],
                        scalar1=cw_col(ct, 0), scalar2=None, op0=OP.mult))
                    for kk in range(1, CK):
                        self.s_dve2.wait(v)
                        self.s_dve2.inc(v.scalar_tensor_tensor(
                            out=cbuf, in0=ps[:, 125 + kk:125 + kk + T],
                            scalar=cw_col(ct, kk), in1=cbuf,
                            op0=OP.mult, op1=OP.add))
                    ev[("psfree", ft)] = self.s_dve2.n
                    ev[("conv", ft)] = self.s_dve2.n

            def act_silu(ft, ct, dst_ap, hs_k=None):
                @blk.scalar
                def _(act):
                    act.wait_ge(self.s_dve2.h, ev[("conv", ft)])
                    if hs_k is not None and ("y1ev", hs_k - 2, NCH - 1) in ev:
                        act.wait_ge(self.s_dve2.h, ev[("y1ev", hs_k - 2, NCH - 1)])
                    cbuf = (self.cacc[:, 0:T] if ft % 2 == 0
                            else self.planes[:, SC0:SC0 + T])
                    self.s_act2.inc(act.activation(
                        out=dst_ap, in_=cbuf,
                        func=AF.Silu, bias=self.convb[:, ct:ct + 1]))
                    ev[("silu", ft)] = self.s_act2.n
                    ev[("siluref", ft)] = self.s_act2.n
                    ev[("psfree_a", ft)] = self.s_act2.n

            gp_prefetch(0)
            gp_prefetch(1)
            te_proj(0)

            @blk.vector
            def _(v):
                v.wait_ge(self.s_pe2.h, ev[("proj", 0)])
                self.s_dve2.inc(v.tensor_copy(out=self.planes[:, DT0:DT0 + T],
                                              in_=psPA[:, 128:TH]))
                ev[("psfree", 0)] = self.s_dve2.n
                ev[("dtraw", 0)] = self.s_dve2.n
            ev[("psfree_a", 0)] = 0

            @blk.scalar
            def _(act):
                act.wait_ge(self.s_dve2.h, ev[("dtraw", 0)])
                self.s_act.inc(act.activation(out=self.planes[:, SC0:SC0 + T],
                                              in_=self.planes[:, DT0:DT0 + T],
                                              func=AF.Abs, bias=self.consts[:, 0:1]))
                self.s_act.wait(act)
                self.s_act.inc(act.activation(out=self.planes[:, SC0:SC0 + T],
                                              in_=self.planes[:, SC0:SC0 + T],
                                              func=AF.Exp, scale=np.float32(-1.0)))
                self.s_act.wait(act)
                self.s_act.inc(act.activation(out=self.planes[:, SC0:SC0 + T],
                                              in_=self.planes[:, SC0:SC0 + T],
                                              func=AF.Ln, bias=np.float32(1.0)))
                self.s_act.wait(act)
                self.s_act.inc(act.activation(out=self.cacc[:, 0:T],
                                              in_=self.planes[:, DT0:DT0 + T],
                                              func=AF.Relu, bias=self.consts[:, 0:1]))
                ev["sp_parts"] = self.s_act.n

            @blk.vector
            def _(v):
                v.wait_ge(self.s_act.h, ev["sp_parts"])
                self.s_dve.inc(v.tensor_add(out=self.planes[:, DT0:DT0 + T],
                                            in0=self.cacc[:, 0:T],
                                            in1=self.planes[:, SC0:SC0 + T]))
                self.s_dve.wait(v)
                self.s_dve.inc(v.tensor_scalar(out=self.planes[:, SC0:SC0 + T],
                                               in0=self.planes[:, DT0:DT0 + T],
                                               scalar1=self.consts[:, 2:3], scalar2=None,
                                               op0=OP.mult))
                self.s_dve.wait(v)
                for c in range(NCH):
                    s0 = SC0 + c * CS
                    self.s_dve.inc(v.tensor_tensor_scan(
                        out=self.planes[:, CU0 + c * CS:CU0 + (c + 1) * CS],
                        data0=self.planes[:, s0:s0 + CS],
                        data1=self.planes[:, s0:s0 + CS],
                        initial=0.0, op0=OP.add, op1=OP.bypass))
                ev["cum"] = self.s_dve.n

            @blk.scalar
            def _(act):
                act.wait_ge(self.s_dve.h, ev["cum"])
                self.s_act.inc(act.activation(out=self.planes_bf[:, 0:T],
                                              in_=self.planes[:, SC0:SC0 + T],
                                              func=AF.Exp))
                ev["edA"] = self.s_act.n
                for c in range(NCH):
                    act.wait_ge(self.s_act.h, self.s_act.n)
                    self.s_act.inc(act.activation(
                        out=self.cacc[:, c * CS:(c + 1) * CS],
                        in_=self.planes[:, CU0 + c * CS:CU0 + (c + 1) * CS],
                        func=AF.Exp, scale=np.float32(-1.0),
                        bias=self.planes[:, CU0 + (c + 1) * CS - 1:CU0 + (c + 1) * CS]))
                ev["wexp"] = self.s_act.n

            @blk.vector
            def _(v):
                v.wait_ge(self.s_act.h, ev["wexp"])
                self.s_dve.inc(v.tensor_tensor(out=self.planes[:, SC0:SC0 + T],
                                               in0=self.cacc[:, 0:T],
                                               in1=self.planes[:, DT0:DT0 + T],
                                               op=OP.mult))
                self.s_dve.wait(v)
                self.s_dve.inc(v.tensor_copy(out=self.scr[:, 30:31],
                                             in_=self.planes[:, CU0 + CS - 1:CU0 + CS]))
                for c in range(1, NCH):
                    self.s_dve.wait(v)
                    self.s_dve.inc(v.tensor_add(
                        out=self.scr[:, 30:31], in0=self.scr[:, 30:31],
                        in1=self.planes[:, CU0 + (c + 1) * CS - 1:CU0 + (c + 1) * CS]))
                ev["wglog"] = self.s_dve.n

            for tc in range(8):
                @blk.tensor
                def _(te, tc=tc):
                    te.wait_ge(self.s_dve.h, ev["wglog"] if tc == 0 else ev[("tmev", tc - 1)])
                    te.transpose(out=psS2[:, 0:128],
                                 in_=self.planes[:, DT0 + tc * 128:DT0 + (tc + 1) * 128],
                                 identity=self.idf32[:, :])
                    self.s_pe.inc(te.transpose(
                        out=psS2[:, 128:256],
                        in_=self.planes[:, SC0 + tc * 128:SC0 + (tc + 1) * 128],
                        identity=self.idf32[:, :]))
                    ev[("tmtp", tc)] = self.s_pe.n

                @blk.vector
                def _(v, tc=tc):
                    v.wait_ge(self.s_pe.h, ev[("tmtp", tc)])
                    self.s_dve.inc(v.tensor_copy(
                        out=self.tmcols[:, tc * 128:(tc + 1) * 128],
                        in_=psS2[:, 0:128]))
                    self.s_dve.inc(v.tensor_copy(
                        out=self.tmcols[:, WT0 + tc * 128:WT0 + (tc + 1) * 128],
                        in_=psS2[:, 128:256]))
                    ev[("tmev", tc)] = self.s_dve.n

            for c in range(NCH):
                @blk.tensor
                def _(te, c=c):
                    te.wait_ge(self.s_dve.h, ev[("tmev", 7)] if c == 0 else ev[("tcev", c - 1)])
                    self.s_pe.inc(te.transpose(
                        out=psS2[0:1, 0:128],
                        in_=self.planes[:, CU0 + (c + 1) * CS - 1:CU0 + (c + 1) * CS],
                        identity=self.idf32[:, :]))
                    ev[("cdtp", c)] = self.s_pe.n

                @blk.scalar
                def _(act, c=c):
                    act.wait_ge(self.s_pe.h, ev[("cdtp", c)])
                    self.s_act.inc(act.activation(
                        out=self.planes_bf[0:1, T + c * 128:T + (c + 1) * 128],
                        in_=psS2[0:1, 0:128], func=AF.Exp))
                    ev[("cdrow", c)] = self.s_act.n

                @blk.tensor
                def _(te, c=c):
                    te.wait_ge(self.s_act.h, ev[("cdrow", c)])
                    self.s_pe.inc(te.matmul(
                        out=psS2[:, 128:256],
                        lhsT=self.ones1[:, :],
                        rhs=self.planes_bf[0:1, T + c * 128:T + (c + 1) * 128],
                        start=True, stop=True))
                    ev[("tck1", c)] = self.s_pe.n

                @blk.vector
                def _(v, c=c):
                    v.wait_ge(self.s_pe.h, ev[("tck1", c)])
                    self.s_dve.inc(v.tensor_copy(
                        out=self.Tc_all[:, c * 128:(c + 1) * 128],
                        in_=psS2[:, 128:256]))
                    ev[("tcev", c)] = self.s_dve.n

            for ft in range(1, 17):
                gp_prefetch(ft + 1)
                te_proj(ft)
                if ft <= 8:
                    g = ft - 1
                    ct = 64 + g
                    dst = self.B_fm[:, g * T:(g + 1) * T]
                else:
                    g = ft - 9
                    ct = 72 + g
                    dst = self.C_fm[:, g * T:(g + 1) * T]
                dve_conv(ft, ct)
                act_silu(ft, ct, dst)

            for ft in range(17, 81):
                k = ft - 17
                gp_prefetch(ft + 1)
                te_proj(ft)
                dve_conv(ft, 0 + k)
                act_silu(ft, 0 + k, self.hs_fm[:, (k % 2) * T:(k % 2) * T + T],
                         hs_k=k)
                if k >= 1:
                    self._p2_ssd_pair(blk, ev, psS1, psS2, k - 1)
                self._p2_hs_stage(blk, ev, k)
            self._p2_ssd_pair(blk, ev, psS1, psS2, 63)

            @blk.gpsimd
            def _(gp):
                gp.wait_ge(self.s_dve2.h, ev[("run", 63, 3)])
                self.c_ag.inc(gp.dma_start(out=self.h_agin[:, 0:INTER],
                                           in_=self.runbf[:, :]), 0)
                self.c_ag.inc(gp.dma_start(allow_slow_non_contiguous=True,
                                           out=self.h_agin[:, INTER:INTER + 1],
                                           in_=self.scr[:, 30:31]), 0)
                self.c_ag.wait(gp, 0)
                gp.collective_compute(
                    "AllGather", OP.bypass,
                    replica_groups=[[0, 1, 2, 3], [4, 5, 6, 7]],
                    ins=[self.h_agin.opt()], outs=[self.h_agout.opt()],
                ).then_inc(self.s_cc.h, 1)
                self.s_cc.n += 1

            for ft in range(81, 145):
                gt = ft - 81
                gp_prefetch(ft + 1)
                te_proj(ft)
                ps = psPA if ft % 2 == 0 else psPB

                @blk.scalar
                def _(act, ft=ft, gt=gt, ps=ps):
                    act.wait_ge(self.s_pe2.h, ev[("proj", ft)])
                    if gt >= 2:
                        self.c_gt.wait(act, gt % 2, at=ev[("gtsp", gt - 2)])
                    self.s_act2.inc(act.activation(
                        out=self.gatest[:, (gt % 2) * T:(gt % 2) * T + T],
                        in_=ps[:, 128:TH], func=AF.Silu))
                    ev[("psfree_a", ft)] = self.s_act2.n
                    ev[("gsilu", gt)] = self.s_act2.n
                ev[("psfree", ft)] = self.s_dve2.n

                @blk.sync
                def _(sync, gt=gt):
                    sync.wait_ge(self.s_act2.h, ev[("gsilu", gt)])
                    self.c_gt.inc(sync.dma_start(
                        out=self.h_gate[gt],
                        in_=self.gatest[:, (gt % 2) * T:(gt % 2) * T + T]), gt % 2)
                    ev[("gtsp", gt)] = self.c_gt.val(gt % 2)

    def _p2_hs_stage(self, blk, ev, k):
        ft = 17 + k

        @blk.scalar
        def _(act):
            act.wait_ge(self.s_act2.h, ev[("silu", ft)])
            if k >= 2:
                act.wait_ge(self.s_pe.h, ev[("ssdpe", k - 2)])
            srcb = self.hs_fm[:, (k % 2) * T:(k % 2) * T + T]
            dst = self.hs_tm[:, (k % 2) * T:(k % 2) * T + T]
            for tc in range(8):
                self.c_tp.inc(act.dma_start(out=dst[:, tc * 128:(tc + 1) * 128],
                                            in_=srcb[:, tc * 128:(tc + 1) * 128],
                                            transpose=True), k % 2)
            ev[("hstm", k)] = self.c_tp.val(k % 2)
            act.wait_ge(self.s_act.h, ev["edA"])
            st = self.sstage
            if ("edbc", k - 1, NCH - 1) in ev:
                act.wait_ge(self.s_pe.h, ev[("edbc", k - 1, NCH - 1)])
            self.c_st.inc(act.dma_start(
                allow_slow_non_contiguous=True,
                out=st[0:1, 0:T],
                in_=self.planes_bf[2 * k:2 * k + 1, 0:T]), 0)
            self.c_st.inc(act.dma_start(
                allow_slow_non_contiguous=True,
                out=st[0:1, T:2 * T],
                in_=self.planes_bf[2 * k + 1:2 * k + 2, 0:T]), 0)
            ev[("edrow", k)] = self.c_st.val(0)

        if k % 8 == 0:
            g = k // 8

            @blk.scalar
            def _(act):
                act.wait_ge(self.s_act2.h, ev[("silu", 1 + g)])
                if g >= 1:
                    act.wait_ge(self.s_pe.h, ev[("stmm", 8 * g - 1, NCH - 1)])
                srcb = self.B_fm[:, g * T:(g + 1) * T]
                for tc in range(8):
                    self.c_bt.inc(act.dma_start(
                        out=self.B_tm[:, tc * 128:(tc + 1) * 128],
                        in_=srcb[:, tc * 128:(tc + 1) * 128],
                        transpose=True), g % 2)
                ev[("btm", g)] = self.c_bt.val(g % 2)

    def _p2_ssd_pair(self, blk, ev, psS1, psS2, k):
        g = k // 8
        ft = 17 + k

        if k % 8 == 0:
            for c in range(NCH):
                @blk.tensor
                def _(te, c=c):
                    if c == 0:
                        te.wait_ge(self.s_act2.h, ev[("silu", 1 + g)])
                        te.wait_ge(self.s_act2.h, ev[("silu", 9 + g)])
                        if ("y1ev", k - 1, NCH - 1) in ev:
                            te.wait_ge(self.s_dve2.h, ev[("y1ev", k - 1, NCH - 1)])
                    else:
                        te.wait_ge(self.s_dve.h, ev[("rawSev", g, c - 1)])
                    Bg = self.B_fm[:, g * T:(g + 1) * T]
                    Cg = self.C_fm[:, g * T:(g + 1) * T]
                    te.matmul(out=psS2[:, 0:256],
                              lhsT=Bg[:, c * CS:c * CS + 128],
                              rhs=Cg[:, c * CS:(c + 1) * CS], start=True, stop=True)
                    self.s_pe.inc(te.matmul(out=psS2[:, 256:384],
                                            lhsT=Bg[:, c * CS + 128:(c + 1) * CS],
                                            rhs=Cg[:, c * CS + 128:(c + 1) * CS],
                                            start=True, stop=True))
                    ev[("rawSmm", g, c)] = self.s_pe.n

                @blk.vector
                def _(v, c=c):
                    v.wait_ge(self.s_pe.h, ev[("rawSmm", g, c)])
                    self.s_dve.inc(v.tensor_copy(
                        out=self.rawS[:, c * 384:c * 384 + 384],
                        in_=psS2[:, 0:384]))
                    ev[("rawSev", g, c)] = self.s_dve.n
            ev[("rawSdone", g)] = ev[("rawSev", g, NCH - 1)]

        for c in range(NCH):
            @blk.tensor
            def _(te, c=c):
                self.c_st.wait(te, 0, at=ev[("edrow", k)])
                if c == 0:
                    if k >= 1:
                        te.wait_ge(self.s_dve2.h, ev[("stev", k - 1, NCH - 1)])
                else:
                    te.wait_ge(self.s_dve2.h, ev[("stev", k, c - 1)])
                st = self.sstage
                te.matmul(out=psS1[:, 0:256],
                          lhsT=self.ones1[:, :],
                          rhs=st[0:1, c * CS:(c + 1) * CS],
                          start=True, stop=True)
                self.s_pe.inc(te.matmul(
                    out=psS1[:, 256:512], lhsT=self.ones1[:, :],
                    rhs=st[0:1, T + c * CS:T + (c + 1) * CS],
                    start=True, stop=True))
                ev[("edbc", k, c)] = self.s_pe.n

            @blk.vector
            def _(v, c=c):
                v.wait_ge(self.s_pe.h, ev[("edbc", k, c)])
                if ("y1ev", k, c - 1) in ev:
                    v.wait_ge(self.s_dve2.h, ev[("y1ev", k, c - 1)])
                for hp in range(2):
                    e0 = hp * 256
                    d0 = hp * 384
                    self.s_dve2.inc(v.tensor_tensor_scan(
                        out=self.d1b[:, d0:d0 + 256],
                        data0=psS1[:, e0:e0 + 256], data1=self.masks[:, 0:256],
                        initial=0.0, op0=OP.mult, op1=OP.add))
                    self.s_dve2.inc(v.tensor_tensor_scan(
                        out=self.d1b[:, d0 + 256:d0 + 384],
                        data0=psS1[:, e0 + 128:e0 + 256], data1=self.masks[:, 384:512],
                        initial=0.0, op0=OP.mult, op1=OP.add))
                self.s_dve2.wait(v)
                ev[("scan", k, c)] = self.s_dve2.n
                for hp in range(2):
                    h = 2 * k + hp
                    d0 = hp * 384
                    for jt in range(2):
                        tcol = self.tmcols[:, (2 * c + jt) * 128 + h:(2 * c + jt) * 128 + h + 1]
                        self.s_dve2.inc(v.scalar_tensor_tensor(
                            out=self.stb[:, d0 + jt * 256:d0 + jt * 256 + (256 if jt == 0 else 128)],
                            in0=self.d1b[:, d0 + jt * 256:d0 + jt * 256 + (256 if jt == 0 else 128)],
                            scalar=tcol,
                            in1=self.rawS[:, c * 384 + jt * 256:c * 384 + jt * 256 + (256 if jt == 0 else 128)],
                            op0=OP.mult, op1=OP.mult))
                self.s_dve2.wait(v)
                ev[("st", k, c)] = self.s_dve2.n

            @blk.tensor
            def _(te, c=c):
                te.wait_ge(self.s_dve2.h, ev[("st", k, c)])
                self.c_tp.wait(te, k % 2, at=ev[("hstm", k)])
                if k % 8 == 0 and c == 0:
                    te.wait_ge(self.s_dve.h, ev[("rawSdone", g)])
                if ("y1ev", k, c - 1) in ev:
                    te.wait_ge(self.s_dve2.h, ev[("y1ev", k, c - 1)])
                elif ("y1ev", k - 1, NCH - 1) in ev:
                    te.wait_ge(self.s_dve2.h, ev[("y1ev", k - 1, NCH - 1)])
                hst = self.hs_tm[:, (k % 2) * T:(k % 2) * T + T]
                for hp in range(2):
                    d0 = hp * 384
                    lt0 = hst[:, (2 * c) * 128 + hp * 64:(2 * c) * 128 + hp * 64 + 64]
                    lt1 = hst[:, (2 * c + 1) * 128 + hp * 64:(2 * c + 1) * 128 + hp * 64 + 64]
                    te.matmul(out=psS2[hp * 64:hp * 64 + 64, 0:256],
                              lhsT=lt0, rhs=self.stb[:, d0:d0 + 256],
                              start=True, stop=False)
                    mm = te.matmul(out=psS2[hp * 64:hp * 64 + 64, 128:256],
                                   lhsT=lt1, rhs=self.stb[:, d0 + 256:d0 + 384],
                                   start=False, stop=True)
                self.s_pe.inc(mm)
                ev[("ymm", k, c)] = self.s_pe.n

            @blk.vector
            def _(v, c=c):
                self.c_bt.wait(v, g % 2, at=ev[("btm", g)])
                self.c_tp.wait(v, k % 2, at=ev[("hstm", k)])
                v.wait_ge(self.s_pe.h, ev.get(("stmm", k, c - 1), ev.get(("stmm", k - 1, NCH - 1), 0)))
                hst = self.hs_tm[:, (k % 2) * T:(k % 2) * T + T]
                for hp in range(2):
                    h = 2 * k + hp
                    for jt in range(2):
                        wcol = self.tmcols[:, WT0 + (2 * c + jt) * 128 + h:WT0 + (2 * c + jt) * 128 + h + 1]
                        self.s_dve2.inc(v.tensor_scalar(
                            out=self.wbb[:, (hp * 2 + jt) * 64:(hp * 2 + jt) * 64 + 64],
                            in0=hst[:, (2 * c + jt) * 128 + hp * 64:(2 * c + jt) * 128 + hp * 64 + 64],
                            scalar1=wcol, scalar2=None, op0=OP.mult))
                self.s_dve2.wait(v)
                ev[("wb", k, c)] = self.s_dve2.n

            @blk.vector
            def _(v, c=c):
                v.wait_ge(self.s_pe.h, ev[("ymm", k, c)])
                if c == 0:
                    self.c_y1.wait(v, k % 2)
                self.s_dve2.inc(v.scalar_tensor_tensor(
                    out=self.y1st[:, (k % 2) * T + c * CS:(k % 2) * T + (c + 1) * CS],
                    in0=self.hs_fm[:, (k % 2) * T + c * CS:(k % 2) * T + (c + 1) * CS],
                    scalar=self.drep[:, k:k + 1], in1=psS2[:, 0:256],
                    op0=OP.mult, op1=OP.add))
                ev[("y1ev", k, c)] = self.s_dve2.n

            @blk.tensor
            def _(te, c=c):
                te.wait_ge(self.s_dve2.h, ev[("wb", k, c)])
                te.wait_ge(self.s_dve2.h, ev[("scan", k, c)])
                for hp in range(2):
                    te.matmul(out=psS1[:, hp * 64:hp * 64 + 64],
                              lhsT=self.B_tm[:, (2 * c) * 128:(2 * c) * 128 + 128],
                              rhs=self.wbb[:, (hp * 2) * 64:(hp * 2) * 64 + 64],
                              start=True, stop=False)
                    mm = te.matmul(out=psS1[:, hp * 64:hp * 64 + 64],
                                   lhsT=self.B_tm[:, (2 * c + 1) * 128:(2 * c + 1) * 128 + 128],
                                   rhs=self.wbb[:, (hp * 2 + 1) * 64:(hp * 2 + 1) * 64 + 64],
                                   start=False, stop=True)
                self.s_pe.inc(mm)
                ev[("stmm", k, c)] = self.s_pe.n
                if c == NCH - 1:
                    ev[("ssdpe", k)] = self.s_pe.n

            @blk.vector
            def _(v, c=c):
                v.wait_ge(self.s_pe.h, ev[("stmm", k, c)])
                slot = (k * NCH + c) % 2
                self.c_ss.wait(v, slot)
                self.s_dve2.inc(v.tensor_copy(
                    out=self.sst[:, slot * 128:slot * 128 + 128],
                    in_=psS1[:, 0:128]))
                self.s_dve2.wait(v)
                self.s_dve2.inc(v.tensor_tensor(
                    out=self.sscr[:, 0:128].rearrange("p (a b) -> p a b", a=2),
                    in0=self.runbf[:, k * 128:(k + 1) * 128].rearrange("p (a b) -> p a b", a=2),
                    in1=self.Tc_all[:, c * 128 + 2 * k:c * 128 + 2 * k + 2]
                        .unsqueeze(2).broadcast_to((128, 2, 64)),
                    op=OP.mult))
                self.s_dve2.wait(v)
                self.s_dve2.inc(v.tensor_add(
                    out=self.runbf[:, k * 128:(k + 1) * 128],
                    in0=self.sscr[:, 0:128],
                    in1=self.sst[:, slot * 128:slot * 128 + 128]))
                self.s_dve2.wait(v)
                ev[("stev", k, c)] = self.s_dve2.n
                if c == NCH - 1 and k == 63:
                    ev[("run", 63, 3)] = self.s_dve2.n
                else:
                    ev[("run", k, c)] = self.s_dve2.n

            @blk.sync
            def _(sync, c=c):
                slot = (k * NCH + c) % 2
                sync.wait_ge(self.s_dve2.h, ev[("stev", k, c)])
                self.c_ss.inc(sync.dma_start(
                    out=self.h_states[c][:, k * 128:(k + 1) * 128],
                    in_=self.sst[:, slot * 128:slot * 128 + 128]), slot)

        @blk.sync
        def _(sync):
            sync.wait_ge(self.s_dve2.h, ev[("y1ev", k, NCH - 1)])
            self.c_y1.inc(sync.dma_start(
                out=self.h_y1[k],
                in_=self.y1st[:, (k % 2) * T:(k % 2) * T + T]), k % 2)
            ev[("y1sp", k)] = self.c_y1.val(k % 2)

    def emit_pb(self):
        nc = self.nc
        ev = {}

        with (nc.psum_tensor("psB1", [128, 2048], F32) as psB1,
              nc.psum_tensor("psB2", [128, 1024], F32) as psB2,
              nc.psum_tensor("psN", [1, 512], F32) as psN,
              nc.psum_tensor("psRq", [128, 512], F32) as psRq,
              nc.Block("pb") as blk):
            psW = psB1

            @blk.gpsimd
            def _(gp):
                gp.wait_ge(self.s_cc.h, self.s_cc.n)
                for j in range(4):
                    self.c_ag.inc(gp.dma_start(
                        allow_slow_non_contiguous=True,
                        out=self.scr[:, 32 + j:33 + j],
                        in_=self.h_agout[128 * j:128 * (j + 1), INTER:INTER + 1]), 0)
                ev["glogs"] = self.c_ag.val(0)

            @blk.vector
            def _(v):
                self.c_ag.wait(v, 0, at=ev["glogs"])
                self.s_dve.inc(v.tensor_tensor(out=self.scr[:, 36:40],
                                               in0=self.scr[:, 32:36],
                                               in1=self.consts[:, 5:9], op=OP.mult))
                self.s_dve.wait(v)
                v.memset(self.scr[:, 43:44], 0.0)
                self.s_dve.inc(v.tensor_copy(out=self.scr[:, 42:43],
                                             in_=self.scr[:, 39:40]))
                self.s_dve.wait(v)
                self.s_dve.inc(v.tensor_add(out=self.scr[:, 41:42],
                                            in0=self.scr[:, 38:39],
                                            in1=self.scr[:, 42:43]))
                self.s_dve.wait(v)
                self.s_dve.inc(v.tensor_add(out=self.scr[:, 40:41],
                                            in0=self.scr[:, 37:38],
                                            in1=self.scr[:, 41:42]))
                ev["tj"] = self.s_dve.n

            @blk.scalar
            def _(act):
                act.wait_ge(self.s_dve.h, ev["tj"])
                self.s_act.inc(act.activation(out=self.scr[:, 44:48],
                                              in_=self.scr[:, 40:44], func=AF.Exp))
                ev["wexp"] = self.s_act.n

            @blk.vector
            def _(v):
                v.wait_ge(self.s_act.h, ev["wexp"])
                self.s_dve.inc(v.tensor_tensor(out=self.scr[:, 44:48],
                                               in0=self.scr[:, 44:48],
                                               in1=self.consts[:, 5:9], op=OP.mult))
                ev["wj"] = self.s_dve.n

            @blk.gpsimd
            def _(gp):
                gp.wait_ge(self.s_dve.h, ev["wj"])
                for j in range(4):
                    self.c_sg.inc(gp.dma_start(
                        allow_slow_non_contiguous=True,
                        out=self.wrow[0:1, 128 * j:128 * j + 128],
                        in_=self.scr[:, 44 + j:45 + j]), 1)
                ev["wrows"] = self.c_sg.val(1)

            for j in range(4):
                @blk.tensor
                def _(te, j=j):
                    self.c_sg.wait(te, 1, at=ev["wrows"])
                    if j > 0:
                        te.wait_ge(self.s_dve.h, ev[("sin", j - 1)])
                    self.s_pe.inc(te.matmul(out=psW[:, 0:128], lhsT=self.ones1[:, :],
                                            rhs=self.wrow[0:1, 128 * j:128 * j + 128],
                                            start=True, stop=True))
                    ev[("wk1", j)] = self.s_pe.n

                for pc in range(8):
                    @blk.sync
                    def _(sync, j=j, pc=pc):
                        if j == 0 and pc == 0:
                            self.wait_everything(sync)
                        slot = pc % 2
                        if (j * 8 + pc) >= 2:
                            self.s_dve.wait(sync, at=ev.get(("sacc", j, pc - 2),
                                            ev.get(("sacc", j - 1, 6 + pc), 0)))
                        self.c_ld.inc(sync.dma_start(
                            out=self.sjp[:, slot * 1024:slot * 1024 + 1024],
                            in_=self.h_agout[128 * j:128 * (j + 1),
                                             1024 * pc:1024 * (pc + 1)]), slot)
                        ev[("sj", j, pc)] = self.c_ld.val(slot)

                    @blk.vector
                    def _(v, j=j, pc=pc):
                        slot = pc % 2
                        self.c_ld.wait(v, slot, at=ev[("sj", j, pc)])
                        v.wait_ge(self.s_pe.h, ev[("wk1", j)])
                        if j == 0:
                            self.s_dve.inc(v.tensor_tensor(
                                out=self.prev[:, 1024 * pc:1024 * (pc + 1)]
                                    .rearrange("p (a b) -> p a b", a=16),
                                in0=self.sjp[:, slot * 1024:slot * 1024 + 1024]
                                    .rearrange("p (a b) -> p a b", a=16),
                                in1=psW[:, 16 * pc:16 * pc + 16].unsqueeze(2)
                                    .broadcast_to((128, 16, 64)),
                                op=OP.mult))
                        else:
                            self.s_dve.inc(v.tensor_tensor(
                                out=self.sjp[:, slot * 1024:slot * 1024 + 1024]
                                    .rearrange("p (a b) -> p a b", a=16),
                                in0=self.sjp[:, slot * 1024:slot * 1024 + 1024]
                                    .rearrange("p (a b) -> p a b", a=16),
                                in1=psW[:, 16 * pc:16 * pc + 16].unsqueeze(2)
                                    .broadcast_to((128, 16, 64)),
                                op=OP.mult))
                            self.s_dve.wait(v)
                            self.s_dve.inc(v.tensor_add(
                                out=self.prev[:, 1024 * pc:1024 * (pc + 1)],
                                in0=self.prev[:, 1024 * pc:1024 * (pc + 1)],
                                in1=self.sjp[:, slot * 1024:slot * 1024 + 1024]))
                        ev[("sacc", j, pc)] = self.s_dve.n
                        if pc == 7:
                            ev[("sin", j)] = self.s_dve.n
            ev["sin"] = ev[("sacc", 3, 7)]

            for c in range(NCH):
                self._pb_chunk(blk, ev, psB1, psB2, psN, psRq, c)

    def _pb_chunk(self, blk, ev, psB1, psB2, psN, psRq, c):
        nc = self.nc

        @blk.scalar
        def _(act):
            if c == 0:
                act.wait_ge(self.s_dve.h, ev["sin"])
            else:
                act.wait_ge(self.s_dve2.h, ev[("ydone", c - 1)])
            self.s_act.inc(act.activation(
                out=self.evpl[:, 0:CS],
                in_=self.planes[:, T + c * CS:T + (c + 1) * CS], func=AF.Exp))
            ev[("evpl", c)] = self.s_act.n

        @blk.vector
        def _(v):
            if c == 0:
                v.wait_ge(self.s_dve.h, ev["sin"])
            else:
                v.wait_ge(self.s_dve.h, ev[("supd", c - 1, 7)])
                v.wait_ge(self.s_pe.h, ev[("ymm2", c - 1, 63)])
            for q in range(8):
                self.s_dve2.inc(v.tensor_copy(
                    out=self.prunb[:, q * 1024:(q + 1) * 1024],
                    in_=self.prev[:, q * 1024:(q + 1) * 1024]))
            ev[("prunb", c)] = self.s_dve2.n

        for half in range(8):
            @blk.sync
            def _(sync, half=half):
                if c == 0 and half == 0:
                    sync.wait_ge(self.s_dve.h, ev["sin"])
                if ("ydone", c - 1) in ev:
                    sync.wait_ge(self.s_dve2.h, ev[("ydone", c - 1)])
                self.c_ld.inc(sync.dma_start(
                    out=self.yb[:, half * 2048:(half + 1) * 2048]
                        .rearrange("p (k t) -> p k t", k=8),
                    in_=self.h_y1[8 * half:8 * (half + 1), :, c * CS:(c + 1) * CS]
                        .rearrange("k p t -> p k t")), 2)
                self.c_ld.inc(sync.dma_start(
                    out=self.gateb[:, half * 2048:(half + 1) * 2048]
                        .rearrange("p (k t) -> p k t", k=8),
                    in_=self.h_gate[8 * half:8 * (half + 1), :, c * CS:(c + 1) * CS]
                        .rearrange("k p t -> p k t")), 3)
        ev[("reloadall", c)] = (self.c_ld.val(2), self.c_ld.val(3))

        for ib in range(0, 72, 4):
            for j in (ib, ib + 1, ib + 2, ib + 3):
                if j < 64 and j % 8 == 0:
                    q = j // 8
                    @blk.gpsimd
                    def _(gp, q=q):
                        gp.wait_ge(self.s_act.h, ev[("evpl", c)])
                        if ("evk1", c, 8 * q - 2) in ev:
                            gp.wait_ge(self.s_pe.h, ev[("evk1", c, 8 * q - 2)])
                        if q > 0:
                            gp.wait_ge(self.s_pe.h, ev[("evk1", c, 8 * q - 1)])
                        elif c > 0 and ("evk1", c - 1, 63) in ev:
                            gp.wait_ge(self.s_pe.h, ev[("evk1", c - 1, 63)])
                        self.c_sg.inc(gp.dma_start(
                            allow_slow_non_contiguous=True,
                            out=self.evst[0:1, 0:4096]
                                .rearrange("o (h t) -> o h t", h=16),
                            in_=self.evpl[16 * q:16 * (q + 1), 0:CS]), 0)
                        ev[("evst", c, q)] = self.c_sg.val(0)
                if j < 64:
                    self._pb_stage1(blk, ev, psB1, c, j)
            for j in (ib - 8, ib - 7, ib - 6, ib - 5):
                if 0 <= j < 64:
                    self._pb_stage3(blk, ev, psN, c, j)
                    if j % 8 == 7:
                        self._pb_group_norm(blk, ev, psN, psRq, c, j // 8)
            for j in (ib - 4, ib - 3, ib - 2, ib - 1):
                if 0 <= j < 64:
                    self._pb_stage2(blk, ev, psB1, psB2, c, j)
        ev[("ydone", c)] = self.s_dve2.n

        for pc in range(8):
            @blk.sync
            def _(sync, pc=pc):
                sync.wait_ge(self.s_dve2.h, ev[("ydone", c)])
                slot = pc % 2
                if ("supd", c, pc - 2) in ev:
                    sync.wait_ge(self.s_dve.h, ev[("supd", c, pc - 2)])
                self.c_ld.inc(sync.dma_start(
                    out=self.sjp[:, slot * 1024:slot * 1024 + 1024],
                    in_=self.h_states[c][:, 1024 * pc:1024 * (pc + 1)]), slot)
                ev[("sld", c, pc)] = self.c_ld.val(slot)

            @blk.vector
            def _(v, pc=pc):
                slot = pc % 2
                self.c_ld.wait(v, slot, at=ev[("sld", c, pc)])
                self.s_dve.inc(v.tensor_tensor(
                    out=self.prev[:, 1024 * pc:1024 * (pc + 1)]
                        .rearrange("p (a b) -> p a b", a=16),
                    in0=self.prev[:, 1024 * pc:1024 * (pc + 1)]
                        .rearrange("p (a b) -> p a b", a=16),
                    in1=self.Tc_all[:, c * 128 + 16 * pc:c * 128 + 16 * pc + 16]
                        .unsqueeze(2).broadcast_to((128, 16, 64)),
                    op=OP.mult))
                self.s_dve.wait(v)
                self.s_dve.inc(v.tensor_add(
                    out=self.prev[:, 1024 * pc:1024 * (pc + 1)],
                    in0=self.prev[:, 1024 * pc:1024 * (pc + 1)],
                    in1=self.sjp[:, slot * 1024:slot * 1024 + 1024]))
                ev[("supd", c, pc)] = self.s_dve.n

    def _pb_stage1(self, blk, ev, psB1, c, k):
        g = k // 8
        q = k // 8
        sl = k % 4
        b1 = psB1[:, sl * 512:sl * 512 + 512]

        @blk.tensor
        def _(te):
            self.c_sg.wait(te, 0, at=ev[("evst", c, q)])
            if ("ctld", c, k - 4) in ev:
                te.wait_ge(self.s_dve2.h, ev[("ctld", c, k - 4)])
            elif c > 0 and ("ctld", c - 1, _prev4(k)) in ev:
                te.wait_ge(self.s_dve2.h, ev[("ctld", c - 1, _prev4(k))])
            r0 = (2 * k) % 16
            te.matmul(out=b1[:, 0:256], lhsT=self.ones1[:, :],
                      rhs=self.evst[0:1, r0 * CS:(r0 + 1) * CS],
                      start=True, stop=True)
            self.s_pe.inc(te.matmul(
                out=b1[:, 256:512], lhsT=self.ones1[:, :],
                rhs=self.evst[0:1, (r0 + 1) * CS:(r0 + 2) * CS],
                start=True, stop=True))
            ev[("evk1", c, k)] = self.s_pe.n

        @blk.vector
        def _(v):
            v.wait_ge(self.s_pe.h, ev[("evk1", c, k)])
            v.wait_ge(self.s_dve2.h, ev[("prunb", c)])
            if ("ymm2", c, k - 8) in ev:
                v.wait_ge(self.s_pe.h, ev[("ymm2", c, k - 8)])
            elif c > 0 and ("ymm2", c - 1, _prev8(k)) in ev:
                v.wait_ge(self.s_pe.h, ev[("ymm2", c - 1, _prev8(k))])
            sl4 = k % 8
            for hp in range(2):
                self.s_dve2.inc(v.tensor_tensor(
                    out=self.ctld[:, sl4 * 512 + hp * CS:sl4 * 512 + (hp + 1) * CS],
                    in0=self.C_fm[:, g * T + c * CS:g * T + (c + 1) * CS],
                    in1=b1[:, hp * 256:hp * 256 + 256], op=OP.mult))
            self.s_dve2.wait(v)
            ev[("ctld", c, k)] = self.s_dve2.n

    def _pb_stage2(self, blk, ev, psB1, psB2, c, k):
        sl = k % 2
        b2 = psB2[:, sl * 512:sl * 512 + 256]

        @blk.tensor
        def _(te):
            te.wait_ge(self.s_dve2.h, ev[("ctld", c, k)])
            if ("gev", c, k - 2) in ev:
                te.wait_ge(self.s_dve2.h, ev[("gev", c, k - 2)])
            elif c > 0 and ("gev", c - 1, 62 + (k % 2)) in ev:
                te.wait_ge(self.s_dve2.h, ev[("gev", c - 1, 62 + (k % 2))])
            sl4 = k % 8
            for hp in range(2):
                mm = te.matmul(out=b2[hp * 64:hp * 64 + 64, 0:256],
                               lhsT=self.prunb[:, k * 128 + hp * 64:k * 128 + hp * 64 + 64],
                               rhs=self.ctld[:, sl4 * 512 + hp * CS:sl4 * 512 + (hp + 1) * CS],
                               start=True, stop=True)
            self.s_pe.inc(mm)
            ev[("ymm2", c, k)] = self.s_pe.n

        @blk.vector
        def _(v):
            v.wait_ge(self.s_pe.h, ev[("ymm2", c, k)])
            rl = ev[("reloadall", c)]
            self.c_ld.wait(v, 2, at=rl[0])
            self.c_ld.wait(v, 3, at=rl[1])
            half = k // 8
            kk = k % 8
            ybs = self.yb[:, half * 2048 + kk * CS:half * 2048 + (kk + 1) * CS]
            gbs = self.gateb[:, half * 2048 + kk * CS:half * 2048 + (kk + 1) * CS]
            self.s_dve2.inc(v.tensor_add(out=self.gy[:, sl * CS:(sl + 1) * CS],
                                         in0=ybs, in1=b2))
            self.s_dve2.wait(v)
            self.s_dve2.inc(v.tensor_tensor(
                out=self.gpre[:, (k % 8) * CS:(k % 8 + 1) * CS],
                in0=self.gy[:, sl * CS:(sl + 1) * CS], in1=gbs, op=OP.mult))
            self.s_dve2.wait(v)
            ev[("gev", c, k)] = self.s_dve2.n

        @blk.scalar
        def _(act):
            act.wait_ge(self.s_dve2.h, ev[("gev", c, k)])
            s4 = k % 4
            self.s_act2.inc(act.activation(
                out=self.gsq[:, s4 * CS:(s4 + 1) * CS],
                in_=self.gpre[:, (k % 8) * CS:(k % 8 + 1) * CS], func=AF.Square))
            ev[("gsq", c, k)] = self.s_act2.n

    def _pb_stage3(self, blk, ev, psN, c, k):
        sl = k % 2

        @blk.tensor
        def _(te):
            te.wait_ge(self.s_act2.h, ev[("gsq", c, k)])
            if k % 8 == 0:
                if ("nsq", c, k // 8 - 1) in ev:
                    te.wait_ge(self.s_act.h, ev[("nsq", c, k // 8 - 1)])
                elif c > 0 and ("nsq", c - 1, 7) in ev:
                    te.wait_ge(self.s_act.h, ev[("nsq", c - 1, 7)])
            self.s_pe.inc(te.matmul(out=psN[0:1, 0:256], lhsT=self.onescol[:, :],
                                    rhs=self.gsq[:, (k % 4) * CS:(k % 4 + 1) * CS],
                                    start=(k % 8 == 0), stop=(k % 8 == 7)))
            ev[("nmm", c, k)] = self.s_pe.n

    def _pb_group_norm(self, blk, ev, psN, psRq, c, g):
        @blk.scalar
        def _(act):
            act.wait_ge(self.s_pe.h, ev[("nmm", c, 8 * g + 7)])
            self.s_act.inc(act.activation(out=self.nrow[0:1, 0:CS],
                                          in_=psN[0:1, 0:256], func=AF.Sqrt,
                                          scale=np.float32(1.0 / 1024.0),
                                          bias=np.float32(EPS)))
            ev[("nsq", c, g)] = self.s_act.n

        @blk.vector
        def _(v):
            v.wait_ge(self.s_act.h, ev[("nsq", c, g)])
            self.s_dve.inc(v.reciprocal(out=self.nrow[0:1, 0:CS],
                                        in_=self.nrow[0:1, 0:CS]))
            self.s_dve.wait(v)
            self.s_dve.inc(v.tensor_copy(out=self.nrow_bf[0:1, 0:CS],
                                         in_=self.nrow[0:1, 0:CS]))
            ev[("rqrow", c, g)] = self.s_dve.n

        @blk.tensor
        def _(te):
            te.wait_ge(self.s_dve.h, ev[("rqrow", c, g)])
            if ("gnorm", c, g - 1) in ev:
                te.wait_ge(self.s_dve2.h, ev[("gnorm", c, g - 1)])
            elif c > 0:
                te.wait_ge(self.s_dve2.h, ev[("gnorm", c - 1, 7)])
            self.s_pe.inc(te.matmul(
                out=psRq[:, 0:256], lhsT=self.ones1[:, :],
                rhs=self.nrow_bf[0:1, 0:CS],
                start=True, stop=True))
            ev[("rqk1", c, g)] = self.s_pe.n

        @blk.vector
        def _(v):
            v.wait_ge(self.s_pe.h, ev[("rqk1", c, g)])
            self.c_gt.wait(v, 0)
            self.c_gt.wait(v, 1)
            for kk in range(8):
                k = 8 * g + kk
                self.s_dve2.inc(v.scalar_tensor_tensor(
                    out=self.gst[:, kk * CS:(kk + 1) * CS],
                    in0=psRq[:, 0:256], scalar=self.nwrep[:, k:k + 1],
                    in1=self.gpre[:, kk * CS:(kk + 1) * CS],
                    op0=OP.mult, op1=OP.mult))
            self.s_dve2.wait(v)
            ev[("gnorm", c, g)] = self.s_dve2.n

        @blk.sync
        def _(sync):
            sync.wait_ge(self.s_dve2.h, ev[("gnorm", c, g)])
            self.c_gt.inc(sync.dma_start(
                out=self.h_g[8 * g:8 * (g + 1), :, c * CS:(c + 1) * CS]
                    .rearrange("k p t -> p k t"),
                in_=self.gst[:, 0:2048].rearrange("p (k t) -> p k t", k=8)), g % 2)
            ev[("gsp", c, g)] = self.c_gt.val(g % 2)

    def emit_p3a(self):
        nc = self.nc
        ev = {}
        with nc.Block("gl") as blk:
            @blk.sync
            def _(sync):
                self.wait_everything(sync)
                for q in range(4):
                    self.c_ld.inc(sync.dma_start(
                        out=self.g_all[:, q * 16384:(q + 1) * 16384]
                            .rearrange("p (k t) -> p k t", k=16),
                        in_=self.h_g[16 * q:16 * (q + 1)]
                            .rearrange("k p t -> p k t")), q % 2)

        bws = [self.c_w.val(j) for j in range(2)]
        bxs = [self.c_x.val(j) for j in range(2)]
        bsp = [self.c_sp.val(j) for j in range(2)]
        bp2, bd2 = self.s_pe2.n, self.s_dve2.n
        w_done = [bws[dt % 2] + 16 * (dt // 2 + 1) for dt in range(32)]
        x_done = [bxs[dt % 2] + 16 * (dt // 2 + 1) for dt in range(32)]
        sp_done = [bsp[dt % 2] + 16 * (dt // 2 + 1) for dt in range(32)]
        pe_done = [bp2 + (dt + 1) for dt in range(32)]
        ev_done = [bd2 + (dt + 1) for dt in range(32)]

        with (nc.psum_tensor("psO_A", [128, 1024], F32) as psA,
              nc.psum_tensor("psO_B", [128, 1024], F32) as psB,
              nc.Block("p3a") as blk):
            @blk.gpsimd
            def _(gp):
                self.wait_everything(gp)
                for dt in range(32):
                    if dt >= 2:
                        gp.wait_ge(self.s_pe2.h, pe_done[dt - 2])
                    slab = self.wslab3[:, (dt % 2) * 8192:(dt % 2) * 8192 + 8192]
                    self.c_w.inc(gp.dma_start(out=slab, in_=self.i_wout[dt]), dt % 2)

            @blk.sync
            def _(sync):
                for dt in range(32):
                    if dt >= 2:
                        sync.wait_ge(self.s_dve2.h, ev_done[dt - 2])
                    buf = self.xtile[:, (dt % 2) * TH:(dt % 2) * TH + T]
                    self.c_x.inc(sync.dma_start(out=buf,
                                                in_=self.h_r1[dt][:, HALO:TH]), dt % 2)
                    if dt >= 1:
                        sync.wait_ge(self.s_dve2.h, ev_done[dt - 1])
                        eb = self.evt[:, ((dt - 1) % 2) * TH:((dt - 1) % 2) * TH + T]
                        self.c_sp.inc(sync.dma_start(out=self.h_r2[dt - 1],
                                                     in_=eb), (dt - 1) % 2)
                sync.wait_ge(self.s_dve2.h, ev_done[31])
                eb = self.evt[:, (31 % 2) * TH:(31 % 2) * TH + T]
                self.c_sp.inc(sync.dma_start(out=self.h_r2[31], in_=eb), 31 % 2)
                self.c_sp.wait(sync, 0)
                self.c_sp.wait(sync, 1)

            @blk.tensor
            def _(te):
                self.c_ld.wait(te, 0)
                self.c_ld.wait(te, 1)
                for dt in range(32):
                    self.c_w.wait(te, dt % 2, at=w_done[dt])
                    if dt >= 2:
                        te.wait_ge(self.s_dve2.h, ev_done[dt - 2])
                    ps = psA if dt % 2 == 0 else psB
                    slab = self.wslab3[:, (dt % 2) * 8192:(dt % 2) * 8192 + 8192]
                    for f in range(64):
                        lt = slab[:, f * 128:(f + 1) * 128]
                        for (c0, c1) in ((0, 512), (512, 1024)):
                            mm = te.matmul(out=ps[:, c0:c1], lhsT=lt,
                                           rhs=self.g_all[:, f * T + c0:f * T + c1],
                                           start=(f == 0), stop=(f == 63))
                    self.s_pe2.inc(mm)

            @blk.vector
            def _(v):
                for dt in range(32):
                    v.wait_ge(self.s_pe2.h, pe_done[dt])
                    self.c_x.wait(v, dt % 2, at=x_done[dt])
                    if dt >= 2:
                        self.c_sp.wait(v, dt % 2, at=sp_done[dt - 2])
                    ps = psA if dt % 2 == 0 else psB
                    eb = self.evt[:, (dt % 2) * TH:(dt % 2) * TH + T]
                    self.s_dve2.inc(v.tensor_add(
                        out=eb, in0=ps[:, 0:T],
                        in1=self.xtile[:, (dt % 2) * TH:(dt % 2) * TH + T]))

    def build(self):
        self.emit_init()
        self.push("C_fm", [128, 8 * T], BF16)
        self.push("planes", [128, 3 * T], F32)
        self.push("planes_bf", [128, T + 512], BF16)
        self.push("tmcols", [128, 2 * 8 * 128], F32)
        self.push("Tc_all", [128, 4 * 128], F32)
        self.push("runbf", [128, INTER], BF16)
        self.push("xn", [128, 32 * TH], BF16)
        self.xst = TH
        self.push("rows", [1, 3 * TH], F32)
        self.push("rows_bf", [1, 2 * TH], BF16)
        self.push("xtile", [128, 2 * TH], F32)
        self.push("sqb", [128, 2 * TH], BF16)
        self.push("evt", [128, 2 * TH], F32)
        self.push("wslab", [128, 2 * 4224], BF16)
        self.emit_ln("ln1", "x", TH, mask_halo=True)
        self.emit_ffn("ffn1", TH, "x", self.h_r1)
        self.emit_ln("ln2", "r1", TH, mask_halo=True)
        self.pop("wslab", "evt", "sqb", "xtile", "rows_bf", "rows")
        self.emit_barrier()
        self.push("wslab2", [128, 2 * 4096], BF16)
        self.push("B_fm", [128, INTER], BF16)
        self.push("B_tm", [128, 1024], BF16)
        self.push("hs_fm", [128, 2 * T], BF16)
        self.push("hs_tm", [128, 2 * T], BF16)
        self.push("rawS", [128, 4 * 384], BF16)
        self.push("d1b", [128, 2 * 384], BF16)
        self.push("stb", [128, 2 * 384], BF16)
        self.push("y1st", [128, 2 * T], BF16)
        self.push("gatest", [128, 2 * T], BF16)
        self.push("cacc", [128, T], F32)
        self.push("wbb", [128, 4 * 64], BF16)
        self.push("sst", [128, 2 * 128], F32)
        self.push("sscr", [128, 128], F32)
        self.push("sstage", [1, 2 * T], BF16)
        self.emit_p2()
        self.pop("sstage", "sscr", "sst", "wbb", "cacc", "gatest", "y1st",
                 "stb", "d1b", "rawS", "hs_tm", "hs_fm", "B_tm", "B_fm",
                 "wslab2", "xn")
        self.pop("runbf")
        self.emit_barrier()
        self.push("prev", [128, INTER], F32)
        self.push("sjp", [128, 2 * 1024], F32)
        self.push("yb", [128, 64 * CS], BF16)
        self.push("gateb", [128, 64 * CS], BF16)
        self.push("gpre", [128, 8 * CS], BF16)
        self.push("gst", [128, 8 * CS], BF16)
        self.push("gy", [128, 2 * CS], F32)
        self.push("gsq", [128, 4 * CS], BF16)
        self.push("ctld", [128, 8 * 512], BF16)
        self.push("prunb", [128, INTER], BF16)
        self.push("evpl", [128, CS], BF16)
        self.push("evst", [1, 4096], BF16)
        self.push("nrow", [1, CS], F32)
        self.push("nrow_bf", [1, CS], BF16)
        self.push("wrow", [1, 512], BF16)
        self.emit_pb()
        self.pop("wrow", "nrow_bf", "nrow", "evst", "evpl", "prunb", "ctld", "gsq", "gy", "gst",
                 "gpre", "gateb", "yb", "sjp", "prev")
        self.pop("Tc_all", "tmcols", "planes_bf", "planes", "C_fm")
        self.emit_barrier()
        self.push("g_all", [128, 64 * T], BF16)
        self.push("wslab3", [128, 2 * 8192], BF16)
        self.push("xtile", [128, 2 * TH], F32)
        self.push("evt", [128, 2 * TH], F32)
        self.emit_p3a()
        self.pop("evt", "xtile", "wslab3", "g_all")
        self.emit_barrier()
        self.push("xn3", [128, 32 * T], BF16)
        self.xn = self.xn3
        self.xst = T
        self.push("rows", [1, 3 * TH], F32)
        self.push("rows_bf", [1, 2 * TH], BF16)
        self.push("xtile", [128, 2 * TH], F32)
        self.push("sqb", [128, 2 * TH], BF16)
        self.push("evt", [128, 2 * TH], F32)
        self.push("wslab", [128, 2 * 4224], BF16)
        self.emit_ln("ln3", "r2", T, mask_halo=False)
        self.emit_ffn("ffn2", T, "r2", self.o_y)
        self.finish()
        return self.nc

    def emit_barrier(self):
        self._barid = getattr(self, "_barid", 0) + 1
        with self.nc.Block(f"bar{self._barid}") as blk:
            self.barrier(blk)

    def finish(self):
        with self.nc.Block("fin") as blk:
            self.barrier(blk)


def kernel(**inputs):
    in_maps = make_inputs(inputs)
    kb = KB()
    nc = kb.build()
    res = run_bass_kernel_spmd(nc, in_maps, core_ids=list(range(8)))
    out = np.empty((2, 4096, 4096), np.float32)
    for k in range(8):
        y = res.results[k]["y_fm"].reshape(4096, T)
        out[k // 4, T * (k % 4):T * (k % 4 + 1), :] = y.T
    return out
